# version 27
# baseline (speedup 1.0000x reference)
import contextlib
import math
import os

import numpy as np
import ml_dtypes

import concourse.bass as bass
import concourse.mybir as mybir
from concourse.bass_utils import run_bass_kernel_spmd

F32 = mybir.dt.float32
BF16 = mybir.dt.bfloat16
F16 = mybir.dt.float16
I32 = mybir.dt.int32
AF = mybir.ActivationFunctionType
ALU = mybir.AluOpType
AX = mybir.AxisListType
AP = bass.AP

TWO_PI = 2.0 * math.pi
TWO_PI_HI = float(np.float32(TWO_PI))
TWO_PI_LO = float(np.float32(TWO_PI - TWO_PI_HI))
INV_2PI = float(np.float32(1.0 / TWO_PI))
LN1E4_OVER_512 = math.log(10000.0) / 512.0
CE_HI = float(np.float32(LN1E4_OVER_512))
CE_LO = float(np.float32(LN1E4_OVER_512 - CE_HI))
HALF_PI = float(np.float32(math.pi / 2.0))
HALF_PI_LO = float(np.float32(math.pi / 2.0 - HALF_PI))
PI_CLIP = float(np.float32(math.pi))


PK0 = [0, 1, 2, 3, 8, 9, 10, 11]
PK64 = [4, 5, 6, 7, 12, 13, 14, 15]
SLOT2HEAD = []
for _k in range(8):
    SLOT2HEAD += [PK0[_k], PK64[_k]]


def permute_wq(wq):
    cols = []
    for k in range(8):
        cols.append(wq[:, PK0[k] * 64:(PK0[k] + 1) * 64])
        cols.append(wq[:, PK64[k] * 64:(PK64[k] + 1) * 64])
    return np.ascontiguousarray(np.concatenate(cols, axis=1))


def q_freq_perm():
    f = np.zeros((1, 512), dtype=np.float32)
    for s in range(16):
        h = SLOT2HEAD[s]
        f[0, s * 32:(s + 1) * 32] = h * 32 + np.arange(32)
    return f



_TABLES = {}


def rope_tables_full(s_max=4096):
    key = ("cs", s_max)
    if key not in _TABLES:
        import subprocess
        import sys
        import tempfile
        child = (
            "import sys\n"
            "import numpy as np\n"
            "import jax.numpy as jnp\n"
            "inv = 1.0 / 10000.0 ** (jnp.arange(0, 1024, 2, dtype=jnp.float32) / 1024)\n"
            "ang = jnp.arange(%d, dtype=jnp.float32)[:, None] * inv[None, :]\n"
            "np.save(sys.argv[1], np.asarray(jnp.cos(ang)))\n"
            "np.save(sys.argv[2], np.asarray(jnp.sin(ang)))\n"
        ) % s_max
        env = dict(os.environ)
        env.pop("TRN_TERMINAL_POOL_IPS", None)
        env["PYTHONPATH"] = ""
        env["JAX_PLATFORMS"] = "cpu"
        with tempfile.TemporaryDirectory() as td:
            fc = os.path.join(td, "c.npy")
            fs = os.path.join(td, "s.npy")
            subprocess.run([sys.executable, "-c", child, fc, fs], env=env,
                           check=True, capture_output=True, timeout=900)
            c = np.load(fc)
            s = np.load(fs)
        perm = q_freq_perm().ravel().astype(np.int64)
        _TABLES[key] = (np.ascontiguousarray(c[:, perm]),
                        np.ascontiguousarray(s[:, perm]))
    return _TABLES[key]


def core_tables(own0, oth0, T, s_max=4096):
    c, s = rope_tables_full(s_max)
    idx = np.concatenate([np.arange(own0, own0 + T), np.arange(oth0, oth0 + T)])
    return np.ascontiguousarray(c[idx]), np.ascontiguousarray(s[idx])


ROUND_BITS = [128.0, 64.0, 32.0, 16.0, 8.0, 4.0, 2.0, 1.0]


def emit_round_ladder(blk, sV, sS, items, fbuf_of, abuf_of, tbuf_of,
                      s_sink=None, sP=None, pool_waits=(), s_pre_waits=()):
    nlev = len(ROUND_BITS)
    ni = len(items)
    s0, v0 = sS.n, sV.n
    ready = v0
    sc = lambda k, idx: s0 + k * ni + idx + 1
    vc = lambda k, idx: v0 + k * ni + idx + 1
    n_s = (nlev + 1) * ni
    n_v = nlev * ni + (0 if sP else 2 * ni)
    sS.n += n_s
    sV.n += n_v
    p0 = sP.n if sP else 0
    if sP:
        sP.n += 4 * ni

    def s_body(scalar):
        for h, v in s_pre_waits:
            if v:
                scalar.wait_ge(h, v)
        D = 128.0
        for k, b in enumerate(ROUND_BITS):
            c = b - D
            for idx, it in enumerate(items):
                f_in = it["src"] if k == 0 else fbuf_of(it)
                if k == 0:
                    scalar.wait_ge(sV.h, ready) if ready else None
                else:
                    scalar.wait_ge(sV.h, vc(k - 1, idx))
                scalar.activation(abuf_of(it), f_in, AF.Sign, bias=-c).then_inc(sS.h, 1)
            D -= b / 2.0
        for idx, it in enumerate(items):
            scalar.wait_ge(sV.h, vc(nlev - 1, idx))
            scalar.activation(abuf_of(it), fbuf_of(it), AF.Sign).then_inc(sS.h, 1)
    if s_sink is not None:
        s_sink.append(s_body)
    else:
        blk.scalar(s_body)

    def v_body(vector):
        if v0:
            vector.wait_ge(sV.h, v0)
        for k, b in enumerate(ROUND_BITS):
            for idx, it in enumerate(items):
                f_in = it["src"] if k == 0 else fbuf_of(it)
                vector.wait_ge(sS.h, sc(k, idx))
                vector.scalar_tensor_tensor(fbuf_of(it), abuf_of(it), -b / 2.0, f_in,
                                            ALU.mult, ALU.add).then_inc(sV.h, 1)
        if sP is None:
            for idx, it in enumerate(items):
                vector.wait_ge(sS.h, s0 + nlev * ni + idx + 1)
                vector.wait_ge(sV.h, v0 + nlev * ni + 2 * idx)
                vector.scalar_tensor_tensor(tbuf_of(it), abuf_of(it), 0.5, it["src"],
                                            ALU.mult, ALU.add).then_inc(sV.h, 1)
                vector.wait_ge(sV.h, v0 + nlev * ni + 2 * idx + 1)
                vector.tensor_sub(it["dst"], tbuf_of(it), fbuf_of(it)).then_inc(sV.h, 1)
    blk.vector(v_body)

    if sP is not None:
        def p_body(gpsimd):
            for h, v in pool_waits:
                if v:
                    gpsimd.wait_ge(h, v)
            gpsimd.wait_ge(sP.h, p0)
            for idx, it in enumerate(items):
                gpsimd.wait_ge(sV.h, vc(nlev - 1, idx))
                gpsimd.wait_ge(sS.h, sc(nlev, idx))
                gpsimd.wait_ge(sP.h, p0 + 4 * idx)
                gpsimd.tensor_sub(tbuf_of(it), it["src"], fbuf_of(it)).then_inc(sP.h, 1)
                gpsimd.tensor_scalar(abuf_of(it), abuf_of(it), 0.5, None,
                                     ALU.mult).then_inc(sP.h, 1)
                gpsimd.wait_ge(sP.h, p0 + 4 * idx + 2)
                gpsimd.tensor_add(tbuf_of(it), tbuf_of(it), abuf_of(it)).then_inc(sP.h, 1)
                gpsimd.wait_ge(sP.h, p0 + 4 * idx + 3)
                gpsimd.tensor_scalar(it["dst"], tbuf_of(it), -128.0, 127.0,
                                     ALU.max, ALU.min).then_inc(sP.h, 1)
        blk.gpsimd(p_body)


class Sem:
    def __init__(self, nc, es, name):
        self.h = es.enter_context(nc.semaphore(name))
        self.n = 0

    def inc(self, instr, k=1):
        instr.then_inc(self.h, k)
        self.n += k
        return instr

    def wait_all(self, eng):
        if self.n:
            eng.wait_ge(self.h, self.n)

    def wait(self, eng, v):
        if v > 0:
            eng.wait_ge(self.h, v)


def build_program(T=2048, S=4096, qhw=None, debug_tap=None):
    D, H, HKV, INNER = 1024, 16, 4, 4096
    MT, ST = T // 128, S // 128
    OT = ST - MT
    HD = D // H
    DKV = HD * HKV
    KC = D // 128
    ICB = INNER // 512
    IC = INNER // 128
    WT = INNER // 128
    QHW = qhw or (1024 if T >= 2048 else T)
    NQH = T // QHW
    QNS = [(s0, min(512, QHW - s0)) for s0 in range(0, QHW, 512)]
    QTB = QHW // 128

    nc = bass.Bass("TRN2", target_bir_lowering=False)
    dram = lambda n, s, dt, k: nc.dram_tensor(n, s, dt, kind=k).ap()
    x_own = dram("x_own", [T, D], F32, "ExternalInput")
    x_oth = dram("x_oth", [S - T, D], F32, "ExternalInput")
    wq_d = dram("wq", [D, D], F32, "ExternalInput")
    wkv_d = dram("wkv", [D, 2 * DKV], F32, "ExternalInput")
    wo_d = dram("wo", [D, D], F32, "ExternalInput")
    wg_d = dram("wg", [INNER, D], F32, "ExternalInput")
    wu_d = dram("wu", [INNER, D], F32, "ExternalInput")
    wd_d = dram("wd", [D, INNER], F32, "ExternalInput")
    out_d = dram("out", [T, D], F32, "ExternalOutput")
    ctab_d = dram("ctab", [S, 512], F32, "ExternalInput")
    stab_d = dram("stab", [S, 512], F32, "ExternalInput")
    tg_s = dram("tg_s", [INNER, D], BF16, "Internal")
    tu_s = dram("tu_s", [INNER, D], BF16, "Internal")
    td_s = dram("td_s", [D, INNER], BF16, "Internal")
    gu_s = dram("gu_s", [T, INNER], F32, "Internal")
    ident_d = nc.inline_tensor(np.eye(128, dtype=ml_dtypes.bfloat16), name="ident128").ap()
    onesf_d = nc.inline_tensor(np.ones((1, 128), dtype=np.float32), name="ones1f").ap()
    onescol_d = nc.inline_tensor(np.ones((128, 1), dtype=np.float32), name="onescolf").ap()
    eye32f_d = nc.inline_tensor(np.eye(32, dtype=np.float32), name="eye32f").ap()
    eye128f_d = nc.inline_tensor(np.eye(128, dtype=np.float32), name="eye128f").ap()
    dbg_d = dram("dbg", [128, min(T // 128, 4) * 1024], F32, "ExternalOutput") if debug_tap else None

    es = contextlib.ExitStack()
    with es:
        sb = lambda n, s, dt: es.enter_context(nc.sbuf_tensor(n, s, dt))

        R = sb("R", [128, MT * D], F32)
        ident_b = sb("ident_b", [128, 128], BF16)
        onesf_b = sb("onesf_b", [1, 128], F32)
        onescol_b = sb("onescol_b", [128, 1], F32)
        eye32f_b = sb("eye32f_b", [32, 32], F32)
        eye128f_b = sb("eye128f_b", [128, 128], F32)
        qn_mx = sb("qn_mx", [128, H], F32)
        kn_mx = sb("kn_mx", [128, HKV], F32)
        bias16 = sb("bias16", [128, H], F32)
        rs_x = sb("rs_x", [128, MT], F32)
        cgu = sb("cgu", [128, 2 * MT], F32)
        wmean = sb("wmean", [128, 4], F32)
        wsc = sb("wsc", [128, 4], F32)
        mw2_row = sb("mw2_row", [1, 4], F32)
        mw2_b = sb("mw2_b", [128, 4], F32)
        wrow = sb("wrow", [1, 4], F32)
        chalf = sb("chalf", [128, 2], F32)

        SN = ("sLB", "sLR", "sLX0", "sLX1", "sLW", "sLT0", "sLT1",
              "sW1a", "sW1b", "sSWa", "sSWb", "sGUa", "sGUb",
              "sOuta", "sOutb", "sTD",
              "sT", "sV", "sS", "sG", "sTs", "sSe", "sTp", "sVo", "sTo", "sVsc", "sP")
        sm = {n: Sem(nc, es, n) for n in SN}
        sLB, sLR, sLW = sm["sLB"], sm["sLR"], sm["sLW"]
        sLX = [sm["sLX0"], sm["sLX1"]]
        sLT = [sm["sLT0"], sm["sLT1"]]
        sW1 = [sm["sW1a"], sm["sW1b"]]
        sSW = [sm["sSWa"], sm["sSWb"]]
        sGU = [sm["sGUa"], sm["sGUb"]]
        sOut = [sm["sOuta"], sm["sOutb"]]
        sTD = sm["sTD"]
        sT, sV, sS, sG = sm["sT"], sm["sV"], sm["sS"], sm["sG"]
        sTs, sSe, sTp = sm["sTs"], sm["sSe"], sm["sTp"]
        sVo, sTo, sVsc = sm["sVo"], sm["sTo"], sm["sVsc"]
        sP = sm["sP"]

        blk = es.enter_context(nc.Block())
        tE, vE, sE, gE, yE = blk.tensor, blk.vector, blk.scalar, blk.gpsimd, blk.sync

        def p0(sync):
            sLB.inc(sync.dma_start(ident_b[:, :], ident_d[:, :]), 16)
            sLB.inc(sync.dma_start(onesf_b[:, :], onesf_d[:, :]), 16)
            sLB.inc(sync.dma_start(onescol_b[:, :], onescol_d[:, :]), 16)
            sLB.inc(sync.dma_start(eye32f_b[:, :], eye32f_d[:, :]), 16)
            sLB.inc(sync.dma_start(eye128f_b[:, :], eye128f_d[:, :]), 16)
            sLR.inc(sync.dma_start(R[:, :].rearrange("p (m d) -> p m d", d=D),
                                   x_own.rearrange("(m p) d -> p m d", p=128)), 16)
        yE(p0)
        n_sLB = sLB.n

        def p0v(vector):
            sV.inc(vector.memset(qn_mx[:, :], 0.0))
            sV.inc(vector.memset(kn_mx[:, :], 0.0))
            sV.inc(vector.memset(wmean[:, :], 0.0))
            sV.inc(vector.memset(chalf[:, 0:1], -0.5))
            sV.inc(vector.memset(chalf[:, 1:2], 0.5))
        vE(p0v)


        with contextlib.ExitStack() as esW:
            sbW = lambda n, s, dt: esW.enter_context(nc.sbuf_tensor(n, s, dt))
            wbig = sbW("wbig", [128, 2 * 4096], F32)
            sgA = sbW("sgA", [128, 4096], BF16)
            sgB = sbW("sgB", [128, 4096], BF16)
            t2big = sbW("t2big", [128, 2 * 4096], BF16)
            wred = sbW("wredW", [128, 1], F32)
            ps_w = esW.enter_context(nc.psum_tensor("ps_w", [128, 16], F32))

            wd_list = [(wg_d, tg_s, 0), (wu_d, tu_s, 1), (wd_d, td_s, 2)]

            def tern_chunk_aps(wi, w, stash, ci, buf):
                if wi < 2:
                    ws = w[ci * 512:(ci + 1) * 512, :].rearrange("(c p) d -> p c d", p=128)
                    ss = stash[ci * 512:(ci + 1) * 512, :].rearrange("(c p) d -> p c d", p=128)
                    bb = buf.rearrange("p (c d) -> p c d", d=1024)
                else:
                    ws = w[ci * 128:(ci + 1) * 128, :]
                    ss = stash[ci * 128:(ci + 1) * 128, :]
                    bb = buf
                return ws, ss, bb

            units = [(wi, w, stash, ci) for (w, stash, wi) in wd_list for ci in range(8)]
            tvd = {}
            for u, (wi, w, stash, ci) in enumerate(units):
                pb = u % 2
                wb = wbig[:, pb * 4096:(pb + 1) * 4096]
                wsrc, _, wbv = tern_chunk_aps(wi, w, stash, ci, wb)

                def ld(sync, u=u, pb=pb, wbv=wbv, wsrc=wsrc):
                    if u >= 2:
                        sV.wait(sync, tvd[u - 2])
                    sW1[pb].inc(sync.dma_start(wbv, wsrc), 16)
                yE(ld)

                def v(vector, u=u, pb=pb, wb=wb, wi=wi):
                    sW1[pb].wait(vector, 16 * (u // 2 + 1))
                    sV.wait_all(vector)
                    sV.inc(vector.reduce_sum(wred[:, 0:1], wb, axis=AX.X, apply_absolute_value=True))
                    sV.wait_all(vector)
                    sV.inc(vector.tensor_add(wmean[:, wi:wi + 1], wmean[:, wi:wi + 1], wred[:, 0:1]))
                    tvd[u] = sV.n
                vE(v)

            def t_fin(tensor):
                sV.wait_all(tensor)
                sLB.wait(tensor, n_sLB)
                sT.inc(tensor.matmul(ps_w[0:1, 8:11], onescol_b[:, 0:1], wmean[:, 0:3], start=True, stop=True))
            tE(t_fin)
            ntf = sT.n

            def v_fin(vector):
                sT.wait(vector, ntf)
                sV.wait_all(vector)
                sV.inc(vector.tensor_scalar_mul(wrow[0:1, 0:3], ps_w[0:1, 8:11], 1.0 / (INNER * D)))
                sV.wait_all(vector)
                sV.inc(vector.tensor_scalar_max(wrow[0:1, 0:3], wrow[0:1, 0:3], 1e-5))
                sV.wait_all(vector)
                sV.inc(vector.tensor_scalar_mul(mw2_row[0:1, 0:3], wrow[0:1, 0:3], 0.5))
                sV.wait_all(vector)
                sV.inc(vector.reciprocal(wrow[0:1, 0:3], wrow[0:1, 0:3]))
                sV.wait_all(vector)
            vE(v_fin)
            nvf = sV.n

            def t_fin2(tensor):
                sV.wait(tensor, nvf)
                sT.inc(tensor.matmul(ps_w[:, 0:3], onesf_b[0:1, :], wrow[0:1, 0:3], start=True, stop=True))
                sT.inc(tensor.matmul(ps_w[:, 4:7], onesf_b[0:1, :], mw2_row[0:1, 0:3], start=True, stop=True))
            tE(t_fin2)
            ntf2 = sT.n

            def v_fin2(vector):
                sT.wait(vector, ntf2)
                sV.inc(vector.tensor_copy(wsc[:, 0:3], ps_w[:, 0:3]))
                sV.inc(vector.tensor_copy(mw2_b[:, 0:3], ps_w[:, 4:7]))
                sV.wait_all(vector)
            vE(v_fin2)
            n_wsc_v = sV.n

            tsd, tvd2 = {}, {}
            s2base = sS.n
            v2base = sV.n
            for u in range(len(units)):
                tsd[u] = s2base + 2 * (u + 1)
                tvd2[u] = v2base + (u + 1)
            sS.n = s2base + 2 * len(units)
            sV.n = v2base + len(units)

            for u, (wi, w, stash, ci) in enumerate(units):
                pb = u % 2
                wsrc, wdst, wbv = tern_chunk_aps(wi, w, stash, ci, wbig[:, pb * 4096:(pb + 1) * 4096])
                _, _, t2v = tern_chunk_aps(wi, w, stash, ci, t2big[:, pb * 4096:(pb + 1) * 4096])

                def ld(sync, u=u, pb=pb, wbv=wbv, wsrc=wsrc):
                    if u >= 2:
                        sS.wait(sync, tsd[u - 2])
                    else:
                        sV.wait(sync, tvd[len(units) - 2 + u])
                    sW1[pb].inc(sync.dma_start(wbv, wsrc), 16)
                yE(ld)

                def g(gpsimd, u=u, pb=pb, t2v=t2v, wdst=wdst):
                    sV.wait(gpsimd, tvd2[u])
                    sSW[pb].inc(gpsimd.dma_start(wdst, t2v), 16)
                gE(g)

            def s_all2(scalar):
                for u, (wi, w, stash, ci) in enumerate(units):
                    pb = u % 2
                    wb = wbig[:, pb * 4096:(pb + 1) * 4096]
                    scalar.wait_ge(sW1[pb].h, 16 * (12 + u // 2 + 1))
                    scalar.wait_ge(sV.h, n_wsc_v)
                    scalar.wait_ge(sV.h, tvd2.get(u - 1, 0))
                    scalar.activation(sgA[:, :], wb, AF.Sign, scale=wsc[:, wi:wi + 1],
                                      bias=chalf[:, 0:1]).then_inc(sS.h, 1)
                    scalar.activation(sgB[:, :], wb, AF.Sign, scale=wsc[:, wi:wi + 1],
                                      bias=chalf[:, 1:2]).then_inc(sS.h, 1)
            sE(s_all2)

            def v_all2(vector):
                for u in range(len(units)):
                    pb = u % 2
                    t2 = t2big[:, pb * 4096:(pb + 1) * 4096]
                    vector.wait_ge(sSW[pb].h, 16 * (u // 2 - 1 + 1) if u >= 2 else 0)
                    vector.wait_ge(sS.h, tsd[u])
                    vector.tensor_add(t2, sgA[:, :], sgB[:, :]).then_inc(sV.h, 1)
            vE(v_all2)
            n_stash = (sSW[0].n, sSW[1].n)
            n_ternW = (sW1[0].n, sW1[1].n, sS.n, sV.n)

        with contextlib.ExitStack() as esQ:
            sbQ = lambda n, s, dt: esQ.enter_context(nc.sbuf_tensor(n, s, dt))
            q_pk = [sbQ(f"qp_{i}", [128, T], BF16) for i in range(8)]
            k_pk = [sbQ(f"kp_{i}", [128, S], BF16) for i in range(2)]
            vaug = sbQ("vaug", [128, HKV * ST * 66], BF16)

            def tern_barrier(eng):
                sW1[0].wait(eng, n_ternW[0])
                sW1[1].wait(eng, n_ternW[1])
                sS.wait(eng, n_ternW[2])
                sV.wait(eng, n_ternW[3])
                sSW[0].wait(eng, n_stash[0])
                sSW[1].wait(eng, n_stash[1])
            for _emit in (vE, sE, tE, gE, yE):
                _emit(tern_barrier)

            def p0v2(vector):
                sV.inc(vector.memset(vaug[:, 64::66], 1.0))
            vE(p0v2)

            with contextlib.ExitStack() as esT:
                sbT = lambda n, s, dt: esT.enter_context(nc.sbuf_tensor(n, s, dt))
                psT = lambda n, s, dt=F32: esT.enter_context(nc.psum_tensor(n, s, dt))
                wq_b = sbT("wq_b", [128, KC * D], BF16)
                wkv_b = sbT("wkv_b", [128, KC * 512], BF16)
                xbuf = sbT("xbuf", [128, 2 * D], F32)
                xng = sbT("xng", [128, 2 * D], BF16)
                xT_t = sbT("xT_t", [128, D], BF16)
                means8 = sbT("means8", [128, 2], F32)
                rstds8 = sbT("rstds8", [128, 2], F32)
                lnt = sbT("lnt", [128, 2], F32)
                stats6 = sbT("stats6", [128, 12], F32)
                stats2 = sbT("stats2", [128, 2], F32)
                ct2 = sbT("ct2", [128, 1024], F32)
                st2 = sbT("st2", [128, 1024], F32)
                rt1 = sbT("rt1", [128, 512], F32)
                rt2 = sbT("rt2", [128, 512], F32)
                qro = sbT("qro", [128, D + 512], BF16)
                qsq = sbT("qsq", [128, 512], F32)
                nrm = sbT("nrm", [128, 16], F32)
                wred = sbT("wred", [128, 1], F32)

                ps_tr = psT("ps_tr", [128, 2048], BF16)
                ps_q = psT("ps_q", [128, 1024], F32)
                ps_kv = psT("ps_kv", [128, 3 * 512], F32)
                ps_ktr = psT("ps_ktr", [128, 256], BF16)

                def ldw(gpsimd):
                    sLW.inc(gpsimd.dma_start(wq_b[:, :].rearrange("p (k n) -> p k n", k=KC),
                                        wq_d.rearrange("(k p) n -> p k n", p=128)), 16)
                    sLW.inc(gpsimd.dma_start(wkv_b[:, :].rearrange("p (k n) -> p k n", k=KC),
                                        wkv_d.rearrange("(k p) n -> p k n", p=128)), 16)
                gE(ldw)
                n_ldw = sLW.n

                def emit_tile_tables(kt, own, nf):
                    pb = kt % 2

                    def ld_tab(sync, kt=kt, pb=pb):
                        sV.wait(sync, rope_done.get(kt - 2, 0))
                        sLT[pb].inc(sync.dma_start(ct2[:, pb * 512:(pb + 1) * 512],
                                                   ctab_d[kt * 128:(kt + 1) * 128, :]), 16)
                        sLT[pb].inc(sync.dma_start(st2[:, pb * 512:(pb + 1) * 512],
                                                   stab_d[kt * 128:(kt + 1) * 128, :]), 16)
                    yE(ld_tab)
                    tab_ready[kt] = sLT[pb].n

                ln_done, trev_done, rope_done = {}, {}, {}
                tab_ready = {}
                oth_defer = [None]
                oth_ktr, kev_ev, own_evt = {}, {}, {}
                p123_s = []

                def rec_s(waits, fn):
                    p123_s.append((list(waits), fn))
                    sS.n += 1
                qsq_free = [0]

                def emit_tile_stage2(gi, kt, own):
                    xn = xng[:, gi * D:(gi + 1) * D]
                    nf = 512 if own else 32
                    pb = kt % 2
                    kv4 = kt % 3

                    emit_tile_tables(kt, own, nf)

                    ts0 = (kt % 2) * 1024

                    def t_tr(tensor):
                        sV.wait(tensor, ln_done[kt])
                        sV.wait(tensor, trev_done.get(kt - 2, 0))
                        for k in range(KC):
                            ins = tensor.transpose(ps_tr[:, ts0 + k * 128:ts0 + (k + 1) * 128],
                                                   xn[:, k * 128:(k + 1) * 128], ident_b[:, :])
                        sT.inc(ins)
                    tE(t_tr)
                    n_ttr = sT.n

                    def v_trev(vector):
                        sT.wait(vector, n_ttr)
                        sV.wait_all(vector)
                        sV.inc(vector.tensor_copy(xT_t[:, :], ps_tr[:, ts0:ts0 + 1024]))
                        trev_done[kt] = sV.n
                    vE(v_trev)

                    def t_proj(tensor):
                        sV.wait(tensor, trev_done[kt])
                        sLW.wait(tensor, n_ldw)
                        sV.wait(tensor, rope_done.get(kt - 1 if own else kt - 2, 0))
                        if own:
                            for nn in range(2):
                                for k in range(KC):
                                    ins = tensor.matmul(ps_q[:, nn * 512:(nn + 1) * 512],
                                                        xT_t[:, k * 128:(k + 1) * 128],
                                                        wq_b[:, k * D + nn * 512:k * D + nn * 512 + 512],
                                                        start=(k == 0), stop=(k == KC - 1))
                            sT.inc(ins)
                        for k in range(KC):
                            ins = tensor.matmul(ps_kv[:, kv4 * 512:(kv4 + 1) * 512],
                                                xT_t[:, k * 128:(k + 1) * 128],
                                                wkv_b[:, k * 512:(k + 1) * 512],
                                                start=(k == 0), stop=(k == KC - 1))
                        sT.inc(ins)
                    tE(t_proj)
                    n_proj = sT.n

                    if oth_defer[0] is not None:
                        oth_defer[0]()
                        oth_defer[0] = None

                    if own:
                        m = kt

                        for half in range(2):
                            rec_s([(sT.h, n_proj), (sV.h, qsq_free[0])],
                                  lambda scalar, half=half: scalar.activation(
                                      qsq[:, :], ps_q[:, half * 512:(half + 1) * 512],
                                      AF.Square).then_inc(sS.h, 1))
                            n_ssq = sS.n

                            def v_qn(vector, half=half, n_ssq=n_ssq):
                                sS.wait(vector, n_ssq)
                                sV.wait_all(vector)
                                for s in range(8):
                                    hh = SLOT2HEAD[half * 8 + s]
                                    sV.inc(vector.reduce_sum(nrm[:, hh:hh + 1], qsq[:, s * 64:(s + 1) * 64], axis=AX.X))
                                sV.wait_all(vector)
                            vE(v_qn)
                            qsq_free[0] = sV.n

                        def v_qn2(vector):
                            sV.wait_all(vector)
                            sV.inc(vector.tensor_max(qn_mx[:, :], qn_mx[:, :], nrm[:, 0:16]))
                        vE(v_qn2)
                        qsq_free[0] = sV.n

                        def v_qrope(vector, pb=pb, ntab=None):
                            sLT[pb].wait(vector, tab_ready[kt])
                            sT.wait(vector, own_evt.get(kt - 1, 0))
                            sV.wait_all(vector)
                            x0 = ps_q[:, 0:1024:2]
                            x1 = ps_q[:, 1:1024:2]
                            tc = ct2[:, pb * 512:pb * 512 + 512]
                            ts = st2[:, pb * 512:pb * 512 + 512]
                            sV.inc(vector.tensor_mul(rt1[:, :], x0, tc))
                            sV.inc(vector.tensor_mul(rt2[:, :], x1, ts))
                            sV.wait_all(vector)
                            sV.inc(vector.tensor_sub(qro[:, 0:1024:2], rt1[:, :], rt2[:, :]))
                            sV.wait_all(vector)
                            sV.inc(vector.tensor_mul(rt1[:, :], x0, ts))
                            sV.inc(vector.tensor_mul(rt2[:, :], x1, tc))
                            sV.wait_all(vector)
                            sV.inc(vector.tensor_add(qro[:, 1:1024:2], rt1[:, :], rt2[:, :]))
                        vE(v_qrope)
                        n_qrope = sV.n

                        def emit_own_evac(kt=kt, m=m, ts0=ts0, n_qrope=n_qrope, kb=1024 + (kt % 2) * 256):
                            def t_qtr(tensor):
                                sV.wait(tensor, n_qrope)
                                for k in range(KC):
                                    ins = tensor.transpose(ps_tr[:, ts0 + k * 128:ts0 + (k + 1) * 128],
                                                           qro[:, k * 128:(k + 1) * 128], ident_b[:, :])
                                sT.inc(ins)
                            tE(t_qtr)
                            n_qtr = sT.n

                            def v_qev(vector):
                                sT.wait(vector, n_qtr)
                                for k in range(KC):
                                    sV.inc(vector.tensor_copy(q_pk[k][:, m * 128:(m + 1) * 128],
                                                              ps_tr[:, ts0 + k * 128:ts0 + (k + 1) * 128]))
                                trev_done[kt] = sV.n
                            vE(v_qev)

                            def t_ktr(tensor):
                                sV.wait(tensor, rope_done[kt])
                                sV.wait(tensor, trev_done[kt])
                                for k in range(2):
                                    ins = tensor.transpose(ps_tr[:, ts0 + k * 128:ts0 + (k + 1) * 128],
                                                           qro[:, kb + k * 128:kb + (k + 1) * 128], ident_b[:, :])
                                sT.inc(ins)
                            tE(t_ktr)
                            own_evt[kt] = sT.n
                            n_ktr = sT.n

                            def v_kev(vector):
                                sT.wait(vector, n_ktr)
                                for k in range(2):
                                    sV.inc(vector.tensor_copy(k_pk[k][:, kt * 128:(kt + 1) * 128],
                                                              ps_tr[:, ts0 + k * 128:ts0 + (k + 1) * 128]))
                                trev_done[kt] = sV.n
                            vE(v_kev)

                    kvs = ps_kv[:, kv4 * 512:(kv4 + 1) * 512]

                    rec_s([(sT.h, n_proj), (sV.h, qsq_free[0])],
                          lambda scalar, kvs=kvs: scalar.activation(
                              qsq[:, 0:256], kvs[:, 0:256], AF.Square).then_inc(sS.h, 1))
                    n_sksq = sS.n

                    def v_kn(vector):
                        sS.wait(vector, n_sksq)
                        sV.wait_all(vector)
                        for h in range(HKV):
                            sV.inc(vector.reduce_sum(nrm[:, h:h + 1], qsq[:, h * 64:(h + 1) * 64], axis=AX.X))
                        sV.wait_all(vector)
                        sV.inc(vector.tensor_max(kn_mx[:, 0:HKV], kn_mx[:, 0:HKV], nrm[:, 0:HKV]))
                    vE(v_kn)
                    qsq_free[0] = sV.n

                    obase = (1024 + (kt % 2) * 256) if own else (kt % 2) * 512

                    def v_krope(vector, pb=pb, kv4=kv4, obase=obase):
                        sLT[pb].wait(vector, tab_ready[kt])
                        if not own:
                            sT.wait(vector, oth_ktr.get(kt - 2, 0))
                        sV.wait_all(vector)
                        x0 = AP(ps_kv, kv4 * 512, [[1536, 128], [64, 4], [2, 32]])
                        x1 = AP(ps_kv, kv4 * 512 + 1, [[1536, 128], [64, 4], [2, 32]])
                        tc = AP(ct2, pb * 512, [[1024, 128], [0, 4], [1, 32]])
                        ts = AP(st2, pb * 512, [[1024, 128], [0, 4], [1, 32]])
                        r1 = rt1[:, 0:128].rearrange("p (h j) -> p h j", h=4)
                        r2 = rt2[:, 0:128].rearrange("p (h j) -> p h j", h=4)
                        q0 = AP(qro, obase, [[1536, 128], [64, 4], [2, 32]])
                        q1 = AP(qro, obase + 1, [[1536, 128], [64, 4], [2, 32]])
                        sV.inc(vector.tensor_mul(r1, x0, tc))
                        sV.inc(vector.tensor_mul(r2, x1, ts))
                        sV.wait_all(vector)
                        sV.inc(vector.tensor_sub(q0, r1, r2))
                        sV.wait_all(vector)
                        sV.inc(vector.tensor_mul(r1, x0, ts))
                        sV.inc(vector.tensor_mul(r2, x1, tc))
                        sV.wait_all(vector)
                        sV.inc(vector.tensor_add(q1, r1, r2))
                        sV.wait_all(vector)
                        src = AP(ps_kv, kv4 * 512 + 256, [[1536, 128], [64, 4], [1, 64]])
                        dst = AP(vaug, kt * 66, [[HKV * ST * 66, 128], [ST * 66, 4], [1, 64]])
                        sV.inc(vector.tensor_copy(dst, src))
                        rope_done[kt] = sV.n
                    vE(v_krope)

                    if own:
                        oth_defer[0] = emit_own_evac
                    else:
                        def emit_kev(kt=kt, obase=obase, nrope=None):
                            def t_ktr(tensor, kt=kt, obase=obase):
                                sV.wait(tensor, rope_done[kt])
                                sV.wait(tensor, kev_ev.get(kt - 2, 0))
                                for k in range(2):
                                    ins = tensor.transpose(ps_ktr[:, k * 128:(k + 1) * 128],
                                                           qro[:, obase + k * 128:obase + (k + 1) * 128],
                                                           ident_b[:, :])
                                sT.inc(ins)
                            tE(t_ktr)
                            oth_ktr[kt] = sT.n
                            n_ktr = sT.n

                            def v_kev(vector, kt=kt, n_ktr=n_ktr):
                                sT.wait(vector, n_ktr)
                                for k in range(2):
                                    sV.inc(vector.tensor_copy(k_pk[k][:, kt * 128:(kt + 1) * 128],
                                                              ps_ktr[:, k * 128:(k + 1) * 128]))
                                kev_ev[kt] = sV.n
                            vE(v_kev)
                        oth_defer[0] = emit_kev

                def emit_group(tiles):
                    for gi, (kt, own) in enumerate(tiles):
                        if not own:
                            j = kt - MT

                            def ld(sync, j=j, gi=gi, kt=kt):
                                sV.wait(sync, ln_done.get(kt - 2, 0))
                                sLX[0].inc(sync.dma_start(xbuf[:, gi * D:(gi + 1) * D],
                                                          x_oth[j * 128:(j + 1) * 128, :]), 16)
                            yE(ld)
                    n_gload = sLX[0].n
                    for gi, (kt, own) in enumerate(tiles):
                        src = R[:, kt * D:(kt + 1) * D] if own else xbuf[:, gi * D:(gi + 1) * D]

                        def v_st(vector, gi=gi, src=src, own=own, kt=kt):
                            if not own:
                                sLX[0].wait(vector, n_gload)
                            elif kt == 0:
                                sLR.wait_all(vector)
                            sV.wait_all(vector)
                            sV.inc(vector.bn_stats(stats6[:, 0:6], src[:, 0:512]))
                            sV.inc(vector.bn_stats(stats6[:, 6:12], src[:, 512:1024]))
                            sV.wait_all(vector)
                            sV.inc(vector.bn_aggr(stats2[:, 0:2], stats6[:, 0:12]))
                            sV.wait_all(vector)
                            sV.inc(vector.tensor_copy(means8[:, gi:gi + 1], stats2[:, 0:1]))
                            sV.inc(vector.tensor_scalar_add(lnt[:, gi:gi + 1], stats2[:, 1:2], 1e-5))
                        vE(v_st)
                    nst = sV.n

                    rec_s([(sV.h, nst)],
                          lambda scalar, n=len(tiles): scalar.activation(
                              rstds8[:, 0:n], lnt[:, 0:n], AF.Sqrt).then_inc(sS.h, 1))
                    ns_s = sS.n

                    def v_rstd(vector, n=len(tiles)):
                        sS.wait(vector, ns_s)
                        sV.wait_all(vector)
                        sV.inc(vector.reciprocal(rstds8[:, 0:n], rstds8[:, 0:n]))
                    vE(v_rstd)
                    nrstd = sV.n

                    for gi, (kt, own) in enumerate(tiles):
                        src = R[:, kt * D:(kt + 1) * D] if own else xbuf[:, gi * D:(gi + 1) * D]

                        def v_nm(vector, gi=gi, src=src, kt=kt):
                            sV.wait(vector, nrstd)
                            sV.wait_all(vector)
                            sV.inc(vector.tensor_scalar(xng[:, gi * D:(gi + 1) * D], src,
                                                        means8[:, gi:gi + 1], rstds8[:, gi:gi + 1],
                                                        ALU.subtract, ALU.mult))
                            ln_done[kt] = sV.n
                        vE(v_nm)

                    for gi, (kt, own) in enumerate(tiles):
                        emit_tile_stage2(gi, kt, own)

                tiles_all = [(kt, True) for kt in range(MT)] + [(MT + j, False) for j in range(OT)]
                for i in range(0, len(tiles_all), 2):
                    emit_group(tiles_all[i:i + 2])
                if oth_defer[0] is not None:
                    oth_defer[0]()
                    oth_defer[0] = None

                def s_p123(scalar):
                    for waits, fn in p123_s:
                        for h, v in waits:
                            if v:
                                scalar.wait_ge(h, v)
                        fn(scalar)
                sE(s_p123)

            with contextlib.ExitStack() as esO:
                sbO = lambda n, s, dt: esO.enter_context(nc.sbuf_tensor(n, s, dt))
                otm = sbO("otm", [128, MT * D], BF16)
                wo_b = sbO("wo_b", [128, KC * D], BF16)
                pT = sbO("pT", [128, 2 * QHW], BF16)
                o_sb = sbO("o_sb", [65, 2 * QHW], BF16)
                recb = sbO("recb", [128, QTB], F32)
                brow = sbO("brow", [1, 16], F32)
                oT_t = sbO("oT_t", [128, D], BF16)
                aow = sbO("aow", [128, 1024], F32)
                lnw = sbO("lnw", [128, 2], F32)
                ostat = sbO("ostat", [128, 12], F32)

                def ldwo(gpsimd):
                    for s_ in (sW1[0], sW1[1], sSW[0], sSW[1], sLX[0], sLX[1]):
                        s_.wait_all(gpsimd)
                    sV.wait_all(gpsimd)
                    sS.wait_all(gpsimd)
                    sT.wait_all(gpsimd)
                    sLW.inc(gpsimd.dma_start(wo_b[:, :].rearrange("p (k n) -> p k n", k=KC),
                                        wo_d.rearrange("(k p) n -> p k n", p=128)), 16)
                gE(ldwo)
                n_ldwo = sLW.n

                with contextlib.ExitStack() as esP4:
                    psO = lambda n, s, dt=F32: esP4.enter_context(nc.psum_tensor(n, s, dt))
                    esB = contextlib.ExitStack()
                    ps_bc2 = esB.enter_context(nc.psum_tensor("ps_bc2", [128, 16], F32))
                    ps_nrm = esB.enter_context(nc.psum_tensor("ps_nrm", [128, 256 + 128], F32))

                    def t_bnd0(tensor):
                        sV.wait_all(tensor)
                        sT.inc(tensor.transpose(ps_nrm[0:16, 0:128], qn_mx[:, 0:16], eye128f_b[:, :]))
                        sT.inc(tensor.transpose(ps_nrm[0:4, 128:256], kn_mx[:, 0:4], eye128f_b[:, :]))
                    tE(t_bnd0)
                    nbt0 = sT.n

                    def v_bound1b(vector):
                        sT.wait(vector, nbt0)
                        sV.inc(vector.reduce_max(qn_mx[0:16, 0:1], ps_nrm[0:16, 0:128], axis=AX.X))
                        sV.inc(vector.reduce_max(kn_mx[0:4, 0:1], ps_nrm[0:4, 128:256], axis=AX.X))
                    vE(v_bound1b)
                    nb0b = sV.n

                    def t_bnd1(tensor):
                        sV.wait(tensor, nb0b)
                        sT.inc(tensor.transpose(ps_nrm[0:1, 256:272], qn_mx[0:16, 0:1], eye32f_b[0:16, 0:16]))
                        sT.inc(tensor.transpose(ps_nrm[0:1, 272:276], kn_mx[0:4, 0:1], eye32f_b[0:4, 0:4]))
                    tE(t_bnd1)
                    nbt1 = sT.n

                    def v_bound1c(vector):
                        sT.wait(vector, nbt1)
                        sV.inc(vector.tensor_copy(brow[0:1, :], ps_nrm[0:1, 256:272]))
                        sV.wait_all(vector)
                        src_kn = AP(ps_nrm, 272, [[384, 1], [1, 4], [0, 4]])
                        sV.inc(vector.tensor_mul(brow[0:1, :].rearrange("p (a b) -> p a b", a=4),
                                                 brow[0:1, :].rearrange("p (a b) -> p a b", a=4), src_kn))
                    vE(v_bound1c)
                    nb1 = sV.n

                    def s_bound(scalar):
                        sV.wait(scalar, nb1)
                        sS.inc(scalar.activation(brow[0:1, :], brow[0:1, :], AF.Sqrt))
                    sE(s_bound)
                    nb2 = sS.n

                    def v_bound2(vector):
                        sS.wait(vector, nb2)
                        sV.wait_all(vector)
                        sV.inc(vector.tensor_scalar_mul(brow[0:1, :], brow[0:1, :], -1.02 / 32.0))
                    vE(v_bound2)
                    nb3 = sV.n

                    def t_bound(tensor):
                        sV.wait(tensor, nb3)
                        sT.inc(tensor.matmul(ps_bc2[:, 0:16], onesf_b[0:1, :], brow[0:1, :], start=True, stop=True))
                    tE(t_bound)
                    nb4 = sT.n

                    def v_bound3(vector):
                        sT.wait(vector, nb4)
                        sV.inc(vector.tensor_copy(bias16[:, :], ps_bc2[:, 0:16]))
                    vE(v_bound3)
                    n_bias_v = sV.n
                    esB.close()
                    ps_sA = psO("ps_sA", [128, QHW])
                    ps_sB = psO("ps_sB", [128, QHW])
                    ps_o = psO("ps_o", [128, QHW])
                    ps_otr = esP4.enter_context(nc.psum_tensor("ps_otr", [128, QTB * 66], BF16))
                    ps_warm = psO("ps_warm", [128, 512])

                    NWARM = 6
                    def t_warm(tensor):
                        for j in range(NWARM):
                            tensor.matmul(ps_warm[:, 0:512], ident_b[:, 0:128],
                                          k_pk[0][:, 0:512],
                                          start=(j == 0), stop=False)
                    tE(t_warm)

                    it = [0]
                    prev_bd = [None]
                    otr_ev = {}
                    s_items = []

                    def emit_otr(h, qh, par, base_qt):
                        def t_otr(tensor):
                            sVo.wait(tensor, otr_ev[(h, qh)])
                            sVsc.wait(tensor, sVsc.n)
                            for j in range(QTB):
                                ins = tensor.transpose(ps_otr[:, j * 66:j * 66 + 65],
                                                       o_sb[0:65, par * QHW + j * 128: par * QHW + (j + 1) * 128],
                                                       ident_b[0:65, 0:65])
                            sTo.inc(ins)
                        tE(t_otr)
                        nto = sTo.n

                        def v_osc(vector):
                            sTo.wait(vector, nto)
                            sV.wait_all(vector)
                            rsrc = AP(ps_otr, 64, [[QTB * 66, 128], [66, QTB]])
                            sV.inc(vector.reciprocal(recb[:, 0:QTB], rsrc))
                            sV.wait_all(vector)
                            osrc = AP(ps_otr, 0, [[QTB * 66, 128], [66, QTB], [1, 64]])
                            odst = AP(otm, base_qt * D + h * 64, [[MT * D, 128], [D, QTB], [1, 64]])
                            rb = AP(recb, 0, [[QTB, 128], [1, QTB], [0, 64]])
                            sVsc.inc(vector.tensor_mul(odst, osrc, rb))
                        vE(v_osc)

                    for h in range(H):
                        hkv = h // (H // HKV)
                        if h in PK0:
                            qpi, qoff = PK0.index(h), 0
                        else:
                            qpi, qoff = PK64.index(h), 64
                        koff = 64 * (hkv % 2)
                        kpi = hkv // 2
                        assert koff == qoff
                        for qh in range(NQH):
                            i0 = it[0]
                            nvo0 = sVo.n
                            sTs.n += ST
                            sSe.n += ST
                            sTp.n += ST

                            def t_body(tensor, h=h, qh=qh, i0=i0, qpi=qpi, qoff=qoff,
                                       kpi=kpi, hkv=hkv, nvo0=nvo0, nbv=n_bias_v):
                                def t_s(kt):
                                    i = i0 + kt
                                    ps_s = ps_sA if i % 2 == 0 else ps_sB
                                    if i == 0:
                                        tensor.wait_ge(sV.h, nbv)
                                    if i >= 2:
                                        tensor.wait_ge(sSe.h, i - 1)
                                    for s0_, w in QNS:
                                        ins = tensor.matmul(ps_s[:, s0_:s0_ + w],
                                                            k_pk[kpi][qoff:qoff + 64, kt * 128:(kt + 1) * 128],
                                                            q_pk[qpi][qoff:qoff + 64, qh * QHW + s0_: qh * QHW + s0_ + w],
                                                            start=True, stop=True)
                                    ins.then_inc(sTs.h, 1)
                                    tensor.matmul(ps_warm[:, 0:256], ident_b[:, 0:128],
                                                  k_pk[0][:, 0:256], start=False, stop=False)

                                def t_pv(kt):
                                    i = i0 + kt
                                    buf = i % 2
                                    tensor.wait_ge(sSe.h, i + 1)
                                    if kt == 0:
                                        if nvo0:
                                            tensor.wait_ge(sVo.h, nvo0)
                                    vsl = vaug[:, (hkv * ST + kt) * 66:(hkv * ST + kt) * 66 + 65]
                                    for s0_, w in QNS:
                                        ins = tensor.matmul(ps_o[0:65, s0_:s0_ + w],
                                                            vsl, pT[:, buf * QHW + s0_: buf * QHW + s0_ + w],
                                                            start=(kt == 0), stop=(kt == ST - 1))
                                    ins.then_inc(sTp.h, 1)

                                t_s(0)
                                for kt in range(1, ST):
                                    t_s(kt)
                                    t_pv(kt - 1)
                                t_pv(ST - 1)
                            tE(t_body)

                            s_items.append((h, i0))
                            it[0] += ST

                            par = (h * NQH + qh) % 2

                            def v_oev(vector, h=h, qh=qh, par=par, i=it[0], nto=sTo.n):
                                sTp.wait(vector, i)
                                sTo.wait(vector, nto)
                                sV.wait_all(vector)
                                sVo.inc(vector.tensor_copy(o_sb[0:65, par * QHW:(par + 1) * QHW], ps_o[0:65, 0:QHW]))
                                otr_ev[(h, qh)] = sVo.n
                            vE(v_oev)

                            if prev_bd[0] is not None:
                                emit_otr(*prev_bd[0])
                            prev_bd[0] = (h, qh, par, qh * QTB)
                    emit_otr(*prev_bd[0])

                    def s_all(scalar, nbv=n_bias_v):
                        for h_, i0_ in s_items:
                            if i0_ == 0:
                                scalar.wait_ge(sV.h, nbv)
                            for kt in range(ST):
                                i = i0_ + kt
                                buf = i % 2
                                ps_s = ps_sA if buf == 0 else ps_sB
                                scalar.wait_ge(sTs.h, i + 1)
                                if i >= 2:
                                    scalar.wait_ge(sTp.h, i - 1)
                                scalar.activation(pT[:, buf * QHW:(buf + 1) * QHW], ps_s[:, 0:QHW],
                                                  AF.Exp, scale=1.0 / 32.0,
                                                  bias=bias16[:, h_:h_ + 1]).then_inc(sSe.h, 1)
                    sE(s_all)

                    def t_warm_end(tensor):
                        sT.inc(tensor.matmul(ps_warm[:, 0:512], ident_b[:, 0:128],
                                             k_pk[0][:, 0:512], start=False, stop=True))
                    tE(t_warm_end)
                    n_warm = sT.n

                with contextlib.ExitStack() as esP5:
                    ps_tr5 = esP5.enter_context(nc.psum_tensor("ps_tr5", [128, 1024], BF16))
                    ps_wo = esP5.enter_context(nc.psum_tensor("ps_wo", [128, 2048], F32))
                    p5_ev, p5_ln = {}, {}
                    p5_sink = []

                    for m in range(MT):
                        def t_otr5(tensor, m=m, nvsc=sVsc.n, nse=sSe.n, nw=n_warm):
                            sSe.wait(tensor, nse)
                            sVsc.wait(tensor, nvsc)
                            sT.wait(tensor, nw)
                            sV.wait(tensor, p5_ev.get(m - 1, 0))
                            for k in range(KC):
                                ins = tensor.transpose(ps_tr5[:, k * 128:(k + 1) * 128],
                                                       otm[:, m * D + k * 128: m * D + (k + 1) * 128], ident_b[:, :])
                            sT.inc(ins)
                        tE(t_otr5)
                        n5a = sT.n

                        def v_oev5(vector, m=m):
                            sT.wait(vector, n5a)
                            sV.wait_all(vector)
                            sV.inc(vector.tensor_copy(oT_t[:, :], ps_tr5[:, 0:1024]))
                            p5_ev[m] = sV.n
                        vE(v_oev5)

                        ws0 = (m % 2) * 1024

                        def t_wo(tensor, m=m, ws0=ws0):
                            sV.wait(tensor, p5_ev[m])
                            sLW.wait(tensor, n_ldwo)
                            sV.wait(tensor, p5_ln.get(m - 2, 0))
                            for nn in range(2):
                                for k in range(KC):
                                    ins = tensor.matmul(ps_wo[:, ws0 + nn * 512:ws0 + (nn + 1) * 512],
                                                        oT_t[:, k * 128:(k + 1) * 128],
                                                        wo_b[:, k * D + nn * 512: k * D + nn * 512 + 512],
                                                        start=(k == 0), stop=(k == KC - 1))
                            sT.inc(ins)
                        tE(t_wo)
                        n5b = sT.n

                        def v_lnw(vector, m=m, ws0=ws0):
                            sT.wait(vector, n5b)
                            sV.wait_all(vector)
                            sV.inc(vector.bn_stats(ostat[:, 0:6], ps_wo[:, ws0:ws0 + 512]))
                            sV.inc(vector.bn_stats(ostat[:, 6:12], ps_wo[:, ws0 + 512:ws0 + 1024]))
                            sV.wait_all(vector)
                            sV.inc(vector.bn_aggr(lnw[:, 0:2], ostat[:, 0:12]))
                            sV.wait_all(vector)
                            sV.inc(vector.tensor_scalar_add(lnw[:, 1:2], lnw[:, 1:2], 1e-5))
                        vE(v_lnw)
                        n5c = sV.n

                        def s_lnw(scalar, n5c=n5c):
                            scalar.wait_ge(sV.h, n5c)
                            scalar.activation(lnw[:, 1:2], lnw[:, 1:2], AF.Sqrt).then_inc(sS.h, 1)
                        p5_sink.append(s_lnw)
                        sS.n += 1
                        n5d = sS.n

                        def v_res(vector, m=m, ws0=ws0):
                            sS.wait(vector, n5d)
                            sV.wait_all(vector)
                            sV.inc(vector.reciprocal(lnw[:, 1:2], lnw[:, 1:2]))
                            sV.wait_all(vector)
                            sV.wait_all(vector)
                            sV.inc(vector.tensor_scalar(aow[:, :], ps_wo[:, ws0:ws0 + 1024], lnw[:, 0:1], lnw[:, 1:2],
                                                        ALU.subtract, ALU.mult))
                            sV.wait_all(vector)
                            sV.inc(vector.tensor_add(R[:, m * D:(m + 1) * D], R[:, m * D:(m + 1) * D], aow[:, :]))
                            p5_ln[m] = sV.n
                        vE(v_res)

                    def s_p5(scalar):
                        for b_ in p5_sink:
                            b_(scalar)
                    sE(s_p5)

        if debug_tap == "x1":
            n_x1 = sV.n

            def g_dbg(gpsimd):
                sV.wait(gpsimd, n_x1)
                sSW[0].inc(gpsimd.dma_start(
                    dbg_d[:, :], R[:, 0:min(MT, 4) * 1024]), 16)
            gE(g_dbg)

        n_ldw_all = sLW.n
        n_T_preF, n_S_preF, n_V_preF = sT.n, sS.n, sV.n
        with contextlib.ExitStack() as esF:
            with contextlib.ExitStack() as esF1:
                sbF1 = lambda n, s, dt: esF1.enter_context(nc.sbuf_tensor(n, s, dt))
                xqT = sbF1("xqT", [128, KC * T], BF16)
                xnf = sbF1("xnf", [128, D], F32)
                yq = sbF1("yq", [128, D], F32)
                rq = sbF1("rq", [128, D], F32)
                uq = sbF1("uq", [128, D], F32)
                aq = sbF1("aq", [128, D], BF16)
                xq_b = sbF1("xq_b", [128, D], BF16)
                fst = sbF1("fst", [128, 12], F32)
                fln = sbF1("fln", [128, 4], F32)
                silu_b = sbF1("silu_b", [128, 512], F32)
                gu_b = sbF1("gu_b", [128, 2 * 512], F32)
                tg_b = sbF1("tg_b", [128, 2 * ICB * 512], BF16)
                tu_b = sbF1("tu_b", [128, 2 * ICB * 512], BF16)
                ps_trF = esF1.enter_context(nc.psum_tensor("ps_trF", [128, 1024], BF16))
                ps_g = esF1.enter_context(nc.psum_tensor("ps_g", [128, 2 * 512], F32))
                ps_u = esF1.enter_context(nc.psum_tensor("ps_u", [128, 2 * 512], F32))

                f6 = {}
                p6_sink = []
                for m in range(MT):
                    def v_f6a(vector, m=m):
                        sV.wait_all(vector)
                        sS.wait_all(vector)
                        sV.inc(vector.bn_stats(fst[:, 0:6], R[:, m * D:m * D + 512]))
                        sV.inc(vector.bn_stats(fst[:, 6:12], R[:, m * D + 512:(m + 1) * D]))
                        sV.wait_all(vector)
                        sV.inc(vector.bn_aggr(fln[:, 0:2], fst[:, 0:12]))
                        sV.wait_all(vector)
                        sV.inc(vector.tensor_scalar_add(fln[:, 1:2], fln[:, 1:2], 1e-5))
                    vE(v_f6a)
                    n6a = sV.n

                    def s_f6(scalar, n6a=n6a):
                        scalar.wait_ge(sV.h, n6a)
                        scalar.activation(fln[:, 1:2], fln[:, 1:2], AF.Sqrt).then_inc(sS.h, 1)
                    p6_sink.append(s_f6)
                    sS.n += 1
                    n6b = sS.n

                    def v_f6b(vector, m=m):
                        sS.wait(vector, n6b)
                        sV.wait_all(vector)
                        sV.inc(vector.reciprocal(fln[:, 1:2], fln[:, 1:2]))
                        sV.wait_all(vector)
                        sV.inc(vector.tensor_scalar_sub(xnf[:, :], R[:, m * D:(m + 1) * D], fln[:, 0:1]))
                        sV.wait_all(vector)
                        sV.inc(vector.reduce_max(fln[:, 2:3], xnf[:, :], axis=AX.X, apply_absolute_value=True))
                        sV.wait_all(vector)
                        sV.inc(vector.tensor_mul(fln[:, 2:3], fln[:, 2:3], fln[:, 1:2]))
                        sV.wait_all(vector)
                        sV.inc(vector.tensor_scalar_max(fln[:, 2:3], fln[:, 2:3], 1e-5))
                        sV.wait_all(vector)
                        sV.inc(vector.tensor_scalar_mul(rs_x[:, m:m + 1], fln[:, 2:3], 1.0 / 127.0))
                        sV.wait_all(vector)
                        sV.inc(vector.reciprocal(fln[:, 3:4], rs_x[:, m:m + 1]))
                        sV.wait_all(vector)
                        sV.inc(vector.tensor_mul(fln[:, 3:4], fln[:, 3:4], fln[:, 1:2]))
                        sV.wait_all(vector)
                        sV.inc(vector.tensor_scalar_mul(yq[:, :], xnf[:, :], fln[:, 3:4]))
                        sV.wait_all(vector)
                        sV.inc(vector.tensor_mul(cgu[:, m:m + 1], rs_x[:, m:m + 1], mw2_b[:, 0:1]))
                        sV.inc(vector.tensor_mul(cgu[:, MT + m:MT + m + 1], rs_x[:, m:m + 1], mw2_b[:, 1:2]))
                    vE(v_f6b)
                    items6 = [{"src": yq[:, hh * 512:(hh + 1) * 512],
                               "dst": xq_b[:, hh * 512:(hh + 1) * 512],
                               "f": rq[:, hh * 512:(hh + 1) * 512],
                               "a": aq[:, hh * 512:(hh + 1) * 512],
                               "t": uq[:, hh * 512:(hh + 1) * 512]} for hh in range(2)]
                    emit_round_ladder(blk, sV, sS, items6,
                                      lambda it: it["f"], lambda it: it["a"], lambda it: it["t"],
                                      s_sink=p6_sink)

                    def v_f6sync(vector):
                        sV.wait_all(vector)
                    vE(v_f6sync)
                    n6c = sV.n

                    def t_f6(tensor, m=m):
                        sV.wait(tensor, n6c)
                        sV.wait(tensor, f6.get(m - 1, 0))
                        for k in range(KC):
                            ins = tensor.transpose(ps_trF[:, k * 128:(k + 1) * 128],
                                                   xq_b[:, k * 128:(k + 1) * 128], ident_b[:, :])
                        sT.inc(ins)
                    tE(t_f6)
                    n6d = sT.n

                    def v_f6c(vector, m=m):
                        sT.wait(vector, n6d)
                        sV.wait_all(vector)
                        dst = xqT[:, :].rearrange("p (k t) -> p k t", k=KC)[:, :, m * 128:(m + 1) * 128]
                        src = ps_trF[:, 0:1024].rearrange("p (k j) -> p k j", k=KC)
                        sV.inc(vector.tensor_copy(dst, src))
                        f6[m] = sV.n
                    vE(v_f6c)
                def s_p6(scalar):
                    for b_ in p6_sink:
                        b_(scalar)
                sE(s_p6)
                n_xq = f6[MT - 1]

                f7t, f7v = {}, {}
                p7_sink = []
                for c in range(ICB):
                    cb = c % 2

                    def ld_t(sync, c=c, cb=cb):
                        sSW[0].wait(sync, n_stash[0])
                        sSW[1].wait(sync, n_stash[1])
                        sLW.wait(sync, n_ldw_all)
                        sT.wait(sync, n_T_preF)
                        sS.wait(sync, n_S_preF)
                        sV.wait(sync, n_V_preF)
                        if c >= 2:
                            sT.wait(sync, f7t.get((c - 2, MT - 1), 0))
                        for k in range(KC):
                            sLT[cb].inc(sync.dma_start(
                                tg_b[:, cb * ICB * 512 + k * 512:cb * ICB * 512 + (k + 1) * 512],
                                tg_s[c * 512:(c + 1) * 512, k * 128:(k + 1) * 128], transpose=True), 16)
                            sLT[cb].inc(sync.dma_start(
                                tu_b[:, cb * ICB * 512 + k * 512:cb * ICB * 512 + (k + 1) * 512],
                                tu_s[c * 512:(c + 1) * 512, k * 128:(k + 1) * 128], transpose=True), 16)
                    yE(ld_t)
                    nlt_c = sLT[cb].n

                    for m in range(MT):
                        gb = m % 2

                        def t_f7(tensor, c=c, m=m, cb=cb, gb=gb, nlt_c=nlt_c):
                            sV.wait(tensor, n_xq)
                            sLT[cb].wait(tensor, nlt_c)
                            if m >= 2:
                                sV.wait(tensor, f7v.get((c, m - 2), 0))
                            elif c > 0:
                                sV.wait(tensor, f7v.get((c - 1, MT - 2 + m), 0))
                            for k in range(KC):
                                ins = tensor.matmul(ps_g[:, gb * 512:(gb + 1) * 512],
                                                    xqT[:, k * T + m * 128:k * T + (m + 1) * 128],
                                                    tg_b[:, cb * ICB * 512 + k * 512:cb * ICB * 512 + (k + 1) * 512],
                                                    start=(k == 0), stop=(k == KC - 1))
                            sT.inc(ins)
                            for k in range(KC):
                                ins = tensor.matmul(ps_u[:, gb * 512:(gb + 1) * 512],
                                                    xqT[:, k * T + m * 128:k * T + (m + 1) * 128],
                                                    tu_b[:, cb * ICB * 512 + k * 512:cb * ICB * 512 + (k + 1) * 512],
                                                    start=(k == 0), stop=(k == KC - 1))
                            sT.inc(ins)
                            f7t[(c, m)] = sT.n
                        tE(t_f7)

                        def s_f7(scalar, c=c, m=m, gb=gb,
                                 nft=f7t[(c, m)],
                                 prev=f7v.get((c, m - 1) if m >= 1 else (c - 1, MT - 1), 0)):
                            scalar.wait_ge(sT.h, nft)
                            if prev:
                                scalar.wait_ge(sV.h, prev)
                            scalar.activation(silu_b[:, :], ps_g[:, gb * 512:(gb + 1) * 512],
                                              AF.Silu, scale=cgu[:, m:m + 1]).then_inc(sS.h, 1)
                        p7_sink.append(s_f7)
                        sS.n += 1
                        n7s = sS.n

                        def v_f7(vector, c=c, m=m, gb=gb, n7s=n7s):
                            sS.wait(vector, n7s)
                            i = c * MT + m
                            if i >= 2:
                                sGU[gb].wait(vector, 16 * ((i - 2) // 2 + 1))
                            sV.wait_all(vector)
                            sV.inc(vector.scalar_tensor_tensor(gu_b[:, gb * 512:(gb + 1) * 512],
                                                               ps_u[:, gb * 512:(gb + 1) * 512],
                                                               cgu[:, MT + m:MT + m + 1], silu_b[:, :],
                                                               ALU.mult, ALU.mult))
                            f7v[(c, m)] = sV.n
                        vE(v_f7)

                        def g_f7(gpsimd, c=c, m=m, gb=gb):
                            sV.wait(gpsimd, f7v[(c, m)])
                            sGU[gb].inc(gpsimd.dma_start(gu_s[m * 128:(m + 1) * 128, c * 512:(c + 1) * 512],
                                                         gu_b[:, gb * 512:(gb + 1) * 512]), 16)
                        gE(g_f7)
                def s_p7(scalar):
                    for b_ in p7_sink:
                        b_(scalar)
                sE(s_p7)

                n_gu = (sGU[0].n, sGU[1].n)
                n_lt7 = (sLT[0].n, sLT[1].n)
                n_t7 = sT.n
                n_s7 = sS.n
                n_v7 = sV.n

            with contextlib.ExitStack() as esF2:
                sb8 = lambda n, s, dt: esF2.enter_context(nc.sbuf_tensor(n, s, dt))
                td_b = sb8("td_b", [128, IC * 1024], BF16)
                gur = sb8("gur", [128, 2 * INNER], F32)
                y8 = sb8("y8", [128, INNER], F32)
                a8 = sb8("a8", [128, INNER], BF16)
                guq = sb8("guq", [128, INNER], BF16)
                guqT = sb8("guqT", [128, INNER], BF16)
                f8s = sb8("f8s", [128, 8], F32)
                ost8 = sb8("ost8", [128, 12], F32)
                outb = sb8("outb", [128, D], F32)
                ps_tr8 = esF2.enter_context(nc.psum_tensor("ps_tr8", [128, 1024], BF16))
                ps_d = esF2.enter_context(nc.psum_tensor("ps_d", [128, 1024], F32))
                ps_r8 = esF2.enter_context(nc.psum_tensor("ps_r8", [128, 2048], F32))

                def ld_td(sync):
                    sSW[0].wait(sync, n_stash[0])
                    sSW[1].wait(sync, n_stash[1])
                    sLT[0].wait(sync, n_lt7[0])
                    sLT[1].wait(sync, n_lt7[1])
                    sT.wait(sync, n_t7)
                    sS.wait(sync, n_s7)
                    sV.wait(sync, n_v7)
                    sGU[0].wait(sync, n_gu[0])
                    sGU[1].wait(sync, n_gu[1])
                    for ic in range(IC):
                        sTD.inc(sync.dma_start(td_b[:, ic * 1024:(ic + 1) * 1024],
                                               td_s[0:1024, ic * 128:(ic + 1) * 128], transpose=True), 16)
                yE(ld_td)
                n_td = sTD.n

                f8, f8y, f8tr_ev, f8dn = {}, {}, {}, {}
                pool_end = {}
                p8_sink = []
                p8_tail = [None]
                for m in range(MT):
                    gb = m % 2

                    g8 = (m % 2) * INNER

                    def ld_gu(sync, m=m, g8=g8):
                        sGU[0].wait(sync, n_gu[0])
                        sGU[1].wait(sync, n_gu[1])
                        if m >= 2:
                            sV.wait(sync, f8.get(m - 2, 0))
                        sLT[m % 2].inc(sync.dma_start(gur[:, g8:g8 + INNER], gu_s[m * 128:(m + 1) * 128, :]), 16)
                    yE(ld_gu)
                    n_lg = sLT[m % 2].n

                    def v_f8a(vector, m=m, n_lg=n_lg, g8=g8):
                        sLT[m % 2].wait(vector, n_lg)
                        sS.wait_all(vector)
                        sV.wait_all(vector)
                        fs = (m % 2) * 4
                        sV.inc(vector.reduce_max(f8s[:, fs:fs + 1], gur[:, g8:g8 + INNER], axis=AX.X, apply_absolute_value=True))
                        sV.wait_all(vector)
                        sV.inc(vector.tensor_scalar_max(f8s[:, fs:fs + 1], f8s[:, fs:fs + 1], 1e-5))
                        sV.wait_all(vector)
                        sV.inc(vector.tensor_scalar_mul(f8s[:, fs + 1:fs + 2], f8s[:, fs:fs + 1], 1.0 / 127.0))
                        sV.wait_all(vector)
                        sV.inc(vector.reciprocal(f8s[:, fs + 2:fs + 3], f8s[:, fs + 1:fs + 2]))
                        sV.wait_all(vector)
                        sV.inc(vector.tensor_mul(f8s[:, fs + 3:fs + 4], f8s[:, fs + 1:fs + 2], mw2_b[:, 2:3]))
                        sV.wait_all(vector)
                        sV.inc(vector.tensor_scalar_mul(y8[:, :], gur[:, g8:g8 + INNER], f8s[:, fs + 2:fs + 3]))
                        sV.wait_all(vector)
                        f8y[m] = sV.n
                    vE(v_f8a)
                    HF = INNER // 2
                    items8 = [{"src": y8[:, hh * HF:(hh + 1) * HF],
                               "dst": guq[:, hh * HF:(hh + 1) * HF],
                               "f": gur[:, g8 + hh * HF:g8 + (hh + 1) * HF],
                               "a": a8[:, hh * HF:(hh + 1) * HF],
                               "t": ps_r8[:, :]} for hh in range(2)]
                    emit_round_ladder(blk, sV, sS, items8,
                                      lambda it: it["f"], lambda it: it["a"], lambda it: it["t"],
                                      s_sink=p8_sink)
                    pool_end[m] = sP.n

                    if p8_tail[0] is not None:
                        p8_tail[0]()
                        p8_tail[0] = None

                    def v_f8sync(vector, m=m):
                        sV.wait_all(vector)
                    vE(v_f8sync)
                    f8[m] = sV.n

                    for b in range(4):
                        def t_b(tensor, m=m, b=b):
                            sV.wait(tensor, f8[m])
                            sP.wait(tensor, pool_end[m])
                            prev = f8tr_ev.get((m, b - 1), 0) if b > 0 else f8tr_ev.get((m - 1, 3), 0)
                            sV.wait(tensor, prev)
                            for k in range(8):
                                ins = tensor.transpose(ps_tr8[:, k * 128:(k + 1) * 128],
                                                       guq[:, b * 1024 + k * 128: b * 1024 + (k + 1) * 128],
                                                       ident_b[:, :])
                            sT.inc(ins)
                        tE(t_b)
                        n8t = sT.n

                        def v_b(vector, m=m, b=b, n8t=n8t):
                            sT.wait(vector, n8t)
                            sV.wait_all(vector)
                            sV.inc(vector.tensor_copy(guqT[:, b * 1024:(b + 1) * 1024], ps_tr8[:, 0:1024]))
                            f8tr_ev[(m, b)] = sV.n
                        vE(v_b)

                    def t_down(tensor, m=m):
                        sV.wait(tensor, f8tr_ev[(m, 3)])
                        sTD.wait(tensor, n_td)
                        sV.wait(tensor, f8dn.get(m - 1, 0))
                        for nn in range(2):
                            for ic in range(IC):
                                ins = tensor.matmul(ps_d[:, nn * 512:(nn + 1) * 512],
                                                    guqT[:, ic * 128:(ic + 1) * 128],
                                                    td_b[:, ic * 1024 + nn * 512: ic * 1024 + nn * 512 + 512],
                                                    start=(ic == 0), stop=(ic == IC - 1))
                        sT.inc(ins)
                    tE(t_down)
                    n8d = sT.n

                    def emit_tail(m=m, n8d=n8d, fs=(m % 2) * 4):
                        def v_f8b(vector):
                            sT.wait(vector, n8d)
                            sV.wait_all(vector)
                            sV.inc(vector.tensor_scalar_mul(y8[:, 0:1024], ps_d[:, 0:1024], f8s[:, fs + 3:fs + 4]))
                            sV.wait_all(vector)
                            sV.inc(vector.bn_stats(ost8[:, 0:6], y8[:, 0:512]))
                            sV.inc(vector.bn_stats(ost8[:, 6:12], y8[:, 512:1024]))
                            sV.wait_all(vector)
                            sV.inc(vector.bn_aggr(f8s[:, fs:fs + 2], ost8[:, 0:12]))
                            sV.wait_all(vector)
                            sV.inc(vector.tensor_scalar_add(f8s[:, fs + 1:fs + 2], f8s[:, fs + 1:fs + 2], 1e-5))
                        vE(v_f8b)
                        n8e = sV.n

                        def s_f8(scalar, n8e=n8e):
                            scalar.wait_ge(sV.h, n8e)
                            scalar.activation(f8s[:, fs + 1:fs + 2], f8s[:, fs + 1:fs + 2],
                                              AF.Sqrt).then_inc(sS.h, 1)
                        p8_sink.append(s_f8)
                        sS.n += 1
                        n8f = sS.n

                        def v_f8c(vector, n8f=n8f):
                            sS.wait(vector, n8f)
                            sV.wait_all(vector)
                            sV.inc(vector.reciprocal(f8s[:, fs + 1:fs + 2], f8s[:, fs + 1:fs + 2]))
                            if m >= 1:
                                sOut[0].wait(vector, 16 * m)
                            sV.wait_all(vector)
                            sV.inc(vector.tensor_scalar(y8[:, 0:1024], y8[:, 0:1024], f8s[:, fs:fs + 1],
                                                        f8s[:, fs + 1:fs + 2], ALU.subtract, ALU.mult))
                            sV.wait_all(vector)
                            sV.inc(vector.tensor_add(outb[:, :], R[:, m * D:(m + 1) * D], y8[:, 0:1024]))
                            f8dn[m] = sV.n
                        vE(v_f8c)

                        def g_out(gpsimd):
                            sV.wait(gpsimd, f8dn[m])
                            sOut[0].inc(gpsimd.dma_start(out_d[m * 128:(m + 1) * 128, :], outb[:, :]), 16)
                        gE(g_out)
                    p8_tail[0] = emit_tail

                if p8_tail[0] is not None:
                    p8_tail[0]()
                    p8_tail[0] = None

                def s_p8(scalar):
                    for b_ in p8_sink:
                        b_(scalar)
                sE(s_p8)

                def g_fin(gpsimd):
                    sOut[0].wait_all(gpsimd)
                    sOut[1].wait_all(gpsimd)
                gE(g_fin)

    return nc


def kernel(L0_attn_x, L0_attn_Wq, L0_attn_Wk, L0_attn_Wv, L0_attn_Wo,
           L0_ffn_w_gate, L0_ffn_w_up, L0_ffn_w_down):
    x = np.ascontiguousarray(np.asarray(L0_attn_x, dtype=np.float32))
    B, S, D = x.shape
    T = S // 2
    wq = permute_wq(np.asarray(L0_attn_Wq, dtype=np.float32))
    wk = np.asarray(L0_attn_Wk, dtype=np.float32)
    wv = np.asarray(L0_attn_Wv, dtype=np.float32)
    wkv = np.ascontiguousarray(np.concatenate([wk, wv], axis=1))
    wo = np.ascontiguousarray(np.asarray(L0_attn_Wo, dtype=np.float32))
    wg = np.ascontiguousarray(np.asarray(L0_ffn_w_gate, dtype=np.float32))
    wu = np.ascontiguousarray(np.asarray(L0_ffn_w_up, dtype=np.float32))
    wd = np.ascontiguousarray(np.asarray(L0_ffn_w_down, dtype=np.float32))

    nc = build_program(T=T, S=S)
    in_maps = []
    for c in range(2 * B):
        b, hh = c // 2, c % 2
        own = np.ascontiguousarray(x[b, hh * T:(hh + 1) * T])
        oth = np.ascontiguousarray(x[b, (1 - hh) * T:(2 - hh) * T])
        ctab, stab = core_tables(hh * T, (1 - hh) * T, T, s_max=S)
        in_maps.append({"x_own": own, "x_oth": oth, "wq": wq, "wkv": wkv,
                        "wo": wo, "wg": wg, "wu": wu, "wd": wd,
                        "ctab": ctab, "stab": stab})
    res = run_bass_kernel_spmd(nc, in_maps, core_ids=list(range(len(in_maps))))
    out = np.empty((B, S, D), dtype=np.float32)
    for c in range(2 * B):
        b, hh = c // 2, c % 2
        out[b, hh * T:(hh + 1) * T] = res.results[c]["out"]
    return out


# version 28 (final)
# speedup vs baseline: 1.0272x; 1.0272x over previous
import contextlib
import math
import os

import numpy as np
import ml_dtypes

import concourse.bass as bass
import concourse.mybir as mybir
from concourse.bass_utils import run_bass_kernel_spmd

F32 = mybir.dt.float32
BF16 = mybir.dt.bfloat16
F16 = mybir.dt.float16
I32 = mybir.dt.int32
AF = mybir.ActivationFunctionType
ALU = mybir.AluOpType
AX = mybir.AxisListType
AP = bass.AP

TWO_PI = 2.0 * math.pi
TWO_PI_HI = float(np.float32(TWO_PI))
TWO_PI_LO = float(np.float32(TWO_PI - TWO_PI_HI))
INV_2PI = float(np.float32(1.0 / TWO_PI))
LN1E4_OVER_512 = math.log(10000.0) / 512.0
CE_HI = float(np.float32(LN1E4_OVER_512))
CE_LO = float(np.float32(LN1E4_OVER_512 - CE_HI))
HALF_PI = float(np.float32(math.pi / 2.0))
HALF_PI_LO = float(np.float32(math.pi / 2.0 - HALF_PI))
PI_CLIP = float(np.float32(math.pi))


PK0 = [0, 1, 2, 3, 8, 9, 10, 11]
PK64 = [4, 5, 6, 7, 12, 13, 14, 15]
SLOT2HEAD = []
for _k in range(8):
    SLOT2HEAD += [PK0[_k], PK64[_k]]


def permute_wq(wq):
    cols = []
    for k in range(8):
        cols.append(wq[:, PK0[k] * 64:(PK0[k] + 1) * 64])
        cols.append(wq[:, PK64[k] * 64:(PK64[k] + 1) * 64])
    return np.ascontiguousarray(np.concatenate(cols, axis=1))


def q_freq_perm():
    f = np.zeros((1, 512), dtype=np.float32)
    for s in range(16):
        h = SLOT2HEAD[s]
        f[0, s * 32:(s + 1) * 32] = h * 32 + np.arange(32)
    return f



_TABLES = {}


def rope_tables_full(s_max=4096):
    key = ("cs", s_max)
    if key not in _TABLES:
        import subprocess
        import sys
        import tempfile
        child = (
            "import sys\n"
            "import numpy as np\n"
            "import jax.numpy as jnp\n"
            "inv = 1.0 / 10000.0 ** (jnp.arange(0, 1024, 2, dtype=jnp.float32) / 1024)\n"
            "ang = jnp.arange(%d, dtype=jnp.float32)[:, None] * inv[None, :]\n"
            "np.save(sys.argv[1], np.asarray(jnp.cos(ang)))\n"
            "np.save(sys.argv[2], np.asarray(jnp.sin(ang)))\n"
        ) % s_max
        env = dict(os.environ)
        env.pop("TRN_TERMINAL_POOL_IPS", None)
        env["PYTHONPATH"] = ""
        env["JAX_PLATFORMS"] = "cpu"
        with tempfile.TemporaryDirectory() as td:
            fc = os.path.join(td, "c.npy")
            fs = os.path.join(td, "s.npy")
            subprocess.run([sys.executable, "-c", child, fc, fs], env=env,
                           check=True, capture_output=True, timeout=900)
            c = np.load(fc)
            s = np.load(fs)
        perm = q_freq_perm().ravel().astype(np.int64)
        _TABLES[key] = (np.ascontiguousarray(c[:, perm]),
                        np.ascontiguousarray(s[:, perm]))
    return _TABLES[key]


def core_tables(own0, oth0, T, s_max=4096):
    c, s = rope_tables_full(s_max)
    idx = np.concatenate([np.arange(own0, own0 + T), np.arange(oth0, oth0 + T)])
    return np.ascontiguousarray(c[idx]), np.ascontiguousarray(s[idx])


ROUND_BITS = [128.0, 64.0, 32.0, 16.0, 8.0, 4.0, 2.0, 1.0]


def emit_round_ladder(blk, sV, sS, items, fbuf_of, abuf_of, tbuf_of,
                      s_sink=None, sP=None, pool_waits=(), s_pre_waits=()):
    nlev = len(ROUND_BITS)
    ni = len(items)
    s0, v0 = sS.n, sV.n
    ready = v0
    sc = lambda k, idx: s0 + k * ni + idx + 1
    vc = lambda k, idx: v0 + k * ni + idx + 1
    n_s = (nlev + 1) * ni
    n_v = nlev * ni + (0 if sP else 2 * ni)
    sS.n += n_s
    sV.n += n_v
    p0 = sP.n if sP else 0
    if sP:
        sP.n += 4 * ni

    def s_body(scalar):
        for h, v in s_pre_waits:
            if v:
                scalar.wait_ge(h, v)
        D = 128.0
        for k, b in enumerate(ROUND_BITS):
            c = b - D
            for idx, it in enumerate(items):
                f_in = it["src"] if k == 0 else fbuf_of(it)
                if k == 0:
                    scalar.wait_ge(sV.h, ready) if ready else None
                else:
                    scalar.wait_ge(sV.h, vc(k - 1, idx))
                scalar.activation(abuf_of(it), f_in, AF.Sign, bias=-c).then_inc(sS.h, 1)
            D -= b / 2.0
        for idx, it in enumerate(items):
            scalar.wait_ge(sV.h, vc(nlev - 1, idx))
            scalar.activation(abuf_of(it), fbuf_of(it), AF.Sign).then_inc(sS.h, 1)
    if s_sink is not None:
        s_sink.append(s_body)
    else:
        blk.scalar(s_body)

    def v_body(vector):
        if v0:
            vector.wait_ge(sV.h, v0)
        for k, b in enumerate(ROUND_BITS):
            for idx, it in enumerate(items):
                f_in = it["src"] if k == 0 else fbuf_of(it)
                vector.wait_ge(sS.h, sc(k, idx))
                vector.scalar_tensor_tensor(fbuf_of(it), abuf_of(it), -b / 2.0, f_in,
                                            ALU.mult, ALU.add).then_inc(sV.h, 1)
        if sP is None:
            for idx, it in enumerate(items):
                vector.wait_ge(sS.h, s0 + nlev * ni + idx + 1)
                vector.wait_ge(sV.h, v0 + nlev * ni + 2 * idx)
                vector.scalar_tensor_tensor(tbuf_of(it), abuf_of(it), 0.5, it["src"],
                                            ALU.mult, ALU.add).then_inc(sV.h, 1)
                vector.wait_ge(sV.h, v0 + nlev * ni + 2 * idx + 1)
                vector.tensor_sub(it["dst"], tbuf_of(it), fbuf_of(it)).then_inc(sV.h, 1)
    blk.vector(v_body)

    if sP is not None:
        def p_body(gpsimd):
            for h, v in pool_waits:
                if v:
                    gpsimd.wait_ge(h, v)
            gpsimd.wait_ge(sP.h, p0)
            for idx, it in enumerate(items):
                gpsimd.wait_ge(sV.h, vc(nlev - 1, idx))
                gpsimd.wait_ge(sS.h, sc(nlev, idx))
                gpsimd.wait_ge(sP.h, p0 + 4 * idx)
                gpsimd.tensor_sub(tbuf_of(it), it["src"], fbuf_of(it)).then_inc(sP.h, 1)
                gpsimd.tensor_scalar(abuf_of(it), abuf_of(it), 0.5, None,
                                     ALU.mult).then_inc(sP.h, 1)
                gpsimd.wait_ge(sP.h, p0 + 4 * idx + 2)
                gpsimd.tensor_add(tbuf_of(it), tbuf_of(it), abuf_of(it)).then_inc(sP.h, 1)
                gpsimd.wait_ge(sP.h, p0 + 4 * idx + 3)
                gpsimd.tensor_scalar(it["dst"], tbuf_of(it), -128.0, 127.0,
                                     ALU.max, ALU.min).then_inc(sP.h, 1)
        blk.gpsimd(p_body)


class Sem:
    def __init__(self, nc, es, name):
        self.h = es.enter_context(nc.semaphore(name))
        self.n = 0

    def inc(self, instr, k=1):
        instr.then_inc(self.h, k)
        self.n += k
        return instr

    def wait_all(self, eng):
        if self.n:
            eng.wait_ge(self.h, self.n)

    def wait(self, eng, v):
        if v > 0:
            eng.wait_ge(self.h, v)


def build_program(T=2048, S=4096, qhw=None, debug_tap=None):
    D, H, HKV, INNER = 1024, 16, 4, 4096
    MT, ST = T // 128, S // 128
    OT = ST - MT
    HD = D // H
    DKV = HD * HKV
    KC = D // 128
    ICB = INNER // 512
    IC = INNER // 128
    WT = INNER // 128
    QHW = qhw or (1024 if T >= 2048 else T)
    NQH = T // QHW
    QNS = [(s0, min(512, QHW - s0)) for s0 in range(0, QHW, 512)]
    QTB = QHW // 128

    nc = bass.Bass("TRN2", target_bir_lowering=False)
    dram = lambda n, s, dt, k: nc.dram_tensor(n, s, dt, kind=k).ap()
    x_own = dram("x_own", [T, D], F32, "ExternalInput")
    x_oth = dram("x_oth", [S - T, D], F32, "ExternalInput")
    wq_d = dram("wq", [D, D], F32, "ExternalInput")
    wkv_d = dram("wkv", [D, 2 * DKV], F32, "ExternalInput")
    wo_d = dram("wo", [D, D], F32, "ExternalInput")
    wg_d = dram("wg", [INNER, D], F32, "ExternalInput")
    wu_d = dram("wu", [INNER, D], F32, "ExternalInput")
    wd_d = dram("wd", [D, INNER], F32, "ExternalInput")
    out_d = dram("out", [T, D], F32, "ExternalOutput")
    ctab_d = dram("ctab", [S, 512], F32, "ExternalInput")
    stab_d = dram("stab", [S, 512], F32, "ExternalInput")
    tg_s = dram("tg_s", [INNER, D], BF16, "Internal")
    tu_s = dram("tu_s", [INNER, D], BF16, "Internal")
    td_s = dram("td_s", [D, INNER], BF16, "Internal")
    gu_s = dram("gu_s", [T, INNER], F32, "Internal")
    ident_d = nc.inline_tensor(np.eye(128, dtype=ml_dtypes.bfloat16), name="ident128").ap()
    onesf_d = nc.inline_tensor(np.ones((1, 128), dtype=np.float32), name="ones1f").ap()
    onescol_d = nc.inline_tensor(np.ones((128, 1), dtype=np.float32), name="onescolf").ap()
    eye32f_d = nc.inline_tensor(np.eye(32, dtype=np.float32), name="eye32f").ap()
    eye128f_d = nc.inline_tensor(np.eye(128, dtype=np.float32), name="eye128f").ap()
    dbg_d = dram("dbg", [128, min(T // 128, 4) * 1024], F32, "ExternalOutput") if debug_tap else None

    es = contextlib.ExitStack()
    with es:
        sb = lambda n, s, dt: es.enter_context(nc.sbuf_tensor(n, s, dt))

        R = sb("R", [128, MT * D], F32)
        ident_b = sb("ident_b", [128, 128], BF16)
        onesf_b = sb("onesf_b", [1, 128], F32)
        onescol_b = sb("onescol_b", [128, 1], F32)
        eye32f_b = sb("eye32f_b", [32, 32], F32)
        eye128f_b = sb("eye128f_b", [128, 128], F32)
        qn_mx = sb("qn_mx", [128, H], F32)
        kn_mx = sb("kn_mx", [128, HKV], F32)
        bias16 = sb("bias16", [128, H], F32)
        rs_x = sb("rs_x", [128, MT], F32)
        cgu = sb("cgu", [128, 2 * MT], F32)
        wmean = sb("wmean", [128, 4], F32)
        wsc = sb("wsc", [128, 4], F32)
        mw2_row = sb("mw2_row", [1, 4], F32)
        mw2_b = sb("mw2_b", [128, 4], F32)
        wrow = sb("wrow", [1, 4], F32)
        chalf = sb("chalf", [128, 2], F32)

        SN = ("sLB", "sLR", "sLX0", "sLX1", "sLW", "sLT0", "sLT1",
              "sW1a", "sW1b", "sSWa", "sSWb", "sGUa", "sGUb",
              "sOuta", "sOutb", "sTD",
              "sT", "sV", "sS", "sG", "sTs", "sSe", "sTp", "sVo", "sTo", "sVsc", "sP")
        sm = {n: Sem(nc, es, n) for n in SN}
        sLB, sLR, sLW = sm["sLB"], sm["sLR"], sm["sLW"]
        sLX = [sm["sLX0"], sm["sLX1"]]
        sLT = [sm["sLT0"], sm["sLT1"]]
        sW1 = [sm["sW1a"], sm["sW1b"]]
        sSW = [sm["sSWa"], sm["sSWb"]]
        sGU = [sm["sGUa"], sm["sGUb"]]
        sOut = [sm["sOuta"], sm["sOutb"]]
        sTD = sm["sTD"]
        sT, sV, sS, sG = sm["sT"], sm["sV"], sm["sS"], sm["sG"]
        sTs, sSe, sTp = sm["sTs"], sm["sSe"], sm["sTp"]
        sVo, sTo, sVsc = sm["sVo"], sm["sTo"], sm["sVsc"]
        sP = sm["sP"]

        blk = es.enter_context(nc.Block())
        tE, vE, sE, gE, yE = blk.tensor, blk.vector, blk.scalar, blk.gpsimd, blk.sync

        def p0(sync):
            sLB.inc(sync.dma_start(ident_b[:, :], ident_d[:, :]), 16)
            sLB.inc(sync.dma_start(onesf_b[:, :], onesf_d[:, :]), 16)
            sLB.inc(sync.dma_start(onescol_b[:, :], onescol_d[:, :]), 16)
            sLB.inc(sync.dma_start(eye32f_b[:, :], eye32f_d[:, :]), 16)
            sLB.inc(sync.dma_start(eye128f_b[:, :], eye128f_d[:, :]), 16)
            sLR.inc(sync.dma_start(R[:, :].rearrange("p (m d) -> p m d", d=D),
                                   x_own.rearrange("(m p) d -> p m d", p=128)), 16)
        yE(p0)
        n_sLB = sLB.n

        def p0v(vector):
            sV.inc(vector.memset(qn_mx[:, :], 0.0))
            sV.inc(vector.memset(kn_mx[:, :], 0.0))
            sV.inc(vector.memset(wmean[:, :], 0.0))
            sV.inc(vector.memset(chalf[:, 0:1], -0.5))
            sV.inc(vector.memset(chalf[:, 1:2], 0.5))
        vE(p0v)


        with contextlib.ExitStack() as esW:
            sbW = lambda n, s, dt: esW.enter_context(nc.sbuf_tensor(n, s, dt))
            wbig = sbW("wbig", [128, 2 * 4096], F32)
            sgA = sbW("sgA", [128, 4096], BF16)
            sgB = sbW("sgB", [128, 4096], BF16)
            t2big = sbW("t2big", [128, 2 * 4096], BF16)
            wred = sbW("wredW", [128, 1], F32)
            ps_w = esW.enter_context(nc.psum_tensor("ps_w", [128, 16], F32))

            wd_list = [(wg_d, tg_s, 0), (wu_d, tu_s, 1), (wd_d, td_s, 2)]

            def tern_chunk_aps(wi, w, stash, ci, buf):
                if wi < 2:
                    ws = w[ci * 512:(ci + 1) * 512, :].rearrange("(c p) d -> p c d", p=128)
                    ss = stash[ci * 512:(ci + 1) * 512, :].rearrange("(c p) d -> p c d", p=128)
                    bb = buf.rearrange("p (c d) -> p c d", d=1024)
                else:
                    ws = w[ci * 128:(ci + 1) * 128, :]
                    ss = stash[ci * 128:(ci + 1) * 128, :]
                    bb = buf
                return ws, ss, bb

            units = [(wi, w, stash, ci) for (w, stash, wi) in wd_list for ci in range(8)]
            tvd = {}
            for u, (wi, w, stash, ci) in enumerate(units):
                pb = u % 2
                wb = wbig[:, pb * 4096:(pb + 1) * 4096]
                wsrc, _, wbv = tern_chunk_aps(wi, w, stash, ci, wb)

                def ld(sync, u=u, pb=pb, wbv=wbv, wsrc=wsrc):
                    if u >= 2:
                        sV.wait(sync, tvd[u - 2])
                    sW1[pb].inc(sync.dma_start(wbv, wsrc), 16)
                yE(ld)

                def v(vector, u=u, pb=pb, wb=wb, wi=wi):
                    sW1[pb].wait(vector, 16 * (u // 2 + 1))
                    sV.wait_all(vector)
                    sV.inc(vector.reduce_sum(wred[:, 0:1], wb, axis=AX.X, apply_absolute_value=True))
                    sV.wait_all(vector)
                    sV.inc(vector.tensor_add(wmean[:, wi:wi + 1], wmean[:, wi:wi + 1], wred[:, 0:1]))
                    tvd[u] = sV.n
                vE(v)

            def t_fin(tensor):
                sV.wait_all(tensor)
                sLB.wait(tensor, n_sLB)
                sT.inc(tensor.matmul(ps_w[0:1, 8:11], onescol_b[:, 0:1], wmean[:, 0:3], start=True, stop=True))
            tE(t_fin)
            ntf = sT.n

            def v_fin(vector):
                sT.wait(vector, ntf)
                sV.wait_all(vector)
                sV.inc(vector.tensor_scalar_mul(wrow[0:1, 0:3], ps_w[0:1, 8:11], 1.0 / (INNER * D)))
                sV.wait_all(vector)
                sV.inc(vector.tensor_scalar_max(wrow[0:1, 0:3], wrow[0:1, 0:3], 1e-5))
                sV.wait_all(vector)
                sV.inc(vector.tensor_scalar_mul(mw2_row[0:1, 0:3], wrow[0:1, 0:3], 0.5))
                sV.wait_all(vector)
                sV.inc(vector.reciprocal(wrow[0:1, 0:3], wrow[0:1, 0:3]))
                sV.wait_all(vector)
            vE(v_fin)
            nvf = sV.n

            def t_fin2(tensor):
                sV.wait(tensor, nvf)
                sT.inc(tensor.matmul(ps_w[:, 0:3], onesf_b[0:1, :], wrow[0:1, 0:3], start=True, stop=True))
                sT.inc(tensor.matmul(ps_w[:, 4:7], onesf_b[0:1, :], mw2_row[0:1, 0:3], start=True, stop=True))
            tE(t_fin2)
            ntf2 = sT.n

            def v_fin2(vector):
                sT.wait(vector, ntf2)
                sV.inc(vector.tensor_copy(wsc[:, 0:3], ps_w[:, 0:3]))
                sV.inc(vector.tensor_copy(mw2_b[:, 0:3], ps_w[:, 4:7]))
                sV.wait_all(vector)
            vE(v_fin2)
            n_wsc_v = sV.n

            tsd, tvd2 = {}, {}
            s2base = sS.n
            v2base = sV.n
            for u in range(len(units)):
                tsd[u] = s2base + 2 * (u + 1)
                tvd2[u] = v2base + (u + 1)
            sS.n = s2base + 2 * len(units)
            sV.n = v2base + len(units)

            for u, (wi, w, stash, ci) in enumerate(units):
                pb = u % 2
                wsrc, wdst, wbv = tern_chunk_aps(wi, w, stash, ci, wbig[:, pb * 4096:(pb + 1) * 4096])
                _, _, t2v = tern_chunk_aps(wi, w, stash, ci, t2big[:, pb * 4096:(pb + 1) * 4096])

                def ld(sync, u=u, pb=pb, wbv=wbv, wsrc=wsrc):
                    if u >= 2:
                        sS.wait(sync, tsd[u - 2])
                    else:
                        sV.wait(sync, tvd[len(units) - 2 + u])
                    sW1[pb].inc(sync.dma_start(wbv, wsrc), 16)
                yE(ld)

                def g(gpsimd, u=u, pb=pb, t2v=t2v, wdst=wdst):
                    sV.wait(gpsimd, tvd2[u])
                    sSW[pb].inc(gpsimd.dma_start(wdst, t2v), 16)
                gE(g)

            def s_all2(scalar):
                for u, (wi, w, stash, ci) in enumerate(units):
                    pb = u % 2
                    wb = wbig[:, pb * 4096:(pb + 1) * 4096]
                    scalar.wait_ge(sW1[pb].h, 16 * (12 + u // 2 + 1))
                    scalar.wait_ge(sV.h, n_wsc_v)
                    scalar.wait_ge(sV.h, tvd2.get(u - 1, 0))
                    scalar.activation(sgA[:, :], wb, AF.Sign, scale=wsc[:, wi:wi + 1],
                                      bias=chalf[:, 0:1]).then_inc(sS.h, 1)
                    scalar.activation(sgB[:, :], wb, AF.Sign, scale=wsc[:, wi:wi + 1],
                                      bias=chalf[:, 1:2]).then_inc(sS.h, 1)
            sE(s_all2)

            def v_all2(vector):
                for u in range(len(units)):
                    pb = u % 2
                    t2 = t2big[:, pb * 4096:(pb + 1) * 4096]
                    vector.wait_ge(sSW[pb].h, 16 * (u // 2 - 1 + 1) if u >= 2 else 0)
                    vector.wait_ge(sS.h, tsd[u])
                    vector.tensor_add(t2, sgA[:, :], sgB[:, :]).then_inc(sV.h, 1)
            vE(v_all2)
            n_stash = (sSW[0].n, sSW[1].n)
            n_ternW = (sW1[0].n, sW1[1].n, sS.n, sV.n)

        with contextlib.ExitStack() as esQ:
            sbQ = lambda n, s, dt: esQ.enter_context(nc.sbuf_tensor(n, s, dt))
            q_pk = [sbQ(f"qp_{i}", [128, T], BF16) for i in range(8)]
            k_pk = [sbQ(f"kp_{i}", [128, S], BF16) for i in range(2)]
            vaug = sbQ("vaug", [128, HKV * ST * 66], BF16)

            def tern_barrier(eng):
                sW1[0].wait(eng, n_ternW[0])
                sW1[1].wait(eng, n_ternW[1])
                sS.wait(eng, n_ternW[2])
                sV.wait(eng, n_ternW[3])
                sSW[0].wait(eng, n_stash[0])
                sSW[1].wait(eng, n_stash[1])
            for _emit in (vE, sE, tE, gE, yE):
                _emit(tern_barrier)

            def p0v2(vector):
                sV.inc(vector.memset(vaug[:, 64::66], 1.0))
            vE(p0v2)

            with contextlib.ExitStack() as esT:
                sbT = lambda n, s, dt: esT.enter_context(nc.sbuf_tensor(n, s, dt))
                psT = lambda n, s, dt=F32: esT.enter_context(nc.psum_tensor(n, s, dt))
                wq_b = sbT("wq_b", [128, KC * D], BF16)
                wkv_b = sbT("wkv_b", [128, KC * 512], BF16)
                xbuf = sbT("xbuf", [128, 2 * D], F32)
                xng = sbT("xng", [128, 2 * D], BF16)
                xT_t = sbT("xT_t", [128, D], BF16)
                means8 = sbT("means8", [128, 2], F32)
                rstds8 = sbT("rstds8", [128, 2], F32)
                lnt = sbT("lnt", [128, 2], F32)
                stats6 = sbT("stats6", [128, 12], F32)
                stats2 = sbT("stats2", [128, 2], F32)
                ct2 = sbT("ct2", [128, 1024], F32)
                st2 = sbT("st2", [128, 1024], F32)
                rt1 = sbT("rt1", [128, 512], F32)
                rt2 = sbT("rt2", [128, 512], F32)
                qro = sbT("qro", [128, D + 512], BF16)
                qsq = sbT("qsq", [128, 512], F32)
                nrm = sbT("nrm", [128, 16], F32)
                wred = sbT("wred", [128, 1], F32)

                ps_tr = psT("ps_tr", [128, 2048], BF16)
                ps_q = psT("ps_q", [128, 1024], F32)
                ps_kv = psT("ps_kv", [128, 3 * 512], F32)
                ps_ktr = psT("ps_ktr", [128, 256], BF16)

                def ldw(gpsimd):
                    sLW.inc(gpsimd.dma_start(wq_b[:, :].rearrange("p (k n) -> p k n", k=KC),
                                        wq_d.rearrange("(k p) n -> p k n", p=128)), 16)
                    sLW.inc(gpsimd.dma_start(wkv_b[:, :].rearrange("p (k n) -> p k n", k=KC),
                                        wkv_d.rearrange("(k p) n -> p k n", p=128)), 16)
                gE(ldw)
                n_ldw = sLW.n

                def emit_tile_tables(kt, own, nf):
                    pb = kt % 2

                    def ld_tab(sync, kt=kt, pb=pb):
                        sV.wait(sync, rope_done.get(kt - 2, 0))
                        sLT[pb].inc(sync.dma_start(ct2[:, pb * 512:(pb + 1) * 512],
                                                   ctab_d[kt * 128:(kt + 1) * 128, :]), 16)
                        sLT[pb].inc(sync.dma_start(st2[:, pb * 512:(pb + 1) * 512],
                                                   stab_d[kt * 128:(kt + 1) * 128, :]), 16)
                    yE(ld_tab)
                    tab_ready[kt] = sLT[pb].n

                ln_done, trev_done, rope_done = {}, {}, {}
                tab_ready = {}
                oth_defer = [None]
                oth_ktr, kev_ev, own_evt = {}, {}, {}
                p123_s = []

                def rec_s(waits, fn):
                    p123_s.append((list(waits), fn))
                    sS.n += 1
                qsq_free = [0]

                def emit_tile_stage2(gi, kt, own):
                    xn = xng[:, gi * D:(gi + 1) * D]
                    nf = 512 if own else 32
                    pb = kt % 2
                    kv4 = kt % 3

                    emit_tile_tables(kt, own, nf)

                    ts0 = (kt % 2) * 1024

                    def t_tr(tensor):
                        sV.wait(tensor, ln_done[kt])
                        sV.wait(tensor, trev_done.get(kt - 2, 0))
                        for k in range(KC):
                            ins = tensor.transpose(ps_tr[:, ts0 + k * 128:ts0 + (k + 1) * 128],
                                                   xn[:, k * 128:(k + 1) * 128], ident_b[:, :])
                        sT.inc(ins)
                    tE(t_tr)
                    n_ttr = sT.n

                    def v_trev(vector):
                        sT.wait(vector, n_ttr)
                        sV.wait_all(vector)
                        sV.inc(vector.tensor_copy(xT_t[:, :], ps_tr[:, ts0:ts0 + 1024]))
                        trev_done[kt] = sV.n
                    vE(v_trev)

                    def t_proj(tensor):
                        sV.wait(tensor, trev_done[kt])
                        sLW.wait(tensor, n_ldw)
                        sV.wait(tensor, rope_done.get(kt - 1 if own else kt - 2, 0))
                        if own:
                            for nn in range(2):
                                for k in range(KC):
                                    ins = tensor.matmul(ps_q[:, nn * 512:(nn + 1) * 512],
                                                        xT_t[:, k * 128:(k + 1) * 128],
                                                        wq_b[:, k * D + nn * 512:k * D + nn * 512 + 512],
                                                        start=(k == 0), stop=(k == KC - 1))
                            sT.inc(ins)
                        for k in range(KC):
                            ins = tensor.matmul(ps_kv[:, kv4 * 512:(kv4 + 1) * 512],
                                                xT_t[:, k * 128:(k + 1) * 128],
                                                wkv_b[:, k * 512:(k + 1) * 512],
                                                start=(k == 0), stop=(k == KC - 1))
                        sT.inc(ins)
                    tE(t_proj)
                    n_proj = sT.n

                    if oth_defer[0] is not None:
                        oth_defer[0]()
                        oth_defer[0] = None

                    if own:
                        m = kt

                        for half in range(2):
                            rec_s([(sT.h, n_proj), (sV.h, qsq_free[0])],
                                  lambda scalar, half=half: scalar.activation(
                                      qsq[:, :], ps_q[:, half * 512:(half + 1) * 512],
                                      AF.Square).then_inc(sS.h, 1))
                            n_ssq = sS.n

                            def v_qn(vector, half=half, n_ssq=n_ssq):
                                sS.wait(vector, n_ssq)
                                sV.wait_all(vector)
                                for s in range(8):
                                    hh = SLOT2HEAD[half * 8 + s]
                                    sV.inc(vector.reduce_sum(nrm[:, hh:hh + 1], qsq[:, s * 64:(s + 1) * 64], axis=AX.X))
                                sV.wait_all(vector)
                            vE(v_qn)
                            qsq_free[0] = sV.n

                        def v_qn2(vector):
                            sV.wait_all(vector)
                            sV.inc(vector.tensor_max(qn_mx[:, :], qn_mx[:, :], nrm[:, 0:16]))
                        vE(v_qn2)
                        qsq_free[0] = sV.n

                        def v_qrope(vector, pb=pb, ntab=None):
                            sLT[pb].wait(vector, tab_ready[kt])
                            sT.wait(vector, own_evt.get(kt - 1, 0))
                            sV.wait_all(vector)
                            x0 = ps_q[:, 0:1024:2]
                            x1 = ps_q[:, 1:1024:2]
                            tc = ct2[:, pb * 512:pb * 512 + 512]
                            ts = st2[:, pb * 512:pb * 512 + 512]
                            sV.inc(vector.tensor_mul(rt1[:, :], x0, tc))
                            sV.inc(vector.tensor_mul(rt2[:, :], x1, ts))
                            sV.wait_all(vector)
                            sV.inc(vector.tensor_sub(qro[:, 0:1024:2], rt1[:, :], rt2[:, :]))
                            sV.wait_all(vector)
                            sV.inc(vector.tensor_mul(rt1[:, :], x0, ts))
                            sV.inc(vector.tensor_mul(rt2[:, :], x1, tc))
                            sV.wait_all(vector)
                            sV.inc(vector.tensor_add(qro[:, 1:1024:2], rt1[:, :], rt2[:, :]))
                        vE(v_qrope)
                        n_qrope = sV.n

                        def emit_own_evac(kt=kt, m=m, ts0=ts0, n_qrope=n_qrope, kb=1024 + (kt % 2) * 256):
                            def t_qtr(tensor):
                                sV.wait(tensor, n_qrope)
                                for k in range(KC):
                                    ins = tensor.transpose(ps_tr[:, ts0 + k * 128:ts0 + (k + 1) * 128],
                                                           qro[:, k * 128:(k + 1) * 128], ident_b[:, :])
                                sT.inc(ins)
                            tE(t_qtr)
                            n_qtr = sT.n

                            def v_qev(vector):
                                sT.wait(vector, n_qtr)
                                for k in range(KC):
                                    sV.inc(vector.tensor_copy(q_pk[k][:, m * 128:(m + 1) * 128],
                                                              ps_tr[:, ts0 + k * 128:ts0 + (k + 1) * 128]))
                                trev_done[kt] = sV.n
                            vE(v_qev)

                            def t_ktr(tensor):
                                sV.wait(tensor, rope_done[kt])
                                sV.wait(tensor, trev_done[kt])
                                for k in range(2):
                                    ins = tensor.transpose(ps_tr[:, ts0 + k * 128:ts0 + (k + 1) * 128],
                                                           qro[:, kb + k * 128:kb + (k + 1) * 128], ident_b[:, :])
                                sT.inc(ins)
                            tE(t_ktr)
                            own_evt[kt] = sT.n
                            n_ktr = sT.n

                            def v_kev(vector):
                                sT.wait(vector, n_ktr)
                                for k in range(2):
                                    sV.inc(vector.tensor_copy(k_pk[k][:, kt * 128:(kt + 1) * 128],
                                                              ps_tr[:, ts0 + k * 128:ts0 + (k + 1) * 128]))
                                trev_done[kt] = sV.n
                            vE(v_kev)

                    kvs = ps_kv[:, kv4 * 512:(kv4 + 1) * 512]

                    rec_s([(sT.h, n_proj), (sV.h, qsq_free[0])],
                          lambda scalar, kvs=kvs: scalar.activation(
                              qsq[:, 0:256], kvs[:, 0:256], AF.Square).then_inc(sS.h, 1))
                    n_sksq = sS.n

                    def v_kn(vector):
                        sS.wait(vector, n_sksq)
                        sV.wait_all(vector)
                        for h in range(HKV):
                            sV.inc(vector.reduce_sum(nrm[:, h:h + 1], qsq[:, h * 64:(h + 1) * 64], axis=AX.X))
                        sV.wait_all(vector)
                        sV.inc(vector.tensor_max(kn_mx[:, 0:HKV], kn_mx[:, 0:HKV], nrm[:, 0:HKV]))
                    vE(v_kn)
                    qsq_free[0] = sV.n

                    obase = (1024 + (kt % 2) * 256) if own else (kt % 2) * 512

                    def v_krope(vector, pb=pb, kv4=kv4, obase=obase):
                        sLT[pb].wait(vector, tab_ready[kt])
                        if not own:
                            sT.wait(vector, oth_ktr.get(kt - 2, 0))
                        sV.wait_all(vector)
                        x0 = AP(ps_kv, kv4 * 512, [[1536, 128], [64, 4], [2, 32]])
                        x1 = AP(ps_kv, kv4 * 512 + 1, [[1536, 128], [64, 4], [2, 32]])
                        tc = AP(ct2, pb * 512, [[1024, 128], [0, 4], [1, 32]])
                        ts = AP(st2, pb * 512, [[1024, 128], [0, 4], [1, 32]])
                        r1 = rt1[:, 0:128].rearrange("p (h j) -> p h j", h=4)
                        r2 = rt2[:, 0:128].rearrange("p (h j) -> p h j", h=4)
                        q0 = AP(qro, obase, [[1536, 128], [64, 4], [2, 32]])
                        q1 = AP(qro, obase + 1, [[1536, 128], [64, 4], [2, 32]])
                        sV.inc(vector.tensor_mul(r1, x0, tc))
                        sV.inc(vector.tensor_mul(r2, x1, ts))
                        sV.wait_all(vector)
                        sV.inc(vector.tensor_sub(q0, r1, r2))
                        sV.wait_all(vector)
                        sV.inc(vector.tensor_mul(r1, x0, ts))
                        sV.inc(vector.tensor_mul(r2, x1, tc))
                        sV.wait_all(vector)
                        sV.inc(vector.tensor_add(q1, r1, r2))
                        sV.wait_all(vector)
                        src = AP(ps_kv, kv4 * 512 + 256, [[1536, 128], [64, 4], [1, 64]])
                        dst = AP(vaug, kt * 66, [[HKV * ST * 66, 128], [ST * 66, 4], [1, 64]])
                        sV.inc(vector.tensor_copy(dst, src))
                        rope_done[kt] = sV.n
                    vE(v_krope)

                    if own:
                        oth_defer[0] = emit_own_evac
                    else:
                        def emit_kev(kt=kt, obase=obase, nrope=None):
                            def t_ktr(tensor, kt=kt, obase=obase):
                                sV.wait(tensor, rope_done[kt])
                                sV.wait(tensor, kev_ev.get(kt - 2, 0))
                                for k in range(2):
                                    ins = tensor.transpose(ps_ktr[:, k * 128:(k + 1) * 128],
                                                           qro[:, obase + k * 128:obase + (k + 1) * 128],
                                                           ident_b[:, :])
                                sT.inc(ins)
                            tE(t_ktr)
                            oth_ktr[kt] = sT.n
                            n_ktr = sT.n

                            def v_kev(vector, kt=kt, n_ktr=n_ktr):
                                sT.wait(vector, n_ktr)
                                for k in range(2):
                                    sV.inc(vector.tensor_copy(k_pk[k][:, kt * 128:(kt + 1) * 128],
                                                              ps_ktr[:, k * 128:(k + 1) * 128]))
                                kev_ev[kt] = sV.n
                            vE(v_kev)
                        oth_defer[0] = emit_kev

                def emit_group(tiles):
                    for gi, (kt, own) in enumerate(tiles):
                        if not own:
                            j = kt - MT

                            def ld(sync, j=j, gi=gi, kt=kt):
                                sV.wait(sync, ln_done.get(kt - 2, 0))
                                sLX[0].inc(sync.dma_start(xbuf[:, gi * D:(gi + 1) * D],
                                                          x_oth[j * 128:(j + 1) * 128, :]), 16)
                            yE(ld)
                    n_gload = sLX[0].n
                    for gi, (kt, own) in enumerate(tiles):
                        src = R[:, kt * D:(kt + 1) * D] if own else xbuf[:, gi * D:(gi + 1) * D]

                        def v_st(vector, gi=gi, src=src, own=own, kt=kt):
                            if not own:
                                sLX[0].wait(vector, n_gload)
                            elif kt == 0:
                                sLR.wait_all(vector)
                            sV.wait_all(vector)
                            sV.inc(vector.bn_stats(stats6[:, 0:6], src[:, 0:512]))
                            sV.inc(vector.bn_stats(stats6[:, 6:12], src[:, 512:1024]))
                            sV.wait_all(vector)
                            sV.inc(vector.bn_aggr(stats2[:, 0:2], stats6[:, 0:12]))
                            sV.wait_all(vector)
                            sV.inc(vector.tensor_copy(means8[:, gi:gi + 1], stats2[:, 0:1]))
                            sV.inc(vector.tensor_scalar_add(lnt[:, gi:gi + 1], stats2[:, 1:2], 1e-5))
                        vE(v_st)
                    nst = sV.n

                    rec_s([(sV.h, nst)],
                          lambda scalar, n=len(tiles): scalar.activation(
                              rstds8[:, 0:n], lnt[:, 0:n], AF.Sqrt).then_inc(sS.h, 1))
                    ns_s = sS.n

                    def v_rstd(vector, n=len(tiles)):
                        sS.wait(vector, ns_s)
                        sV.wait_all(vector)
                        sV.inc(vector.reciprocal(rstds8[:, 0:n], rstds8[:, 0:n]))
                    vE(v_rstd)
                    nrstd = sV.n

                    for gi, (kt, own) in enumerate(tiles):
                        src = R[:, kt * D:(kt + 1) * D] if own else xbuf[:, gi * D:(gi + 1) * D]

                        def v_nm(vector, gi=gi, src=src, kt=kt):
                            sV.wait(vector, nrstd)
                            sV.wait_all(vector)
                            sV.inc(vector.tensor_scalar(xng[:, gi * D:(gi + 1) * D], src,
                                                        means8[:, gi:gi + 1], rstds8[:, gi:gi + 1],
                                                        ALU.subtract, ALU.mult))
                            ln_done[kt] = sV.n
                        vE(v_nm)

                    for gi, (kt, own) in enumerate(tiles):
                        emit_tile_stage2(gi, kt, own)

                tiles_all = [(kt, True) for kt in range(MT)] + [(MT + j, False) for j in range(OT)]
                for i in range(0, len(tiles_all), 2):
                    emit_group(tiles_all[i:i + 2])
                if oth_defer[0] is not None:
                    oth_defer[0]()
                    oth_defer[0] = None

                def s_p123(scalar):
                    for waits, fn in p123_s:
                        for h, v in waits:
                            if v:
                                scalar.wait_ge(h, v)
                        fn(scalar)
                sE(s_p123)

            with contextlib.ExitStack() as esO:
                sbO = lambda n, s, dt: esO.enter_context(nc.sbuf_tensor(n, s, dt))
                otm = sbO("otm", [128, MT * D], BF16)
                wo_b = sbO("wo_b", [128, KC * D], BF16)
                pT = sbO("pT", [128, 2 * QHW], BF16)
                o_sb = sbO("o_sb", [65, 2 * QHW], BF16)
                recb = sbO("recb", [128, QTB], F32)
                brow = sbO("brow", [1, 16], F32)
                oT_t = sbO("oT_t", [128, D], BF16)
                aow = sbO("aow", [128, 1024], F32)
                lnw = sbO("lnw", [128, 2], F32)
                ostat = sbO("ostat", [128, 12], F32)

                def ldwo(gpsimd):
                    for s_ in (sW1[0], sW1[1], sSW[0], sSW[1], sLX[0], sLX[1]):
                        s_.wait_all(gpsimd)
                    sV.wait_all(gpsimd)
                    sS.wait_all(gpsimd)
                    sT.wait_all(gpsimd)
                    sLW.inc(gpsimd.dma_start(wo_b[:, :].rearrange("p (k n) -> p k n", k=KC),
                                        wo_d.rearrange("(k p) n -> p k n", p=128)), 16)
                gE(ldwo)
                n_ldwo = sLW.n

                with contextlib.ExitStack() as esP4:
                    psO = lambda n, s, dt=F32: esP4.enter_context(nc.psum_tensor(n, s, dt))
                    esB = contextlib.ExitStack()
                    ps_bc2 = esB.enter_context(nc.psum_tensor("ps_bc2", [128, 16], F32))
                    ps_nrm = esB.enter_context(nc.psum_tensor("ps_nrm", [128, 256 + 128], F32))

                    def t_bnd0(tensor):
                        sV.wait_all(tensor)
                        sT.inc(tensor.transpose(ps_nrm[0:16, 0:128], qn_mx[:, 0:16], eye128f_b[:, :]))
                        sT.inc(tensor.transpose(ps_nrm[0:4, 128:256], kn_mx[:, 0:4], eye128f_b[:, :]))
                    tE(t_bnd0)
                    nbt0 = sT.n

                    def v_bound1b(vector):
                        sT.wait(vector, nbt0)
                        sV.inc(vector.reduce_max(qn_mx[0:16, 0:1], ps_nrm[0:16, 0:128], axis=AX.X))
                        sV.inc(vector.reduce_max(kn_mx[0:4, 0:1], ps_nrm[0:4, 128:256], axis=AX.X))
                    vE(v_bound1b)
                    nb0b = sV.n

                    def t_bnd1(tensor):
                        sV.wait(tensor, nb0b)
                        sT.inc(tensor.transpose(ps_nrm[0:1, 256:272], qn_mx[0:16, 0:1], eye32f_b[0:16, 0:16]))
                        sT.inc(tensor.transpose(ps_nrm[0:1, 272:276], kn_mx[0:4, 0:1], eye32f_b[0:4, 0:4]))
                    tE(t_bnd1)
                    nbt1 = sT.n

                    def v_bound1c(vector):
                        sT.wait(vector, nbt1)
                        sV.inc(vector.tensor_copy(brow[0:1, :], ps_nrm[0:1, 256:272]))
                        sV.wait_all(vector)
                        src_kn = AP(ps_nrm, 272, [[384, 1], [1, 4], [0, 4]])
                        sV.inc(vector.tensor_mul(brow[0:1, :].rearrange("p (a b) -> p a b", a=4),
                                                 brow[0:1, :].rearrange("p (a b) -> p a b", a=4), src_kn))
                    vE(v_bound1c)
                    nb1 = sV.n

                    def s_bound(scalar):
                        sV.wait(scalar, nb1)
                        sS.inc(scalar.activation(brow[0:1, :], brow[0:1, :], AF.Sqrt))
                    sE(s_bound)
                    nb2 = sS.n

                    def v_bound2(vector):
                        sS.wait(vector, nb2)
                        sV.wait_all(vector)
                        sV.inc(vector.tensor_scalar_mul(brow[0:1, :], brow[0:1, :], -1.02 / 32.0))
                    vE(v_bound2)
                    nb3 = sV.n

                    def t_bound(tensor):
                        sV.wait(tensor, nb3)
                        sT.inc(tensor.matmul(ps_bc2[:, 0:16], onesf_b[0:1, :], brow[0:1, :], start=True, stop=True))
                    tE(t_bound)
                    nb4 = sT.n

                    def v_bound3(vector):
                        sT.wait(vector, nb4)
                        sV.inc(vector.tensor_copy(bias16[:, :], ps_bc2[:, 0:16]))
                    vE(v_bound3)
                    n_bias_v = sV.n
                    esB.close()
                    ps_sA = psO("ps_sA", [128, QHW])
                    ps_sB = psO("ps_sB", [128, QHW])
                    ps_o = psO("ps_o", [128, QHW])
                    ps_otr = esP4.enter_context(nc.psum_tensor("ps_otr", [128, QTB * 66], BF16))
                    ps_warm = psO("ps_warm", [128, 512])

                    NWARM = 6
                    def t_warm(tensor):
                        for j in range(NWARM):
                            tensor.matmul(ps_warm[:, 0:512], ident_b[:, 0:128],
                                          k_pk[0][:, 0:512],
                                          start=(j == 0), stop=False)
                    tE(t_warm)

                    it = [0]
                    prev_bd = [None]
                    otr_ev = {}
                    s_items = []

                    def emit_otr(h, qh, par, base_qt):
                        def t_otr(tensor):
                            sVo.wait(tensor, otr_ev[(h, qh)])
                            sVsc.wait(tensor, sVsc.n)
                            for j in range(QTB):
                                ins = tensor.transpose(ps_otr[:, j * 66:j * 66 + 65],
                                                       o_sb[0:65, par * QHW + j * 128: par * QHW + (j + 1) * 128],
                                                       ident_b[0:65, 0:65])
                            sTo.inc(ins)
                        tE(t_otr)
                        nto = sTo.n

                        def v_osc(vector):
                            sTo.wait(vector, nto)
                            sV.wait_all(vector)
                            rsrc = AP(ps_otr, 64, [[QTB * 66, 128], [66, QTB]])
                            sV.inc(vector.reciprocal(recb[:, 0:QTB], rsrc))
                            sV.wait_all(vector)
                            osrc = AP(ps_otr, 0, [[QTB * 66, 128], [66, QTB], [1, 64]])
                            odst = AP(otm, base_qt * D + h * 64, [[MT * D, 128], [D, QTB], [1, 64]])
                            rb = AP(recb, 0, [[QTB, 128], [1, QTB], [0, 64]])
                            sVsc.inc(vector.tensor_mul(odst, osrc, rb))
                        vE(v_osc)

                    for h in range(H):
                        hkv = h // (H // HKV)
                        if h in PK0:
                            qpi, qoff = PK0.index(h), 0
                        else:
                            qpi, qoff = PK64.index(h), 64
                        koff = 64 * (hkv % 2)
                        kpi = hkv // 2
                        assert koff == qoff
                        for qh in range(NQH):
                            i0 = it[0]
                            nvo0 = sVo.n
                            sTs.n += ST
                            sSe.n += ST
                            sTp.n += ST

                            def t_body(tensor, h=h, qh=qh, i0=i0, qpi=qpi, qoff=qoff,
                                       kpi=kpi, hkv=hkv, nvo0=nvo0, nbv=n_bias_v):
                                def t_s(kt):
                                    i = i0 + kt
                                    ps_s = ps_sA if i % 2 == 0 else ps_sB
                                    if i == 0:
                                        tensor.wait_ge(sV.h, nbv)
                                    if i >= 2:
                                        tensor.wait_ge(sSe.h, i - 1)
                                    for s0_, w in QNS:
                                        ins = tensor.matmul(ps_s[:, s0_:s0_ + w],
                                                            k_pk[kpi][qoff:qoff + 64, kt * 128:(kt + 1) * 128],
                                                            q_pk[qpi][qoff:qoff + 64, qh * QHW + s0_: qh * QHW + s0_ + w],
                                                            start=True, stop=True)
                                    ins.then_inc(sTs.h, 1)
                                    tensor.matmul(ps_warm[:, 0:512], ident_b[:, 0:128],
                                                  k_pk[0][:, 0:512], start=False, stop=False)

                                def t_pv(kt):
                                    i = i0 + kt
                                    buf = i % 2
                                    tensor.wait_ge(sSe.h, i + 1)
                                    if kt == 0:
                                        if nvo0:
                                            tensor.wait_ge(sVo.h, nvo0)
                                    vsl = vaug[:, (hkv * ST + kt) * 66:(hkv * ST + kt) * 66 + 65]
                                    for s0_, w in QNS:
                                        ins = tensor.matmul(ps_o[0:65, s0_:s0_ + w],
                                                            vsl, pT[:, buf * QHW + s0_: buf * QHW + s0_ + w],
                                                            start=(kt == 0), stop=(kt == ST - 1))
                                    ins.then_inc(sTp.h, 1)

                                t_s(0)
                                for kt in range(1, ST):
                                    t_s(kt)
                                    t_pv(kt - 1)
                                t_pv(ST - 1)
                            tE(t_body)

                            s_items.append((h, i0))
                            it[0] += ST

                            par = (h * NQH + qh) % 2

                            def v_oev(vector, h=h, qh=qh, par=par, i=it[0], nto=sTo.n):
                                sTp.wait(vector, i)
                                sTo.wait(vector, nto)
                                sV.wait_all(vector)
                                sVo.inc(vector.tensor_copy(o_sb[0:65, par * QHW:(par + 1) * QHW], ps_o[0:65, 0:QHW]))
                                otr_ev[(h, qh)] = sVo.n
                            vE(v_oev)

                            if prev_bd[0] is not None:
                                emit_otr(*prev_bd[0])
                            prev_bd[0] = (h, qh, par, qh * QTB)
                    emit_otr(*prev_bd[0])

                    def s_all(scalar, nbv=n_bias_v):
                        for h_, i0_ in s_items:
                            if i0_ == 0:
                                scalar.wait_ge(sV.h, nbv)
                            for kt in range(ST):
                                i = i0_ + kt
                                buf = i % 2
                                ps_s = ps_sA if buf == 0 else ps_sB
                                scalar.wait_ge(sTs.h, i + 1)
                                if i >= 2:
                                    scalar.wait_ge(sTp.h, i - 1)
                                scalar.activation(pT[:, buf * QHW:(buf + 1) * QHW], ps_s[:, 0:QHW],
                                                  AF.Exp, scale=1.0 / 32.0,
                                                  bias=bias16[:, h_:h_ + 1]).then_inc(sSe.h, 1)
                    sE(s_all)

                    def t_warm_end(tensor):
                        sT.inc(tensor.matmul(ps_warm[:, 0:512], ident_b[:, 0:128],
                                             k_pk[0][:, 0:512], start=False, stop=True))
                    tE(t_warm_end)
                    n_warm = sT.n

                with contextlib.ExitStack() as esP5:
                    ps_tr5 = esP5.enter_context(nc.psum_tensor("ps_tr5", [128, 1024], BF16))
                    ps_wo = esP5.enter_context(nc.psum_tensor("ps_wo", [128, 2048], F32))
                    p5_ev, p5_ln = {}, {}
                    p5_sink = []
                    p5_tail = [None]

                    for m in range(MT):
                        def t_otr5(tensor, m=m, nvsc=sVsc.n, nse=sSe.n, nw=n_warm):
                            sSe.wait(tensor, nse)
                            sVsc.wait(tensor, nvsc)
                            sT.wait(tensor, nw)
                            sV.wait(tensor, p5_ev.get(m - 1, 0))
                            for k in range(KC):
                                ins = tensor.transpose(ps_tr5[:, k * 128:(k + 1) * 128],
                                                       otm[:, m * D + k * 128: m * D + (k + 1) * 128], ident_b[:, :])
                            sT.inc(ins)
                        tE(t_otr5)
                        n5a = sT.n

                        def v_oev5(vector, m=m):
                            sT.wait(vector, n5a)
                            sV.wait_all(vector)
                            sV.inc(vector.tensor_copy(oT_t[:, :], ps_tr5[:, 0:1024]))
                            p5_ev[m] = sV.n
                        vE(v_oev5)

                        if p5_tail[0] is not None:
                            p5_tail[0]()
                            p5_tail[0] = None

                        ws0 = (m % 2) * 1024

                        def t_wo(tensor, m=m, ws0=ws0):
                            sV.wait(tensor, p5_ev[m])
                            sLW.wait(tensor, n_ldwo)
                            sV.wait(tensor, p5_ln.get(m - 2, 0))
                            for nn in range(2):
                                for k in range(KC):
                                    ins = tensor.matmul(ps_wo[:, ws0 + nn * 512:ws0 + (nn + 1) * 512],
                                                        oT_t[:, k * 128:(k + 1) * 128],
                                                        wo_b[:, k * D + nn * 512: k * D + nn * 512 + 512],
                                                        start=(k == 0), stop=(k == KC - 1))
                            sT.inc(ins)
                        tE(t_wo)
                        n5b = sT.n

                        def emit_p5_tail(m=m, ws0=ws0, n5b=n5b):
                            def v_lnw(vector):
                                sT.wait(vector, n5b)
                                sV.wait_all(vector)
                                sV.inc(vector.bn_stats(ostat[:, 0:6], ps_wo[:, ws0:ws0 + 512]))
                                sV.inc(vector.bn_stats(ostat[:, 6:12], ps_wo[:, ws0 + 512:ws0 + 1024]))
                                sV.wait_all(vector)
                                sV.inc(vector.bn_aggr(lnw[:, 0:2], ostat[:, 0:12]))
                                sV.wait_all(vector)
                                sV.inc(vector.tensor_scalar_add(lnw[:, 1:2], lnw[:, 1:2], 1e-5))
                            vE(v_lnw)
                            n5c = sV.n

                            def s_lnw(scalar, n5c=n5c):
                                scalar.wait_ge(sV.h, n5c)
                                scalar.activation(lnw[:, 1:2], lnw[:, 1:2], AF.Sqrt).then_inc(sS.h, 1)
                            p5_sink.append(s_lnw)
                            sS.n += 1
                            n5d = sS.n

                            def v_res(vector, n5d=n5d):
                                sS.wait(vector, n5d)
                                sV.wait_all(vector)
                                sV.inc(vector.reciprocal(lnw[:, 1:2], lnw[:, 1:2]))
                                sV.wait_all(vector)
                                sV.inc(vector.tensor_scalar(aow[:, :], ps_wo[:, ws0:ws0 + 1024], lnw[:, 0:1], lnw[:, 1:2],
                                                            ALU.subtract, ALU.mult))
                                sV.wait_all(vector)
                                sV.inc(vector.tensor_add(R[:, m * D:(m + 1) * D], R[:, m * D:(m + 1) * D], aow[:, :]))
                                p5_ln[m] = sV.n
                            vE(v_res)
                        p5_tail[0] = emit_p5_tail

                    if p5_tail[0] is not None:
                        p5_tail[0]()
                        p5_tail[0] = None

                    def s_p5(scalar):
                        for b_ in p5_sink:
                            b_(scalar)
                    sE(s_p5)

        if debug_tap == "x1":
            n_x1 = sV.n

            def g_dbg(gpsimd):
                sV.wait(gpsimd, n_x1)
                sSW[0].inc(gpsimd.dma_start(
                    dbg_d[:, :], R[:, 0:min(MT, 4) * 1024]), 16)
            gE(g_dbg)

        n_ldw_all = sLW.n
        n_T_preF, n_S_preF, n_V_preF = sT.n, sS.n, sV.n
        with contextlib.ExitStack() as esF:
            with contextlib.ExitStack() as esF1:
                sbF1 = lambda n, s, dt: esF1.enter_context(nc.sbuf_tensor(n, s, dt))
                xqT = sbF1("xqT", [128, KC * T], BF16)
                xnf = sbF1("xnf", [128, D], F32)
                yq = sbF1("yq", [128, D], F32)
                rq = sbF1("rq", [128, D], F32)
                uq = sbF1("uq", [128, D], F32)
                aq = sbF1("aq", [128, D], BF16)
                xq_b = sbF1("xq_b", [128, D], BF16)
                fst = sbF1("fst", [128, 12], F32)
                fln = sbF1("fln", [128, 4], F32)
                silu_b = sbF1("silu_b", [128, 512], F32)
                gu_b = sbF1("gu_b", [128, 2 * 512], F32)
                tg_b = sbF1("tg_b", [128, 2 * ICB * 512], BF16)
                tu_b = sbF1("tu_b", [128, 2 * ICB * 512], BF16)
                ps_trF = esF1.enter_context(nc.psum_tensor("ps_trF", [128, 1024], BF16))
                ps_g = esF1.enter_context(nc.psum_tensor("ps_g", [128, 2 * 512], F32))
                ps_u = esF1.enter_context(nc.psum_tensor("ps_u", [128, 2 * 512], F32))

                f6 = {}
                p6_sink = []
                for m in range(MT):
                    def v_f6a(vector, m=m):
                        sV.wait_all(vector)
                        sS.wait_all(vector)
                        sV.inc(vector.bn_stats(fst[:, 0:6], R[:, m * D:m * D + 512]))
                        sV.inc(vector.bn_stats(fst[:, 6:12], R[:, m * D + 512:(m + 1) * D]))
                        sV.wait_all(vector)
                        sV.inc(vector.bn_aggr(fln[:, 0:2], fst[:, 0:12]))
                        sV.wait_all(vector)
                        sV.inc(vector.tensor_scalar_add(fln[:, 1:2], fln[:, 1:2], 1e-5))
                    vE(v_f6a)
                    n6a = sV.n

                    def s_f6(scalar, n6a=n6a):
                        scalar.wait_ge(sV.h, n6a)
                        scalar.activation(fln[:, 1:2], fln[:, 1:2], AF.Sqrt).then_inc(sS.h, 1)
                    p6_sink.append(s_f6)
                    sS.n += 1
                    n6b = sS.n

                    def v_f6b(vector, m=m):
                        sS.wait(vector, n6b)
                        sV.wait_all(vector)
                        sV.inc(vector.reciprocal(fln[:, 1:2], fln[:, 1:2]))
                        sV.wait_all(vector)
                        sV.inc(vector.tensor_scalar_sub(xnf[:, :], R[:, m * D:(m + 1) * D], fln[:, 0:1]))
                        sV.wait_all(vector)
                        sV.inc(vector.reduce_max(fln[:, 2:3], xnf[:, :], axis=AX.X, apply_absolute_value=True))
                        sV.wait_all(vector)
                        sV.inc(vector.tensor_mul(fln[:, 2:3], fln[:, 2:3], fln[:, 1:2]))
                        sV.wait_all(vector)
                        sV.inc(vector.tensor_scalar_max(fln[:, 2:3], fln[:, 2:3], 1e-5))
                        sV.wait_all(vector)
                        sV.inc(vector.tensor_scalar_mul(rs_x[:, m:m + 1], fln[:, 2:3], 1.0 / 127.0))
                        sV.wait_all(vector)
                        sV.inc(vector.reciprocal(fln[:, 3:4], rs_x[:, m:m + 1]))
                        sV.wait_all(vector)
                        sV.inc(vector.tensor_mul(fln[:, 3:4], fln[:, 3:4], fln[:, 1:2]))
                        sV.wait_all(vector)
                        sV.inc(vector.tensor_scalar_mul(yq[:, :], xnf[:, :], fln[:, 3:4]))
                        sV.wait_all(vector)
                        sV.inc(vector.tensor_mul(cgu[:, m:m + 1], rs_x[:, m:m + 1], mw2_b[:, 0:1]))
                        sV.inc(vector.tensor_mul(cgu[:, MT + m:MT + m + 1], rs_x[:, m:m + 1], mw2_b[:, 1:2]))
                    vE(v_f6b)
                    items6 = [{"src": yq[:, hh * 512:(hh + 1) * 512],
                               "dst": xq_b[:, hh * 512:(hh + 1) * 512],
                               "f": rq[:, hh * 512:(hh + 1) * 512],
                               "a": aq[:, hh * 512:(hh + 1) * 512],
                               "t": uq[:, hh * 512:(hh + 1) * 512]} for hh in range(2)]
                    emit_round_ladder(blk, sV, sS, items6,
                                      lambda it: it["f"], lambda it: it["a"], lambda it: it["t"],
                                      s_sink=p6_sink)

                    def v_f6sync(vector):
                        sV.wait_all(vector)
                    vE(v_f6sync)
                    n6c = sV.n

                    def t_f6(tensor, m=m):
                        sV.wait(tensor, n6c)
                        sV.wait(tensor, f6.get(m - 1, 0))
                        for k in range(KC):
                            ins = tensor.transpose(ps_trF[:, k * 128:(k + 1) * 128],
                                                   xq_b[:, k * 128:(k + 1) * 128], ident_b[:, :])
                        sT.inc(ins)
                    tE(t_f6)
                    n6d = sT.n

                    def v_f6c(vector, m=m):
                        sT.wait(vector, n6d)
                        sV.wait_all(vector)
                        dst = xqT[:, :].rearrange("p (k t) -> p k t", k=KC)[:, :, m * 128:(m + 1) * 128]
                        src = ps_trF[:, 0:1024].rearrange("p (k j) -> p k j", k=KC)
                        sV.inc(vector.tensor_copy(dst, src))
                        f6[m] = sV.n
                    vE(v_f6c)
                def s_p6(scalar):
                    for b_ in p6_sink:
                        b_(scalar)
                sE(s_p6)
                n_xq = f6[MT - 1]

                f7t, f7v = {}, {}
                p7_sink = []
                for c in range(ICB):
                    cb = c % 2

                    def ld_t(sync, c=c, cb=cb):
                        sSW[0].wait(sync, n_stash[0])
                        sSW[1].wait(sync, n_stash[1])
                        sLW.wait(sync, n_ldw_all)
                        sT.wait(sync, n_T_preF)
                        sS.wait(sync, n_S_preF)
                        sV.wait(sync, n_V_preF)
                        if c >= 2:
                            sT.wait(sync, f7t.get((c - 2, MT - 1), 0))
                        for k in range(KC):
                            sLT[cb].inc(sync.dma_start(
                                tg_b[:, cb * ICB * 512 + k * 512:cb * ICB * 512 + (k + 1) * 512],
                                tg_s[c * 512:(c + 1) * 512, k * 128:(k + 1) * 128], transpose=True), 16)
                            sLT[cb].inc(sync.dma_start(
                                tu_b[:, cb * ICB * 512 + k * 512:cb * ICB * 512 + (k + 1) * 512],
                                tu_s[c * 512:(c + 1) * 512, k * 128:(k + 1) * 128], transpose=True), 16)
                    yE(ld_t)
                    nlt_c = sLT[cb].n

                    for m in range(MT):
                        gb = m % 2

                        def t_f7(tensor, c=c, m=m, cb=cb, gb=gb, nlt_c=nlt_c):
                            sV.wait(tensor, n_xq)
                            sLT[cb].wait(tensor, nlt_c)
                            if m >= 2:
                                sV.wait(tensor, f7v.get((c, m - 2), 0))
                            elif c > 0:
                                sV.wait(tensor, f7v.get((c - 1, MT - 2 + m), 0))
                            for k in range(KC):
                                ins = tensor.matmul(ps_g[:, gb * 512:(gb + 1) * 512],
                                                    xqT[:, k * T + m * 128:k * T + (m + 1) * 128],
                                                    tg_b[:, cb * ICB * 512 + k * 512:cb * ICB * 512 + (k + 1) * 512],
                                                    start=(k == 0), stop=(k == KC - 1))
                            sT.inc(ins)
                            for k in range(KC):
                                ins = tensor.matmul(ps_u[:, gb * 512:(gb + 1) * 512],
                                                    xqT[:, k * T + m * 128:k * T + (m + 1) * 128],
                                                    tu_b[:, cb * ICB * 512 + k * 512:cb * ICB * 512 + (k + 1) * 512],
                                                    start=(k == 0), stop=(k == KC - 1))
                            sT.inc(ins)
                            f7t[(c, m)] = sT.n
                        tE(t_f7)

                        def s_f7(scalar, c=c, m=m, gb=gb,
                                 nft=f7t[(c, m)],
                                 prev=f7v.get((c, m - 1) if m >= 1 else (c - 1, MT - 1), 0)):
                            scalar.wait_ge(sT.h, nft)
                            if prev:
                                scalar.wait_ge(sV.h, prev)
                            scalar.activation(silu_b[:, :], ps_g[:, gb * 512:(gb + 1) * 512],
                                              AF.Silu, scale=cgu[:, m:m + 1]).then_inc(sS.h, 1)
                        p7_sink.append(s_f7)
                        sS.n += 1
                        n7s = sS.n

                        def v_f7(vector, c=c, m=m, gb=gb, n7s=n7s):
                            sS.wait(vector, n7s)
                            i = c * MT + m
                            if i >= 2:
                                sGU[gb].wait(vector, 16 * ((i - 2) // 2 + 1))
                            sV.wait_all(vector)
                            sV.inc(vector.scalar_tensor_tensor(gu_b[:, gb * 512:(gb + 1) * 512],
                                                               ps_u[:, gb * 512:(gb + 1) * 512],
                                                               cgu[:, MT + m:MT + m + 1], silu_b[:, :],
                                                               ALU.mult, ALU.mult))
                            f7v[(c, m)] = sV.n
                        vE(v_f7)

                        def g_f7(gpsimd, c=c, m=m, gb=gb):
                            sV.wait(gpsimd, f7v[(c, m)])
                            sGU[gb].inc(gpsimd.dma_start(gu_s[m * 128:(m + 1) * 128, c * 512:(c + 1) * 512],
                                                         gu_b[:, gb * 512:(gb + 1) * 512]), 16)
                        gE(g_f7)
                def s_p7(scalar):
                    for b_ in p7_sink:
                        b_(scalar)
                sE(s_p7)

                n_gu = (sGU[0].n, sGU[1].n)
                n_lt7 = (sLT[0].n, sLT[1].n)
                n_t7 = sT.n
                n_s7 = sS.n
                n_v7 = sV.n

            with contextlib.ExitStack() as esF2:
                sb8 = lambda n, s, dt: esF2.enter_context(nc.sbuf_tensor(n, s, dt))
                td_b = sb8("td_b", [128, IC * 1024], BF16)
                gur = sb8("gur", [128, 2 * INNER], F32)
                y8 = sb8("y8", [128, INNER], F32)
                a8 = sb8("a8", [128, INNER], BF16)
                guq = sb8("guq", [128, INNER], BF16)
                guqT = sb8("guqT", [128, INNER], BF16)
                f8s = sb8("f8s", [128, 8], F32)
                ost8 = sb8("ost8", [128, 12], F32)
                outb = sb8("outb", [128, D], F32)
                ps_tr8 = esF2.enter_context(nc.psum_tensor("ps_tr8", [128, 1024], BF16))
                ps_d = esF2.enter_context(nc.psum_tensor("ps_d", [128, 1024], F32))
                ps_r8 = esF2.enter_context(nc.psum_tensor("ps_r8", [128, 2048], F32))

                def ld_td(sync):
                    sSW[0].wait(sync, n_stash[0])
                    sSW[1].wait(sync, n_stash[1])
                    sLT[0].wait(sync, n_lt7[0])
                    sLT[1].wait(sync, n_lt7[1])
                    sT.wait(sync, n_t7)
                    sS.wait(sync, n_s7)
                    sV.wait(sync, n_v7)
                    sGU[0].wait(sync, n_gu[0])
                    sGU[1].wait(sync, n_gu[1])
                    for ic in range(IC):
                        sTD.inc(sync.dma_start(td_b[:, ic * 1024:(ic + 1) * 1024],
                                               td_s[0:1024, ic * 128:(ic + 1) * 128], transpose=True), 16)
                yE(ld_td)
                n_td = sTD.n

                f8, f8y, f8tr_ev, f8dn = {}, {}, {}, {}
                pool_end = {}
                p8_sink = []
                p8_tail = [None]
                for m in range(MT):
                    gb = m % 2

                    g8 = (m % 2) * INNER

                    def ld_gu(sync, m=m, g8=g8):
                        sGU[0].wait(sync, n_gu[0])
                        sGU[1].wait(sync, n_gu[1])
                        if m >= 2:
                            sV.wait(sync, f8.get(m - 2, 0))
                        sLT[m % 2].inc(sync.dma_start(gur[:, g8:g8 + INNER], gu_s[m * 128:(m + 1) * 128, :]), 16)
                    yE(ld_gu)
                    n_lg = sLT[m % 2].n

                    def v_f8a(vector, m=m, n_lg=n_lg, g8=g8):
                        sLT[m % 2].wait(vector, n_lg)
                        sS.wait_all(vector)
                        sV.wait_all(vector)
                        fs = (m % 2) * 4
                        sV.inc(vector.reduce_max(f8s[:, fs:fs + 1], gur[:, g8:g8 + INNER], axis=AX.X, apply_absolute_value=True))
                        sV.wait_all(vector)
                        sV.inc(vector.tensor_scalar_max(f8s[:, fs:fs + 1], f8s[:, fs:fs + 1], 1e-5))
                        sV.wait_all(vector)
                        sV.inc(vector.tensor_scalar_mul(f8s[:, fs + 1:fs + 2], f8s[:, fs:fs + 1], 1.0 / 127.0))
                        sV.wait_all(vector)
                        sV.inc(vector.reciprocal(f8s[:, fs + 2:fs + 3], f8s[:, fs + 1:fs + 2]))
                        sV.wait_all(vector)
                        sV.inc(vector.tensor_mul(f8s[:, fs + 3:fs + 4], f8s[:, fs + 1:fs + 2], mw2_b[:, 2:3]))
                        sV.wait_all(vector)
                        sV.inc(vector.tensor_scalar_mul(y8[:, :], gur[:, g8:g8 + INNER], f8s[:, fs + 2:fs + 3]))
                        sV.wait_all(vector)
                        f8y[m] = sV.n
                    vE(v_f8a)
                    HF = INNER // 2
                    items8 = [{"src": y8[:, hh * HF:(hh + 1) * HF],
                               "dst": guq[:, hh * HF:(hh + 1) * HF],
                               "f": gur[:, g8 + hh * HF:g8 + (hh + 1) * HF],
                               "a": a8[:, hh * HF:(hh + 1) * HF],
                               "t": ps_r8[:, :]} for hh in range(2)]
                    emit_round_ladder(blk, sV, sS, items8,
                                      lambda it: it["f"], lambda it: it["a"], lambda it: it["t"],
                                      s_sink=p8_sink)
                    pool_end[m] = sP.n

                    if p8_tail[0] is not None:
                        p8_tail[0]()
                        p8_tail[0] = None

                    def v_f8sync(vector, m=m):
                        sV.wait_all(vector)
                    vE(v_f8sync)
                    f8[m] = sV.n

                    for b in range(4):
                        def t_b(tensor, m=m, b=b):
                            sV.wait(tensor, f8[m])
                            sP.wait(tensor, pool_end[m])
                            prev = f8tr_ev.get((m, b - 1), 0) if b > 0 else f8tr_ev.get((m - 1, 3), 0)
                            sV.wait(tensor, prev)
                            for k in range(8):
                                ins = tensor.transpose(ps_tr8[:, k * 128:(k + 1) * 128],
                                                       guq[:, b * 1024 + k * 128: b * 1024 + (k + 1) * 128],
                                                       ident_b[:, :])
                            sT.inc(ins)
                        tE(t_b)
                        n8t = sT.n

                        def v_b(vector, m=m, b=b, n8t=n8t):
                            sT.wait(vector, n8t)
                            sV.wait_all(vector)
                            sV.inc(vector.tensor_copy(guqT[:, b * 1024:(b + 1) * 1024], ps_tr8[:, 0:1024]))
                            f8tr_ev[(m, b)] = sV.n
                        vE(v_b)

                    def t_down(tensor, m=m):
                        sV.wait(tensor, f8tr_ev[(m, 3)])
                        sTD.wait(tensor, n_td)
                        sV.wait(tensor, f8dn.get(m - 1, 0))
                        for nn in range(2):
                            for ic in range(IC):
                                ins = tensor.matmul(ps_d[:, nn * 512:(nn + 1) * 512],
                                                    guqT[:, ic * 128:(ic + 1) * 128],
                                                    td_b[:, ic * 1024 + nn * 512: ic * 1024 + nn * 512 + 512],
                                                    start=(ic == 0), stop=(ic == IC - 1))
                        sT.inc(ins)
                    tE(t_down)
                    n8d = sT.n

                    def emit_tail(m=m, n8d=n8d, fs=(m % 2) * 4):
                        def v_f8b(vector):
                            sT.wait(vector, n8d)
                            sV.wait_all(vector)
                            sV.inc(vector.tensor_scalar_mul(y8[:, 0:1024], ps_d[:, 0:1024], f8s[:, fs + 3:fs + 4]))
                            sV.wait_all(vector)
                            sV.inc(vector.bn_stats(ost8[:, 0:6], y8[:, 0:512]))
                            sV.inc(vector.bn_stats(ost8[:, 6:12], y8[:, 512:1024]))
                            sV.wait_all(vector)
                            sV.inc(vector.bn_aggr(f8s[:, fs:fs + 2], ost8[:, 0:12]))
                            sV.wait_all(vector)
                            sV.inc(vector.tensor_scalar_add(f8s[:, fs + 1:fs + 2], f8s[:, fs + 1:fs + 2], 1e-5))
                        vE(v_f8b)
                        n8e = sV.n

                        def s_f8(scalar, n8e=n8e):
                            scalar.wait_ge(sV.h, n8e)
                            scalar.activation(f8s[:, fs + 1:fs + 2], f8s[:, fs + 1:fs + 2],
                                              AF.Sqrt).then_inc(sS.h, 1)
                        p8_sink.append(s_f8)
                        sS.n += 1
                        n8f = sS.n

                        def v_f8c(vector, n8f=n8f):
                            sS.wait(vector, n8f)
                            sV.wait_all(vector)
                            sV.inc(vector.reciprocal(f8s[:, fs + 1:fs + 2], f8s[:, fs + 1:fs + 2]))
                            if m >= 1:
                                sOut[0].wait(vector, 16 * m)
                            sV.wait_all(vector)
                            sV.inc(vector.tensor_scalar(y8[:, 0:1024], y8[:, 0:1024], f8s[:, fs:fs + 1],
                                                        f8s[:, fs + 1:fs + 2], ALU.subtract, ALU.mult))
                            sV.wait_all(vector)
                            sV.inc(vector.tensor_add(outb[:, :], R[:, m * D:(m + 1) * D], y8[:, 0:1024]))
                            f8dn[m] = sV.n
                        vE(v_f8c)

                        def g_out(gpsimd):
                            sV.wait(gpsimd, f8dn[m])
                            sOut[0].inc(gpsimd.dma_start(out_d[m * 128:(m + 1) * 128, :], outb[:, :]), 16)
                        gE(g_out)
                    p8_tail[0] = emit_tail

                if p8_tail[0] is not None:
                    p8_tail[0]()
                    p8_tail[0] = None

                def s_p8(scalar):
                    for b_ in p8_sink:
                        b_(scalar)
                sE(s_p8)

                def g_fin(gpsimd):
                    sOut[0].wait_all(gpsimd)
                    sOut[1].wait_all(gpsimd)
                gE(g_fin)

    return nc


def kernel(L0_attn_x, L0_attn_Wq, L0_attn_Wk, L0_attn_Wv, L0_attn_Wo,
           L0_ffn_w_gate, L0_ffn_w_up, L0_ffn_w_down):
    x = np.ascontiguousarray(np.asarray(L0_attn_x, dtype=np.float32))
    B, S, D = x.shape
    T = S // 2
    wq = permute_wq(np.asarray(L0_attn_Wq, dtype=np.float32))
    wk = np.asarray(L0_attn_Wk, dtype=np.float32)
    wv = np.asarray(L0_attn_Wv, dtype=np.float32)
    wkv = np.ascontiguousarray(np.concatenate([wk, wv], axis=1))
    wo = np.ascontiguousarray(np.asarray(L0_attn_Wo, dtype=np.float32))
    wg = np.ascontiguousarray(np.asarray(L0_ffn_w_gate, dtype=np.float32))
    wu = np.ascontiguousarray(np.asarray(L0_ffn_w_up, dtype=np.float32))
    wd = np.ascontiguousarray(np.asarray(L0_ffn_w_down, dtype=np.float32))

    nc = build_program(T=T, S=S)
    in_maps = []
    for c in range(2 * B):
        b, hh = c // 2, c % 2
        own = np.ascontiguousarray(x[b, hh * T:(hh + 1) * T])
        oth = np.ascontiguousarray(x[b, (1 - hh) * T:(2 - hh) * T])
        ctab, stab = core_tables(hh * T, (1 - hh) * T, T, s_max=S)
        in_maps.append({"x_own": own, "x_oth": oth, "wq": wq, "wkv": wkv,
                        "wo": wo, "wg": wg, "wu": wu, "wd": wd,
                        "ctab": ctab, "stab": stab})
    res = run_bass_kernel_spmd(nc, in_maps, core_ids=list(range(len(in_maps))))
    out = np.empty((B, S, D), dtype=np.float32)
    for c in range(2 * B):
        b, hh = c // 2, c % 2
        out[b, hh * T:(hh + 1) * T] = res.results[c]["out"]
    return out


# version 29
# speedup vs baseline: 1.0294x; 1.0021x over previous
import contextlib
import math
import os

import numpy as np
import ml_dtypes

import concourse.bass as bass
import concourse.mybir as mybir
from concourse.bass_utils import run_bass_kernel_spmd

F32 = mybir.dt.float32
BF16 = mybir.dt.bfloat16
F16 = mybir.dt.float16
I32 = mybir.dt.int32
AF = mybir.ActivationFunctionType
ALU = mybir.AluOpType
AX = mybir.AxisListType
AP = bass.AP

TWO_PI = 2.0 * math.pi
TWO_PI_HI = float(np.float32(TWO_PI))
TWO_PI_LO = float(np.float32(TWO_PI - TWO_PI_HI))
INV_2PI = float(np.float32(1.0 / TWO_PI))
LN1E4_OVER_512 = math.log(10000.0) / 512.0
CE_HI = float(np.float32(LN1E4_OVER_512))
CE_LO = float(np.float32(LN1E4_OVER_512 - CE_HI))
HALF_PI = float(np.float32(math.pi / 2.0))
HALF_PI_LO = float(np.float32(math.pi / 2.0 - HALF_PI))
PI_CLIP = float(np.float32(math.pi))


PK0 = [0, 1, 2, 3, 8, 9, 10, 11]
PK64 = [4, 5, 6, 7, 12, 13, 14, 15]
SLOT2HEAD = []
for _k in range(8):
    SLOT2HEAD += [PK0[_k], PK64[_k]]


def permute_wq(wq):
    cols = []
    for k in range(8):
        cols.append(wq[:, PK0[k] * 64:(PK0[k] + 1) * 64])
        cols.append(wq[:, PK64[k] * 64:(PK64[k] + 1) * 64])
    return np.ascontiguousarray(np.concatenate(cols, axis=1))


def q_freq_perm():
    f = np.zeros((1, 512), dtype=np.float32)
    for s in range(16):
        h = SLOT2HEAD[s]
        f[0, s * 32:(s + 1) * 32] = h * 32 + np.arange(32)
    return f



_TABLES = {}


def rope_tables_full(s_max=4096):
    key = ("cs", s_max)
    if key not in _TABLES:
        import subprocess
        import sys
        import tempfile
        child = (
            "import sys\n"
            "import numpy as np\n"
            "import jax.numpy as jnp\n"
            "inv = 1.0 / 10000.0 ** (jnp.arange(0, 1024, 2, dtype=jnp.float32) / 1024)\n"
            "ang = jnp.arange(%d, dtype=jnp.float32)[:, None] * inv[None, :]\n"
            "np.save(sys.argv[1], np.asarray(jnp.cos(ang)))\n"
            "np.save(sys.argv[2], np.asarray(jnp.sin(ang)))\n"
        ) % s_max
        env = dict(os.environ)
        env.pop("TRN_TERMINAL_POOL_IPS", None)
        env["PYTHONPATH"] = ""
        env["JAX_PLATFORMS"] = "cpu"
        with tempfile.TemporaryDirectory() as td:
            fc = os.path.join(td, "c.npy")
            fs = os.path.join(td, "s.npy")
            subprocess.run([sys.executable, "-c", child, fc, fs], env=env,
                           check=True, capture_output=True, timeout=900)
            c = np.load(fc)
            s = np.load(fs)
        perm = q_freq_perm().ravel().astype(np.int64)
        _TABLES[key] = (np.ascontiguousarray(c[:, perm]),
                        np.ascontiguousarray(s[:, perm]))
    return _TABLES[key]


def core_tables(own0, oth0, T, s_max=4096):
    c, s = rope_tables_full(s_max)
    idx = np.concatenate([np.arange(own0, own0 + T), np.arange(oth0, oth0 + T)])
    return np.ascontiguousarray(c[idx]), np.ascontiguousarray(s[idx])


ROUND_BITS = [128.0, 64.0, 32.0, 16.0, 8.0, 4.0, 2.0, 1.0]


def emit_round_ladder(blk, sV, sS, items, fbuf_of, abuf_of, tbuf_of,
                      s_sink=None, sP=None, pool_waits=(), s_pre_waits=()):
    nlev = len(ROUND_BITS)
    ni = len(items)
    s0, v0 = sS.n, sV.n
    ready = v0
    sc = lambda k, idx: s0 + k * ni + idx + 1
    vc = lambda k, idx: v0 + k * ni + idx + 1
    n_s = (nlev + 1) * ni
    n_v = nlev * ni + (0 if sP else 2 * ni)
    sS.n += n_s
    sV.n += n_v
    p0 = sP.n if sP else 0
    if sP:
        sP.n += 4 * ni

    def s_body(scalar):
        for h, v in s_pre_waits:
            if v:
                scalar.wait_ge(h, v)
        D = 128.0
        for k, b in enumerate(ROUND_BITS):
            c = b - D
            for idx, it in enumerate(items):
                f_in = it["src"] if k == 0 else fbuf_of(it)
                if k == 0:
                    scalar.wait_ge(sV.h, ready) if ready else None
                else:
                    scalar.wait_ge(sV.h, vc(k - 1, idx))
                scalar.activation(abuf_of(it), f_in, AF.Sign, bias=-c).then_inc(sS.h, 1)
            D -= b / 2.0
        for idx, it in enumerate(items):
            scalar.wait_ge(sV.h, vc(nlev - 1, idx))
            scalar.activation(abuf_of(it), fbuf_of(it), AF.Sign).then_inc(sS.h, 1)
    if s_sink is not None:
        s_sink.append(s_body)
    else:
        blk.scalar(s_body)

    def v_body(vector):
        if v0:
            vector.wait_ge(sV.h, v0)
        for k, b in enumerate(ROUND_BITS):
            for idx, it in enumerate(items):
                f_in = it["src"] if k == 0 else fbuf_of(it)
                vector.wait_ge(sS.h, sc(k, idx))
                vector.scalar_tensor_tensor(fbuf_of(it), abuf_of(it), -b / 2.0, f_in,
                                            ALU.mult, ALU.add).then_inc(sV.h, 1)
        if sP is None:
            for idx, it in enumerate(items):
                vector.wait_ge(sS.h, s0 + nlev * ni + idx + 1)
                vector.wait_ge(sV.h, v0 + nlev * ni + 2 * idx)
                vector.scalar_tensor_tensor(tbuf_of(it), abuf_of(it), 0.5, it["src"],
                                            ALU.mult, ALU.add).then_inc(sV.h, 1)
                vector.wait_ge(sV.h, v0 + nlev * ni + 2 * idx + 1)
                vector.tensor_sub(it["dst"], tbuf_of(it), fbuf_of(it)).then_inc(sV.h, 1)
    blk.vector(v_body)

    if sP is not None:
        def p_body(gpsimd):
            for h, v in pool_waits:
                if v:
                    gpsimd.wait_ge(h, v)
            gpsimd.wait_ge(sP.h, p0)
            for idx, it in enumerate(items):
                gpsimd.wait_ge(sV.h, vc(nlev - 1, idx))
                gpsimd.wait_ge(sS.h, sc(nlev, idx))
                gpsimd.wait_ge(sP.h, p0 + 4 * idx)
                gpsimd.tensor_sub(tbuf_of(it), it["src"], fbuf_of(it)).then_inc(sP.h, 1)
                gpsimd.tensor_scalar(abuf_of(it), abuf_of(it), 0.5, None,
                                     ALU.mult).then_inc(sP.h, 1)
                gpsimd.wait_ge(sP.h, p0 + 4 * idx + 2)
                gpsimd.tensor_add(tbuf_of(it), tbuf_of(it), abuf_of(it)).then_inc(sP.h, 1)
                gpsimd.wait_ge(sP.h, p0 + 4 * idx + 3)
                gpsimd.tensor_scalar(it["dst"], tbuf_of(it), -128.0, 127.0,
                                     ALU.max, ALU.min).then_inc(sP.h, 1)
        blk.gpsimd(p_body)


class Sem:
    def __init__(self, nc, es, name):
        self.h = es.enter_context(nc.semaphore(name))
        self.n = 0

    def inc(self, instr, k=1):
        instr.then_inc(self.h, k)
        self.n += k
        return instr

    def wait_all(self, eng):
        if self.n:
            eng.wait_ge(self.h, self.n)

    def wait(self, eng, v):
        if v > 0:
            eng.wait_ge(self.h, v)


def build_program(T=2048, S=4096, qhw=None, debug_tap=None):
    D, H, HKV, INNER = 1024, 16, 4, 4096
    MT, ST = T // 128, S // 128
    OT = ST - MT
    HD = D // H
    DKV = HD * HKV
    KC = D // 128
    ICB = INNER // 512
    IC = INNER // 128
    WT = INNER // 128
    QHW = qhw or (1024 if T >= 2048 else T)
    NQH = T // QHW
    QNS = [(s0, min(512, QHW - s0)) for s0 in range(0, QHW, 512)]
    QTB = QHW // 128

    nc = bass.Bass("TRN2", target_bir_lowering=False)
    dram = lambda n, s, dt, k: nc.dram_tensor(n, s, dt, kind=k).ap()
    x_own = dram("x_own", [T, D], F32, "ExternalInput")
    x_oth = dram("x_oth", [S - T, D], F32, "ExternalInput")
    wq_d = dram("wq", [D, D], F32, "ExternalInput")
    wkv_d = dram("wkv", [D, 2 * DKV], F32, "ExternalInput")
    wo_d = dram("wo", [D, D], F32, "ExternalInput")
    wg_d = dram("wg", [INNER, D], F32, "ExternalInput")
    wu_d = dram("wu", [INNER, D], F32, "ExternalInput")
    wd_d = dram("wd", [D, INNER], F32, "ExternalInput")
    out_d = dram("out", [T, D], F32, "ExternalOutput")
    ctab_d = dram("ctab", [S, 512], F32, "ExternalInput")
    stab_d = dram("stab", [S, 512], F32, "ExternalInput")
    tg_s = dram("tg_s", [INNER, D], BF16, "Internal")
    tu_s = dram("tu_s", [INNER, D], BF16, "Internal")
    td_s = dram("td_s", [D, INNER], BF16, "Internal")
    gu_s = dram("gu_s", [T, INNER], F32, "Internal")
    ident_d = nc.inline_tensor(np.eye(128, dtype=ml_dtypes.bfloat16), name="ident128").ap()
    onesf_d = nc.inline_tensor(np.ones((1, 128), dtype=np.float32), name="ones1f").ap()
    onescol_d = nc.inline_tensor(np.ones((128, 1), dtype=np.float32), name="onescolf").ap()
    eye32f_d = nc.inline_tensor(np.eye(32, dtype=np.float32), name="eye32f").ap()
    eye128f_d = nc.inline_tensor(np.eye(128, dtype=np.float32), name="eye128f").ap()
    dbg_d = dram("dbg", [128, min(T // 128, 4) * 1024], F32, "ExternalOutput") if debug_tap else None

    es = contextlib.ExitStack()
    with es:
        sb = lambda n, s, dt: es.enter_context(nc.sbuf_tensor(n, s, dt))

        R = sb("R", [128, MT * D], F32)
        ident_b = sb("ident_b", [128, 128], BF16)
        onesf_b = sb("onesf_b", [1, 128], F32)
        onescol_b = sb("onescol_b", [128, 1], F32)
        eye32f_b = sb("eye32f_b", [32, 32], F32)
        eye128f_b = sb("eye128f_b", [128, 128], F32)
        qn_mx = sb("qn_mx", [128, H], F32)
        kn_mx = sb("kn_mx", [128, HKV], F32)
        bias16 = sb("bias16", [128, H], F32)
        rs_x = sb("rs_x", [128, MT], F32)
        cgu = sb("cgu", [128, 2 * MT], F32)
        wmean = sb("wmean", [128, 4], F32)
        wsc = sb("wsc", [128, 4], F32)
        mw2_row = sb("mw2_row", [1, 4], F32)
        mw2_b = sb("mw2_b", [128, 4], F32)
        wrow = sb("wrow", [1, 4], F32)
        chalf = sb("chalf", [128, 2], F32)

        SN = ("sLB", "sLR", "sLX0", "sLX1", "sLW", "sLT0", "sLT1",
              "sW1a", "sW1b", "sSWa", "sSWb", "sGUa", "sGUb",
              "sOuta", "sOutb", "sTD",
              "sT", "sV", "sS", "sG", "sTs", "sSe", "sTp", "sVo", "sTo", "sVsc", "sP")
        sm = {n: Sem(nc, es, n) for n in SN}
        sLB, sLR, sLW = sm["sLB"], sm["sLR"], sm["sLW"]
        sLX = [sm["sLX0"], sm["sLX1"]]
        sLT = [sm["sLT0"], sm["sLT1"]]
        sW1 = [sm["sW1a"], sm["sW1b"]]
        sSW = [sm["sSWa"], sm["sSWb"]]
        sGU = [sm["sGUa"], sm["sGUb"]]
        sOut = [sm["sOuta"], sm["sOutb"]]
        sTD = sm["sTD"]
        sT, sV, sS, sG = sm["sT"], sm["sV"], sm["sS"], sm["sG"]
        sTs, sSe, sTp = sm["sTs"], sm["sSe"], sm["sTp"]
        sVo, sTo, sVsc = sm["sVo"], sm["sTo"], sm["sVsc"]
        sP = sm["sP"]

        blk = es.enter_context(nc.Block())
        tE, vE, sE, gE, yE = blk.tensor, blk.vector, blk.scalar, blk.gpsimd, blk.sync

        def p0(sync):
            sLB.inc(sync.dma_start(ident_b[:, :], ident_d[:, :]), 16)
            sLB.inc(sync.dma_start(onesf_b[:, :], onesf_d[:, :]), 16)
            sLB.inc(sync.dma_start(onescol_b[:, :], onescol_d[:, :]), 16)
            sLB.inc(sync.dma_start(eye32f_b[:, :], eye32f_d[:, :]), 16)
            sLB.inc(sync.dma_start(eye128f_b[:, :], eye128f_d[:, :]), 16)
            sLR.inc(sync.dma_start(R[:, :].rearrange("p (m d) -> p m d", d=D),
                                   x_own.rearrange("(m p) d -> p m d", p=128)), 16)
        yE(p0)
        n_sLB = sLB.n

        def p0v(vector):
            sV.inc(vector.memset(qn_mx[:, :], 0.0))
            sV.inc(vector.memset(kn_mx[:, :], 0.0))
            sV.inc(vector.memset(wmean[:, :], 0.0))
            sV.inc(vector.memset(chalf[:, 0:1], -0.5))
            sV.inc(vector.memset(chalf[:, 1:2], 0.5))
        vE(p0v)


        with contextlib.ExitStack() as esW:
            sbW = lambda n, s, dt: esW.enter_context(nc.sbuf_tensor(n, s, dt))
            wbig = sbW("wbig", [128, 2 * 4096], F32)
            sgA = sbW("sgA", [128, 4096], BF16)
            sgB = sbW("sgB", [128, 4096], BF16)
            t2big = sbW("t2big", [128, 2 * 4096], BF16)
            wred = sbW("wredW", [128, 1], F32)
            ps_w = esW.enter_context(nc.psum_tensor("ps_w", [128, 16], F32))

            wd_list = [(wg_d, tg_s, 0), (wu_d, tu_s, 1), (wd_d, td_s, 2)]

            def tern_chunk_aps(wi, w, stash, ci, buf):
                if wi < 2:
                    ws = w[ci * 512:(ci + 1) * 512, :].rearrange("(c p) d -> p c d", p=128)
                    ss = stash[ci * 512:(ci + 1) * 512, :].rearrange("(c p) d -> p c d", p=128)
                    bb = buf.rearrange("p (c d) -> p c d", d=1024)
                else:
                    ws = w[ci * 128:(ci + 1) * 128, :]
                    ss = stash[ci * 128:(ci + 1) * 128, :]
                    bb = buf
                return ws, ss, bb

            units = [(wi, w, stash, ci) for (w, stash, wi) in wd_list for ci in range(8)]
            tvd = {}
            for u, (wi, w, stash, ci) in enumerate(units):
                pb = u % 2
                wb = wbig[:, pb * 4096:(pb + 1) * 4096]
                wsrc, _, wbv = tern_chunk_aps(wi, w, stash, ci, wb)

                def ld(sync, u=u, pb=pb, wbv=wbv, wsrc=wsrc):
                    if u >= 2:
                        sV.wait(sync, tvd[u - 2])
                    sW1[pb].inc(sync.dma_start(wbv, wsrc), 16)
                yE(ld)

                def v(vector, u=u, pb=pb, wb=wb, wi=wi):
                    sW1[pb].wait(vector, 16 * (u // 2 + 1))
                    sV.wait_all(vector)
                    sV.inc(vector.reduce_sum(wred[:, 0:1], wb, axis=AX.X, apply_absolute_value=True))
                    sV.wait_all(vector)
                    sV.inc(vector.tensor_add(wmean[:, wi:wi + 1], wmean[:, wi:wi + 1], wred[:, 0:1]))
                    tvd[u] = sV.n
                vE(v)

            def t_fin(tensor):
                sV.wait_all(tensor)
                sLB.wait(tensor, n_sLB)
                sT.inc(tensor.matmul(ps_w[0:1, 8:11], onescol_b[:, 0:1], wmean[:, 0:3], start=True, stop=True))
            tE(t_fin)
            ntf = sT.n

            def v_fin(vector):
                sT.wait(vector, ntf)
                sV.wait_all(vector)
                sV.inc(vector.tensor_scalar_mul(wrow[0:1, 0:3], ps_w[0:1, 8:11], 1.0 / (INNER * D)))
                sV.wait_all(vector)
                sV.inc(vector.tensor_scalar_max(wrow[0:1, 0:3], wrow[0:1, 0:3], 1e-5))
                sV.wait_all(vector)
                sV.inc(vector.tensor_scalar_mul(mw2_row[0:1, 0:3], wrow[0:1, 0:3], 0.5))
                sV.wait_all(vector)
                sV.inc(vector.reciprocal(wrow[0:1, 0:3], wrow[0:1, 0:3]))
                sV.wait_all(vector)
            vE(v_fin)
            nvf = sV.n

            def t_fin2(tensor):
                sV.wait(tensor, nvf)
                sT.inc(tensor.matmul(ps_w[:, 0:3], onesf_b[0:1, :], wrow[0:1, 0:3], start=True, stop=True))
                sT.inc(tensor.matmul(ps_w[:, 4:7], onesf_b[0:1, :], mw2_row[0:1, 0:3], start=True, stop=True))
            tE(t_fin2)
            ntf2 = sT.n

            def v_fin2(vector):
                sT.wait(vector, ntf2)
                sV.inc(vector.tensor_copy(wsc[:, 0:3], ps_w[:, 0:3]))
                sV.inc(vector.tensor_copy(mw2_b[:, 0:3], ps_w[:, 4:7]))
                sV.wait_all(vector)
            vE(v_fin2)
            n_wsc_v = sV.n

            tsd, tvd2 = {}, {}
            s2base = sS.n
            v2base = sV.n
            for u in range(len(units)):
                tsd[u] = s2base + 2 * (u + 1)
                tvd2[u] = v2base + (u + 1)
            sS.n = s2base + 2 * len(units)
            sV.n = v2base + len(units)

            for u, (wi, w, stash, ci) in enumerate(units):
                pb = u % 2
                wsrc, wdst, wbv = tern_chunk_aps(wi, w, stash, ci, wbig[:, pb * 4096:(pb + 1) * 4096])
                _, _, t2v = tern_chunk_aps(wi, w, stash, ci, t2big[:, pb * 4096:(pb + 1) * 4096])

                def ld(sync, u=u, pb=pb, wbv=wbv, wsrc=wsrc):
                    if u >= 2:
                        sS.wait(sync, tsd[u - 2])
                    else:
                        sV.wait(sync, tvd[len(units) - 2 + u])
                    sW1[pb].inc(sync.dma_start(wbv, wsrc), 16)
                yE(ld)

                def g(gpsimd, u=u, pb=pb, t2v=t2v, wdst=wdst):
                    sV.wait(gpsimd, tvd2[u])
                    sSW[pb].inc(gpsimd.dma_start(wdst, t2v), 16)
                gE(g)

            def s_all2(scalar):
                for u, (wi, w, stash, ci) in enumerate(units):
                    pb = u % 2
                    wb = wbig[:, pb * 4096:(pb + 1) * 4096]
                    scalar.wait_ge(sW1[pb].h, 16 * (12 + u // 2 + 1))
                    scalar.wait_ge(sV.h, n_wsc_v)
                    scalar.wait_ge(sV.h, tvd2.get(u - 1, 0))
                    scalar.activation(sgA[:, :], wb, AF.Sign, scale=wsc[:, wi:wi + 1],
                                      bias=chalf[:, 0:1]).then_inc(sS.h, 1)
                    scalar.activation(sgB[:, :], wb, AF.Sign, scale=wsc[:, wi:wi + 1],
                                      bias=chalf[:, 1:2]).then_inc(sS.h, 1)
            sE(s_all2)

            def v_all2(vector):
                for u in range(len(units)):
                    pb = u % 2
                    t2 = t2big[:, pb * 4096:(pb + 1) * 4096]
                    vector.wait_ge(sSW[pb].h, 16 * (u // 2 - 1 + 1) if u >= 2 else 0)
                    vector.wait_ge(sS.h, tsd[u])
                    vector.tensor_add(t2, sgA[:, :], sgB[:, :]).then_inc(sV.h, 1)
            vE(v_all2)
            n_stash = (sSW[0].n, sSW[1].n)
            n_ternW = (sW1[0].n, sW1[1].n, sS.n, sV.n)

        with contextlib.ExitStack() as esQ:
            sbQ = lambda n, s, dt: esQ.enter_context(nc.sbuf_tensor(n, s, dt))
            q_pk = [sbQ(f"qp_{i}", [128, T], BF16) for i in range(8)]
            k_pk = [sbQ(f"kp_{i}", [128, S], BF16) for i in range(2)]
            vaug = sbQ("vaug", [128, HKV * ST * 66], BF16)

            def tern_barrier(eng):
                sW1[0].wait(eng, n_ternW[0])
                sW1[1].wait(eng, n_ternW[1])
                sS.wait(eng, n_ternW[2])
                sV.wait(eng, n_ternW[3])
                sSW[0].wait(eng, n_stash[0])
                sSW[1].wait(eng, n_stash[1])
            for _emit in (vE, sE, tE, gE, yE):
                _emit(tern_barrier)

            def p0v2(vector):
                sV.inc(vector.memset(vaug[:, 64::66], 1.0))
            vE(p0v2)

            with contextlib.ExitStack() as esT:
                sbT = lambda n, s, dt: esT.enter_context(nc.sbuf_tensor(n, s, dt))
                psT = lambda n, s, dt=F32: esT.enter_context(nc.psum_tensor(n, s, dt))
                wq_b = sbT("wq_b", [128, KC * D], BF16)
                wkv_b = sbT("wkv_b", [128, KC * 512], BF16)
                xbuf = sbT("xbuf", [128, 2 * D], F32)
                xng = sbT("xng", [128, 2 * D], BF16)
                xT_t = sbT("xT_t", [128, D], BF16)
                means8 = sbT("means8", [128, 2], F32)
                rstds8 = sbT("rstds8", [128, 2], F32)
                lnt = sbT("lnt", [128, 2], F32)
                stats6 = sbT("stats6", [128, 12], F32)
                stats2 = sbT("stats2", [128, 2], F32)
                ct2 = sbT("ct2", [128, 1024], F32)
                st2 = sbT("st2", [128, 1024], F32)
                rt1 = sbT("rt1", [128, 512], F32)
                rt2 = sbT("rt2", [128, 512], F32)
                qro = sbT("qro", [128, D + 512], BF16)
                qsq = sbT("qsq", [128, 512], F32)
                nrm = sbT("nrm", [128, 16], F32)
                wred = sbT("wred", [128, 1], F32)

                ps_tr = psT("ps_tr", [128, 2048], BF16)
                ps_q = psT("ps_q", [128, 1024], F32)
                ps_kv = psT("ps_kv", [128, 3 * 512], F32)
                ps_ktr = psT("ps_ktr", [128, 256], BF16)

                def ldw(gpsimd):
                    sLW.inc(gpsimd.dma_start(wq_b[:, :].rearrange("p (k n) -> p k n", k=KC),
                                        wq_d.rearrange("(k p) n -> p k n", p=128)), 16)
                    sLW.inc(gpsimd.dma_start(wkv_b[:, :].rearrange("p (k n) -> p k n", k=KC),
                                        wkv_d.rearrange("(k p) n -> p k n", p=128)), 16)
                gE(ldw)
                n_ldw = sLW.n

                def emit_tile_tables(kt, own, nf):
                    pb = kt % 2

                    def ld_tab(sync, kt=kt, pb=pb):
                        sV.wait(sync, rope_done.get(kt - 2, 0))
                        sLT[pb].inc(sync.dma_start(ct2[:, pb * 512:(pb + 1) * 512],
                                                   ctab_d[kt * 128:(kt + 1) * 128, :]), 16)
                        sLT[pb].inc(sync.dma_start(st2[:, pb * 512:(pb + 1) * 512],
                                                   stab_d[kt * 128:(kt + 1) * 128, :]), 16)
                    yE(ld_tab)
                    tab_ready[kt] = sLT[pb].n

                ln_done, trev_done, rope_done = {}, {}, {}
                tab_ready = {}
                oth_defer = [None]
                oth_ktr, kev_ev, own_evt = {}, {}, {}
                p123_s = []

                def rec_s(waits, fn):
                    p123_s.append((list(waits), fn))
                    sS.n += 1
                qsq_free = [0]

                def emit_tile_stage2(gi, kt, own):
                    xn = xng[:, gi * D:(gi + 1) * D]
                    nf = 512 if own else 32
                    pb = kt % 2
                    kv4 = kt % 3

                    emit_tile_tables(kt, own, nf)

                    ts0 = (kt % 2) * 1024

                    def t_tr(tensor):
                        sV.wait(tensor, ln_done[kt])
                        sV.wait(tensor, trev_done.get(kt - 2, 0))
                        for k in range(KC):
                            ins = tensor.transpose(ps_tr[:, ts0 + k * 128:ts0 + (k + 1) * 128],
                                                   xn[:, k * 128:(k + 1) * 128], ident_b[:, :])
                        sT.inc(ins)
                    tE(t_tr)
                    n_ttr = sT.n

                    def v_trev(vector):
                        sT.wait(vector, n_ttr)
                        sV.wait_all(vector)
                        sV.inc(vector.tensor_copy(xT_t[:, :], ps_tr[:, ts0:ts0 + 1024]))
                        trev_done[kt] = sV.n
                    vE(v_trev)

                    def t_proj(tensor):
                        sV.wait(tensor, trev_done[kt])
                        sLW.wait(tensor, n_ldw)
                        sV.wait(tensor, rope_done.get(kt - 1 if own else kt - 2, 0))
                        if own:
                            for nn in range(2):
                                for k in range(KC):
                                    ins = tensor.matmul(ps_q[:, nn * 512:(nn + 1) * 512],
                                                        xT_t[:, k * 128:(k + 1) * 128],
                                                        wq_b[:, k * D + nn * 512:k * D + nn * 512 + 512],
                                                        start=(k == 0), stop=(k == KC - 1))
                            sT.inc(ins)
                        for k in range(KC):
                            ins = tensor.matmul(ps_kv[:, kv4 * 512:(kv4 + 1) * 512],
                                                xT_t[:, k * 128:(k + 1) * 128],
                                                wkv_b[:, k * 512:(k + 1) * 512],
                                                start=(k == 0), stop=(k == KC - 1))
                        sT.inc(ins)
                    tE(t_proj)
                    n_proj = sT.n

                    if oth_defer[0] is not None:
                        oth_defer[0]()
                        oth_defer[0] = None

                    if own:
                        m = kt

                        for half in range(2):
                            rec_s([(sT.h, n_proj), (sV.h, qsq_free[0])],
                                  lambda scalar, half=half: scalar.activation(
                                      qsq[:, :], ps_q[:, half * 512:(half + 1) * 512],
                                      AF.Square).then_inc(sS.h, 1))
                            n_ssq = sS.n

                            def v_qn(vector, half=half, n_ssq=n_ssq):
                                sS.wait(vector, n_ssq)
                                sV.wait_all(vector)
                                for s in range(8):
                                    hh = SLOT2HEAD[half * 8 + s]
                                    sV.inc(vector.reduce_sum(nrm[:, hh:hh + 1], qsq[:, s * 64:(s + 1) * 64], axis=AX.X))
                                sV.wait_all(vector)
                            vE(v_qn)
                            qsq_free[0] = sV.n

                        def v_qn2(vector):
                            sV.wait_all(vector)
                            sV.inc(vector.tensor_max(qn_mx[:, :], qn_mx[:, :], nrm[:, 0:16]))
                        vE(v_qn2)
                        qsq_free[0] = sV.n

                        def v_qrope(vector, pb=pb, ntab=None):
                            sLT[pb].wait(vector, tab_ready[kt])
                            sT.wait(vector, own_evt.get(kt - 1, 0))
                            sV.wait_all(vector)
                            x0 = ps_q[:, 0:1024:2]
                            x1 = ps_q[:, 1:1024:2]
                            tc = ct2[:, pb * 512:pb * 512 + 512]
                            ts = st2[:, pb * 512:pb * 512 + 512]
                            sV.inc(vector.tensor_mul(rt1[:, :], x0, tc))
                            sV.inc(vector.tensor_mul(rt2[:, :], x1, ts))
                            sV.wait_all(vector)
                            sV.inc(vector.tensor_sub(qro[:, 0:1024:2], rt1[:, :], rt2[:, :]))
                            sV.wait_all(vector)
                            sV.inc(vector.tensor_mul(rt1[:, :], x0, ts))
                            sV.inc(vector.tensor_mul(rt2[:, :], x1, tc))
                            sV.wait_all(vector)
                            sV.inc(vector.tensor_add(qro[:, 1:1024:2], rt1[:, :], rt2[:, :]))
                        vE(v_qrope)
                        n_qrope = sV.n

                        def emit_own_evac(kt=kt, m=m, ts0=ts0, n_qrope=n_qrope, kb=1024 + (kt % 2) * 256):
                            def t_qtr(tensor):
                                sV.wait(tensor, n_qrope)
                                for k in range(KC):
                                    ins = tensor.transpose(ps_tr[:, ts0 + k * 128:ts0 + (k + 1) * 128],
                                                           qro[:, k * 128:(k + 1) * 128], ident_b[:, :])
                                sT.inc(ins)
                            tE(t_qtr)
                            n_qtr = sT.n

                            def v_qev(vector):
                                sT.wait(vector, n_qtr)
                                for k in range(KC):
                                    sV.inc(vector.tensor_copy(q_pk[k][:, m * 128:(m + 1) * 128],
                                                              ps_tr[:, ts0 + k * 128:ts0 + (k + 1) * 128]))
                                trev_done[kt] = sV.n
                            vE(v_qev)

                            def t_ktr(tensor):
                                sV.wait(tensor, rope_done[kt])
                                sV.wait(tensor, trev_done[kt])
                                for k in range(2):
                                    ins = tensor.transpose(ps_tr[:, ts0 + k * 128:ts0 + (k + 1) * 128],
                                                           qro[:, kb + k * 128:kb + (k + 1) * 128], ident_b[:, :])
                                sT.inc(ins)
                            tE(t_ktr)
                            own_evt[kt] = sT.n
                            n_ktr = sT.n

                            def v_kev(vector):
                                sT.wait(vector, n_ktr)
                                for k in range(2):
                                    sV.inc(vector.tensor_copy(k_pk[k][:, kt * 128:(kt + 1) * 128],
                                                              ps_tr[:, ts0 + k * 128:ts0 + (k + 1) * 128]))
                                trev_done[kt] = sV.n
                            vE(v_kev)

                    kvs = ps_kv[:, kv4 * 512:(kv4 + 1) * 512]

                    rec_s([(sT.h, n_proj), (sV.h, qsq_free[0])],
                          lambda scalar, kvs=kvs: scalar.activation(
                              qsq[:, 0:256], kvs[:, 0:256], AF.Square).then_inc(sS.h, 1))
                    n_sksq = sS.n

                    def v_kn(vector):
                        sS.wait(vector, n_sksq)
                        sV.wait_all(vector)
                        for h in range(HKV):
                            sV.inc(vector.reduce_sum(nrm[:, h:h + 1], qsq[:, h * 64:(h + 1) * 64], axis=AX.X))
                        sV.wait_all(vector)
                        sV.inc(vector.tensor_max(kn_mx[:, 0:HKV], kn_mx[:, 0:HKV], nrm[:, 0:HKV]))
                    vE(v_kn)
                    qsq_free[0] = sV.n

                    obase = (1024 + (kt % 2) * 256) if own else (kt % 2) * 512

                    def v_krope(vector, pb=pb, kv4=kv4, obase=obase):
                        sLT[pb].wait(vector, tab_ready[kt])
                        if not own:
                            sT.wait(vector, oth_ktr.get(kt - 2, 0))
                        sV.wait_all(vector)
                        x0 = AP(ps_kv, kv4 * 512, [[1536, 128], [64, 4], [2, 32]])
                        x1 = AP(ps_kv, kv4 * 512 + 1, [[1536, 128], [64, 4], [2, 32]])
                        tc = AP(ct2, pb * 512, [[1024, 128], [0, 4], [1, 32]])
                        ts = AP(st2, pb * 512, [[1024, 128], [0, 4], [1, 32]])
                        r1 = rt1[:, 0:128].rearrange("p (h j) -> p h j", h=4)
                        r2 = rt2[:, 0:128].rearrange("p (h j) -> p h j", h=4)
                        q0 = AP(qro, obase, [[1536, 128], [64, 4], [2, 32]])
                        q1 = AP(qro, obase + 1, [[1536, 128], [64, 4], [2, 32]])
                        sV.inc(vector.tensor_mul(r1, x0, tc))
                        sV.inc(vector.tensor_mul(r2, x1, ts))
                        sV.wait_all(vector)
                        sV.inc(vector.tensor_sub(q0, r1, r2))
                        sV.wait_all(vector)
                        sV.inc(vector.tensor_mul(r1, x0, ts))
                        sV.inc(vector.tensor_mul(r2, x1, tc))
                        sV.wait_all(vector)
                        sV.inc(vector.tensor_add(q1, r1, r2))
                        sV.wait_all(vector)
                        src = AP(ps_kv, kv4 * 512 + 256, [[1536, 128], [64, 4], [1, 64]])
                        dst = AP(vaug, kt * 66, [[HKV * ST * 66, 128], [ST * 66, 4], [1, 64]])
                        sV.inc(vector.tensor_copy(dst, src))
                        rope_done[kt] = sV.n
                    vE(v_krope)

                    if own:
                        oth_defer[0] = emit_own_evac
                    else:
                        def emit_kev(kt=kt, obase=obase, nrope=None):
                            def t_ktr(tensor, kt=kt, obase=obase):
                                sV.wait(tensor, rope_done[kt])
                                sV.wait(tensor, kev_ev.get(kt - 2, 0))
                                for k in range(2):
                                    ins = tensor.transpose(ps_ktr[:, k * 128:(k + 1) * 128],
                                                           qro[:, obase + k * 128:obase + (k + 1) * 128],
                                                           ident_b[:, :])
                                sT.inc(ins)
                            tE(t_ktr)
                            oth_ktr[kt] = sT.n
                            n_ktr = sT.n

                            def v_kev(vector, kt=kt, n_ktr=n_ktr):
                                sT.wait(vector, n_ktr)
                                for k in range(2):
                                    sV.inc(vector.tensor_copy(k_pk[k][:, kt * 128:(kt + 1) * 128],
                                                              ps_ktr[:, k * 128:(k + 1) * 128]))
                                kev_ev[kt] = sV.n
                            vE(v_kev)
                        oth_defer[0] = emit_kev

                def emit_group(tiles):
                    for gi, (kt, own) in enumerate(tiles):
                        if not own:
                            j = kt - MT

                            def ld(sync, j=j, gi=gi, kt=kt):
                                sV.wait(sync, ln_done.get(kt - 2, 0))
                                sLX[0].inc(sync.dma_start(xbuf[:, gi * D:(gi + 1) * D],
                                                          x_oth[j * 128:(j + 1) * 128, :]), 16)
                            yE(ld)
                    n_gload = sLX[0].n
                    for gi, (kt, own) in enumerate(tiles):
                        src = R[:, kt * D:(kt + 1) * D] if own else xbuf[:, gi * D:(gi + 1) * D]

                        def v_st(vector, gi=gi, src=src, own=own, kt=kt):
                            if not own:
                                sLX[0].wait(vector, n_gload)
                            elif kt == 0:
                                sLR.wait_all(vector)
                            sV.wait_all(vector)
                            sV.inc(vector.bn_stats(stats6[:, 0:6], src[:, 0:512]))
                            sV.inc(vector.bn_stats(stats6[:, 6:12], src[:, 512:1024]))
                            sV.wait_all(vector)
                            sV.inc(vector.bn_aggr(stats2[:, 0:2], stats6[:, 0:12]))
                            sV.wait_all(vector)
                            sV.inc(vector.tensor_copy(means8[:, gi:gi + 1], stats2[:, 0:1]))
                            sV.inc(vector.tensor_scalar_add(lnt[:, gi:gi + 1], stats2[:, 1:2], 1e-5))
                        vE(v_st)
                    nst = sV.n

                    rec_s([(sV.h, nst)],
                          lambda scalar, n=len(tiles): scalar.activation(
                              rstds8[:, 0:n], lnt[:, 0:n], AF.Sqrt).then_inc(sS.h, 1))
                    ns_s = sS.n

                    def v_rstd(vector, n=len(tiles)):
                        sS.wait(vector, ns_s)
                        sV.wait_all(vector)
                        sV.inc(vector.reciprocal(rstds8[:, 0:n], rstds8[:, 0:n]))
                    vE(v_rstd)
                    nrstd = sV.n

                    for gi, (kt, own) in enumerate(tiles):
                        src = R[:, kt * D:(kt + 1) * D] if own else xbuf[:, gi * D:(gi + 1) * D]

                        def v_nm(vector, gi=gi, src=src, kt=kt):
                            sV.wait(vector, nrstd)
                            sV.wait_all(vector)
                            sV.inc(vector.tensor_scalar(xng[:, gi * D:(gi + 1) * D], src,
                                                        means8[:, gi:gi + 1], rstds8[:, gi:gi + 1],
                                                        ALU.subtract, ALU.mult))
                            ln_done[kt] = sV.n
                        vE(v_nm)

                    for gi, (kt, own) in enumerate(tiles):
                        emit_tile_stage2(gi, kt, own)

                tiles_all = [(kt, True) for kt in range(MT)] + [(MT + j, False) for j in range(OT)]
                for i in range(0, len(tiles_all), 2):
                    emit_group(tiles_all[i:i + 2])
                if oth_defer[0] is not None:
                    oth_defer[0]()
                    oth_defer[0] = None

                def s_p123(scalar):
                    for waits, fn in p123_s:
                        for h, v in waits:
                            if v:
                                scalar.wait_ge(h, v)
                        fn(scalar)
                sE(s_p123)

            with contextlib.ExitStack() as esO:
                sbO = lambda n, s, dt: esO.enter_context(nc.sbuf_tensor(n, s, dt))
                otm = sbO("otm", [128, MT * D], BF16)
                wo_b = sbO("wo_b", [128, KC * D], BF16)
                pT = sbO("pT", [128, 2 * QHW], BF16)
                o_sb = sbO("o_sb", [65, 2 * QHW], BF16)
                recb = sbO("recb", [128, QTB], F32)
                brow = sbO("brow", [1, 16], F32)
                oT_t = sbO("oT_t", [128, D], BF16)
                aow = sbO("aow", [128, 1024], F32)
                lnw = sbO("lnw", [128, 2], F32)
                ostat = sbO("ostat", [128, 12], F32)

                def ldwo(gpsimd):
                    for s_ in (sW1[0], sW1[1], sSW[0], sSW[1], sLX[0], sLX[1]):
                        s_.wait_all(gpsimd)
                    sV.wait_all(gpsimd)
                    sS.wait_all(gpsimd)
                    sT.wait_all(gpsimd)
                    sLW.inc(gpsimd.dma_start(wo_b[:, :].rearrange("p (k n) -> p k n", k=KC),
                                        wo_d.rearrange("(k p) n -> p k n", p=128)), 16)
                gE(ldwo)
                n_ldwo = sLW.n

                with contextlib.ExitStack() as esP4:
                    psO = lambda n, s, dt=F32: esP4.enter_context(nc.psum_tensor(n, s, dt))
                    esB = contextlib.ExitStack()
                    ps_bc2 = esB.enter_context(nc.psum_tensor("ps_bc2", [128, 16], F32))
                    ps_nrm = esB.enter_context(nc.psum_tensor("ps_nrm", [128, 256 + 128], F32))

                    def t_bnd0(tensor):
                        sV.wait_all(tensor)
                        sT.inc(tensor.transpose(ps_nrm[0:16, 0:128], qn_mx[:, 0:16], eye128f_b[:, :]))
                        sT.inc(tensor.transpose(ps_nrm[0:4, 128:256], kn_mx[:, 0:4], eye128f_b[:, :]))
                    tE(t_bnd0)
                    nbt0 = sT.n

                    def v_bound1b(vector):
                        sT.wait(vector, nbt0)
                        sV.inc(vector.reduce_max(qn_mx[0:16, 0:1], ps_nrm[0:16, 0:128], axis=AX.X))
                        sV.inc(vector.reduce_max(kn_mx[0:4, 0:1], ps_nrm[0:4, 128:256], axis=AX.X))
                    vE(v_bound1b)
                    nb0b = sV.n

                    def t_bnd1(tensor):
                        sV.wait(tensor, nb0b)
                        sT.inc(tensor.transpose(ps_nrm[0:1, 256:272], qn_mx[0:16, 0:1], eye32f_b[0:16, 0:16]))
                        sT.inc(tensor.transpose(ps_nrm[0:1, 272:276], kn_mx[0:4, 0:1], eye32f_b[0:4, 0:4]))
                    tE(t_bnd1)
                    nbt1 = sT.n

                    def v_bound1c(vector):
                        sT.wait(vector, nbt1)
                        sV.inc(vector.tensor_copy(brow[0:1, :], ps_nrm[0:1, 256:272]))
                        sV.wait_all(vector)
                        src_kn = AP(ps_nrm, 272, [[384, 1], [1, 4], [0, 4]])
                        sV.inc(vector.tensor_mul(brow[0:1, :].rearrange("p (a b) -> p a b", a=4),
                                                 brow[0:1, :].rearrange("p (a b) -> p a b", a=4), src_kn))
                    vE(v_bound1c)
                    nb1 = sV.n

                    def s_bound(scalar):
                        sV.wait(scalar, nb1)
                        sS.inc(scalar.activation(brow[0:1, :], brow[0:1, :], AF.Sqrt))
                    sE(s_bound)
                    nb2 = sS.n

                    def v_bound2(vector):
                        sS.wait(vector, nb2)
                        sV.wait_all(vector)
                        sV.inc(vector.tensor_scalar_mul(brow[0:1, :], brow[0:1, :], -1.02 / 32.0))
                    vE(v_bound2)
                    nb3 = sV.n

                    def t_bound(tensor):
                        sV.wait(tensor, nb3)
                        sT.inc(tensor.matmul(ps_bc2[:, 0:16], onesf_b[0:1, :], brow[0:1, :], start=True, stop=True))
                    tE(t_bound)
                    nb4 = sT.n

                    def v_bound3(vector):
                        sT.wait(vector, nb4)
                        sV.inc(vector.tensor_copy(bias16[:, :], ps_bc2[:, 0:16]))
                    vE(v_bound3)
                    n_bias_v = sV.n
                    esB.close()
                    ps_sA = psO("ps_sA", [128, QHW])
                    ps_sB = psO("ps_sB", [128, QHW])
                    ps_o = psO("ps_o", [128, QHW])
                    ps_otr = esP4.enter_context(nc.psum_tensor("ps_otr", [128, QTB * 66], BF16))
                    ps_warm = psO("ps_warm", [128, 512])

                    NWARM = 6
                    def t_warm(tensor):
                        for j in range(NWARM):
                            tensor.matmul(ps_warm[:, 0:512], ident_b[:, 0:128],
                                          k_pk[0][:, 0:512],
                                          start=(j == 0), stop=False)
                    tE(t_warm)

                    it = [0]
                    prev_bd = [None]
                    otr_ev = {}
                    s_items = []

                    def emit_otr(h, qh, par, base_qt):
                        def t_otr(tensor):
                            sVo.wait(tensor, otr_ev[(h, qh)])
                            sVsc.wait(tensor, sVsc.n)
                            for j in range(QTB):
                                ins = tensor.transpose(ps_otr[:, j * 66:j * 66 + 65],
                                                       o_sb[0:65, par * QHW + j * 128: par * QHW + (j + 1) * 128],
                                                       ident_b[0:65, 0:65])
                            sTo.inc(ins)
                        tE(t_otr)
                        nto = sTo.n

                        def v_osc(vector):
                            sTo.wait(vector, nto)
                            sV.wait_all(vector)
                            rsrc = AP(ps_otr, 64, [[QTB * 66, 128], [66, QTB]])
                            sV.inc(vector.reciprocal(recb[:, 0:QTB], rsrc))
                            sV.wait_all(vector)
                            osrc = AP(ps_otr, 0, [[QTB * 66, 128], [66, QTB], [1, 64]])
                            odst = AP(otm, base_qt * D + h * 64, [[MT * D, 128], [D, QTB], [1, 64]])
                            rb = AP(recb, 0, [[QTB, 128], [1, QTB], [0, 64]])
                            sVsc.inc(vector.tensor_mul(odst, osrc, rb))
                        vE(v_osc)

                    for h in range(H):
                        hkv = h // (H // HKV)
                        if h in PK0:
                            qpi, qoff = PK0.index(h), 0
                        else:
                            qpi, qoff = PK64.index(h), 64
                        koff = 64 * (hkv % 2)
                        kpi = hkv // 2
                        assert koff == qoff
                        for qh in range(NQH):
                            i0 = it[0]
                            nvo0 = sVo.n
                            sTs.n += ST
                            sSe.n += ST
                            sTp.n += ST

                            def t_body(tensor, h=h, qh=qh, i0=i0, qpi=qpi, qoff=qoff,
                                       kpi=kpi, hkv=hkv, nvo0=nvo0, nbv=n_bias_v):
                                def t_s(kt):
                                    i = i0 + kt
                                    ps_s = ps_sA if i % 2 == 0 else ps_sB
                                    if i == 0:
                                        tensor.wait_ge(sV.h, nbv)
                                    if i >= 2:
                                        tensor.wait_ge(sSe.h, i - 1)
                                    for s0_, w in QNS:
                                        ins = tensor.matmul(ps_s[:, s0_:s0_ + w],
                                                            k_pk[kpi][qoff:qoff + 64, kt * 128:(kt + 1) * 128],
                                                            q_pk[qpi][qoff:qoff + 64, qh * QHW + s0_: qh * QHW + s0_ + w],
                                                            start=True, stop=True)
                                    ins.then_inc(sTs.h, 1)
                                    tensor.matmul(ps_warm[:, 0:512], ident_b[:, 0:128],
                                                  k_pk[0][:, 0:512], start=False, stop=False)

                                def t_pv(kt):
                                    i = i0 + kt
                                    buf = i % 2
                                    tensor.wait_ge(sSe.h, i + 1)
                                    if kt == 0:
                                        if nvo0:
                                            tensor.wait_ge(sVo.h, nvo0)
                                    vsl = vaug[:, (hkv * ST + kt) * 66:(hkv * ST + kt) * 66 + 65]
                                    for s0_, w in QNS:
                                        ins = tensor.matmul(ps_o[0:65, s0_:s0_ + w],
                                                            vsl, pT[:, buf * QHW + s0_: buf * QHW + s0_ + w],
                                                            start=(kt == 0), stop=(kt == ST - 1))
                                    ins.then_inc(sTp.h, 1)

                                t_s(0)
                                for kt in range(1, ST):
                                    t_s(kt)
                                    t_pv(kt - 1)
                                t_pv(ST - 1)
                            tE(t_body)

                            s_items.append((h, i0))
                            it[0] += ST

                            par = (h * NQH + qh) % 2

                            def v_oev(vector, h=h, qh=qh, par=par, i=it[0], nto=sTo.n):
                                sTp.wait(vector, i)
                                sTo.wait(vector, nto)
                                sV.wait_all(vector)
                                sVo.inc(vector.tensor_copy(o_sb[0:65, par * QHW:(par + 1) * QHW], ps_o[0:65, 0:QHW]))
                                otr_ev[(h, qh)] = sVo.n
                            vE(v_oev)

                            if prev_bd[0] is not None:
                                emit_otr(*prev_bd[0])
                            prev_bd[0] = (h, qh, par, qh * QTB)
                    emit_otr(*prev_bd[0])

                    def s_all(scalar, nbv=n_bias_v):
                        for h_, i0_ in s_items:
                            if i0_ == 0:
                                scalar.wait_ge(sV.h, nbv)
                            for kt in range(ST):
                                i = i0_ + kt
                                buf = i % 2
                                ps_s = ps_sA if buf == 0 else ps_sB
                                scalar.wait_ge(sTs.h, i + 1)
                                if i >= 2:
                                    scalar.wait_ge(sTp.h, i - 1)
                                scalar.activation(pT[:, buf * QHW:(buf + 1) * QHW], ps_s[:, 0:QHW],
                                                  AF.Exp, scale=1.0 / 32.0,
                                                  bias=bias16[:, h_:h_ + 1]).then_inc(sSe.h, 1)
                    sE(s_all)

                    def t_warm_end(tensor):
                        sT.inc(tensor.matmul(ps_warm[:, 0:512], ident_b[:, 0:128],
                                             k_pk[0][:, 0:512], start=False, stop=True))
                    tE(t_warm_end)
                    n_warm = sT.n

                with contextlib.ExitStack() as esP5:
                    ps_tr5 = esP5.enter_context(nc.psum_tensor("ps_tr5", [128, 1024], BF16))
                    ps_wo = esP5.enter_context(nc.psum_tensor("ps_wo", [128, 2048], F32))
                    p5_ev, p5_ln = {}, {}
                    p5_sink = []
                    p5_tail = [None]

                    for m in range(MT):
                        def t_otr5(tensor, m=m, nvsc=sVsc.n, nse=sSe.n, nw=n_warm):
                            sSe.wait(tensor, nse)
                            sVsc.wait(tensor, nvsc)
                            sT.wait(tensor, nw)
                            sV.wait(tensor, p5_ev.get(m - 1, 0))
                            for k in range(KC):
                                ins = tensor.transpose(ps_tr5[:, k * 128:(k + 1) * 128],
                                                       otm[:, m * D + k * 128: m * D + (k + 1) * 128], ident_b[:, :])
                            sT.inc(ins)
                        tE(t_otr5)
                        n5a = sT.n

                        def v_oev5(vector, m=m):
                            sT.wait(vector, n5a)
                            sV.wait_all(vector)
                            sV.inc(vector.tensor_copy(oT_t[:, :], ps_tr5[:, 0:1024]))
                            p5_ev[m] = sV.n
                        vE(v_oev5)

                        if p5_tail[0] is not None:
                            p5_tail[0]()
                            p5_tail[0] = None

                        ws0 = (m % 2) * 1024

                        def t_wo(tensor, m=m, ws0=ws0):
                            sV.wait(tensor, p5_ev[m])
                            sLW.wait(tensor, n_ldwo)
                            sV.wait(tensor, p5_ln.get(m - 2, 0))
                            for nn in range(2):
                                for k in range(KC):
                                    ins = tensor.matmul(ps_wo[:, ws0 + nn * 512:ws0 + (nn + 1) * 512],
                                                        oT_t[:, k * 128:(k + 1) * 128],
                                                        wo_b[:, k * D + nn * 512: k * D + nn * 512 + 512],
                                                        start=(k == 0), stop=(k == KC - 1))
                            sT.inc(ins)
                        tE(t_wo)
                        n5b = sT.n

                        def emit_p5_tail(m=m, ws0=ws0, n5b=n5b):
                            def v_lnw(vector):
                                sT.wait(vector, n5b)
                                sV.wait_all(vector)
                                sV.inc(vector.bn_stats(ostat[:, 0:6], ps_wo[:, ws0:ws0 + 512]))
                                sV.inc(vector.bn_stats(ostat[:, 6:12], ps_wo[:, ws0 + 512:ws0 + 1024]))
                                sV.wait_all(vector)
                                sV.inc(vector.bn_aggr(lnw[:, 0:2], ostat[:, 0:12]))
                                sV.wait_all(vector)
                                sV.inc(vector.tensor_scalar_add(lnw[:, 1:2], lnw[:, 1:2], 1e-5))
                            vE(v_lnw)
                            n5c = sV.n

                            def s_lnw(scalar, n5c=n5c):
                                scalar.wait_ge(sV.h, n5c)
                                scalar.activation(lnw[:, 1:2], lnw[:, 1:2], AF.Sqrt).then_inc(sS.h, 1)
                            p5_sink.append(s_lnw)
                            sS.n += 1
                            n5d = sS.n

                            def v_res(vector, n5d=n5d):
                                sS.wait(vector, n5d)
                                sV.wait_all(vector)
                                sV.inc(vector.reciprocal(lnw[:, 1:2], lnw[:, 1:2]))
                                sV.wait_all(vector)
                                sV.inc(vector.tensor_scalar(aow[:, :], ps_wo[:, ws0:ws0 + 1024], lnw[:, 0:1], lnw[:, 1:2],
                                                            ALU.subtract, ALU.mult))
                                sV.wait_all(vector)
                                sV.inc(vector.tensor_add(R[:, m * D:(m + 1) * D], R[:, m * D:(m + 1) * D], aow[:, :]))
                                p5_ln[m] = sV.n
                            vE(v_res)
                        p5_tail[0] = emit_p5_tail

                    if p5_tail[0] is not None:
                        p5_tail[0]()
                        p5_tail[0] = None

                    def s_p5(scalar):
                        for b_ in p5_sink:
                            b_(scalar)
                    sE(s_p5)

        if debug_tap == "x1":
            n_x1 = sV.n

            def g_dbg(gpsimd):
                sV.wait(gpsimd, n_x1)
                sSW[0].inc(gpsimd.dma_start(
                    dbg_d[:, :], R[:, 0:min(MT, 4) * 1024]), 16)
            gE(g_dbg)

        n_ldw_all = sLW.n
        n_T_preF, n_S_preF, n_V_preF = sT.n, sS.n, sV.n
        with contextlib.ExitStack() as esF:
            with contextlib.ExitStack() as esF1:
                sbF1 = lambda n, s, dt: esF1.enter_context(nc.sbuf_tensor(n, s, dt))
                xqT = sbF1("xqT", [128, KC * T], BF16)
                xnf = sbF1("xnf", [128, D], F32)
                yq = sbF1("yq", [128, D], F32)
                rq = sbF1("rq", [128, D], F32)
                uq = sbF1("uq", [128, D], F32)
                aq = sbF1("aq", [128, D], BF16)
                xq_b = sbF1("xq_b", [128, D], BF16)
                fst = sbF1("fst", [128, 12], F32)
                fln = sbF1("fln", [128, 4], F32)
                silu_b = sbF1("silu_b", [128, 512], F32)
                gu_b = sbF1("gu_b", [128, 2 * 512], F32)
                tg_b = sbF1("tg_b", [128, 2 * ICB * 512], BF16)
                tu_b = sbF1("tu_b", [128, 2 * ICB * 512], BF16)
                ps_trF = esF1.enter_context(nc.psum_tensor("ps_trF", [128, 1024], BF16))
                ps_g = esF1.enter_context(nc.psum_tensor("ps_g", [128, 2 * 512], F32))
                ps_u = esF1.enter_context(nc.psum_tensor("ps_u", [128, 2 * 512], F32))

                f6 = {}
                p6_sink = []
                p6_tail = [None]
                for m in range(MT):
                    def v_f6a(vector, m=m):
                        sV.wait_all(vector)
                        sS.wait_all(vector)
                        sV.inc(vector.bn_stats(fst[:, 0:6], R[:, m * D:m * D + 512]))
                        sV.inc(vector.bn_stats(fst[:, 6:12], R[:, m * D + 512:(m + 1) * D]))
                        sV.wait_all(vector)
                        sV.inc(vector.bn_aggr(fln[:, 0:2], fst[:, 0:12]))
                        sV.wait_all(vector)
                        sV.inc(vector.tensor_scalar_add(fln[:, 1:2], fln[:, 1:2], 1e-5))
                    vE(v_f6a)
                    n6a = sV.n

                    def s_f6(scalar, n6a=n6a):
                        scalar.wait_ge(sV.h, n6a)
                        scalar.activation(fln[:, 1:2], fln[:, 1:2], AF.Sqrt).then_inc(sS.h, 1)
                    p6_sink.append(s_f6)
                    sS.n += 1
                    n6b = sS.n

                    def v_f6b(vector, m=m):
                        sS.wait(vector, n6b)
                        sV.wait_all(vector)
                        sV.inc(vector.reciprocal(fln[:, 1:2], fln[:, 1:2]))
                        sV.wait_all(vector)
                        sV.inc(vector.tensor_scalar_sub(xnf[:, :], R[:, m * D:(m + 1) * D], fln[:, 0:1]))
                        sV.wait_all(vector)
                        sV.inc(vector.reduce_max(fln[:, 2:3], xnf[:, :], axis=AX.X, apply_absolute_value=True))
                        sV.wait_all(vector)
                        sV.inc(vector.tensor_mul(fln[:, 2:3], fln[:, 2:3], fln[:, 1:2]))
                        sV.wait_all(vector)
                        sV.inc(vector.tensor_scalar_max(fln[:, 2:3], fln[:, 2:3], 1e-5))
                        sV.wait_all(vector)
                        sV.inc(vector.tensor_scalar_mul(rs_x[:, m:m + 1], fln[:, 2:3], 1.0 / 127.0))
                        sV.wait_all(vector)
                        sV.inc(vector.reciprocal(fln[:, 3:4], rs_x[:, m:m + 1]))
                        sV.wait_all(vector)
                        sV.inc(vector.tensor_mul(fln[:, 3:4], fln[:, 3:4], fln[:, 1:2]))
                        sV.wait_all(vector)
                        sV.inc(vector.tensor_scalar_mul(yq[:, :], xnf[:, :], fln[:, 3:4]))
                        sV.wait_all(vector)
                        sV.inc(vector.tensor_mul(cgu[:, m:m + 1], rs_x[:, m:m + 1], mw2_b[:, 0:1]))
                        sV.inc(vector.tensor_mul(cgu[:, MT + m:MT + m + 1], rs_x[:, m:m + 1], mw2_b[:, 1:2]))
                    vE(v_f6b)
                    if p6_tail[0] is not None:
                        p6_tail[0]()
                        p6_tail[0] = None
                    items6 = [{"src": yq[:, hh * 512:(hh + 1) * 512],
                               "dst": xq_b[:, hh * 512:(hh + 1) * 512],
                               "f": rq[:, hh * 512:(hh + 1) * 512],
                               "a": aq[:, hh * 512:(hh + 1) * 512],
                               "t": uq[:, hh * 512:(hh + 1) * 512]} for hh in range(2)]
                    emit_round_ladder(blk, sV, sS, items6,
                                      lambda it: it["f"], lambda it: it["a"], lambda it: it["t"],
                                      s_sink=p6_sink)

                    def v_f6sync(vector):
                        sV.wait_all(vector)
                    vE(v_f6sync)
                    n6c = sV.n

                    def emit_f6_tail(m=m, n6c=n6c):
                        def t_f6(tensor):
                            sV.wait(tensor, n6c)
                            sV.wait(tensor, f6.get(m - 1, 0))
                            for k in range(KC):
                                ins = tensor.transpose(ps_trF[:, k * 128:(k + 1) * 128],
                                                       xq_b[:, k * 128:(k + 1) * 128], ident_b[:, :])
                            sT.inc(ins)
                        tE(t_f6)
                        n6d = sT.n

                        def v_f6c(vector, n6d=n6d):
                            sT.wait(vector, n6d)
                            sV.wait_all(vector)
                            dst = xqT[:, :].rearrange("p (k t) -> p k t", k=KC)[:, :, m * 128:(m + 1) * 128]
                            src = ps_trF[:, 0:1024].rearrange("p (k j) -> p k j", k=KC)
                            sV.inc(vector.tensor_copy(dst, src))
                            f6[m] = sV.n
                        vE(v_f6c)
                    p6_tail[0] = emit_f6_tail
                if p6_tail[0] is not None:
                    p6_tail[0]()
                    p6_tail[0] = None
                def s_p6(scalar):
                    for b_ in p6_sink:
                        b_(scalar)
                sE(s_p6)
                n_xq = f6[MT - 1]

                f7t, f7v = {}, {}
                p7_sink = []
                for c in range(ICB):
                    cb = c % 2

                    def ld_t(sync, c=c, cb=cb):
                        sSW[0].wait(sync, n_stash[0])
                        sSW[1].wait(sync, n_stash[1])
                        sLW.wait(sync, n_ldw_all)
                        sT.wait(sync, n_T_preF)
                        sS.wait(sync, n_S_preF)
                        sV.wait(sync, n_V_preF)
                        if c >= 2:
                            sT.wait(sync, f7t.get((c - 2, MT - 1), 0))
                        for k in range(KC):
                            sLT[cb].inc(sync.dma_start(
                                tg_b[:, cb * ICB * 512 + k * 512:cb * ICB * 512 + (k + 1) * 512],
                                tg_s[c * 512:(c + 1) * 512, k * 128:(k + 1) * 128], transpose=True), 16)
                            sLT[cb].inc(sync.dma_start(
                                tu_b[:, cb * ICB * 512 + k * 512:cb * ICB * 512 + (k + 1) * 512],
                                tu_s[c * 512:(c + 1) * 512, k * 128:(k + 1) * 128], transpose=True), 16)
                    yE(ld_t)
                    nlt_c = sLT[cb].n

                    for m in range(MT):
                        gb = m % 2

                        def t_f7(tensor, c=c, m=m, cb=cb, gb=gb, nlt_c=nlt_c):
                            sV.wait(tensor, n_xq)
                            sLT[cb].wait(tensor, nlt_c)
                            if m >= 2:
                                sV.wait(tensor, f7v.get((c, m - 2), 0))
                            elif c > 0:
                                sV.wait(tensor, f7v.get((c - 1, MT - 2 + m), 0))
                            for k in range(KC):
                                ins = tensor.matmul(ps_g[:, gb * 512:(gb + 1) * 512],
                                                    xqT[:, k * T + m * 128:k * T + (m + 1) * 128],
                                                    tg_b[:, cb * ICB * 512 + k * 512:cb * ICB * 512 + (k + 1) * 512],
                                                    start=(k == 0), stop=(k == KC - 1))
                            sT.inc(ins)
                            for k in range(KC):
                                ins = tensor.matmul(ps_u[:, gb * 512:(gb + 1) * 512],
                                                    xqT[:, k * T + m * 128:k * T + (m + 1) * 128],
                                                    tu_b[:, cb * ICB * 512 + k * 512:cb * ICB * 512 + (k + 1) * 512],
                                                    start=(k == 0), stop=(k == KC - 1))
                            sT.inc(ins)
                            f7t[(c, m)] = sT.n
                        tE(t_f7)

                        def s_f7(scalar, c=c, m=m, gb=gb,
                                 nft=f7t[(c, m)],
                                 prev=f7v.get((c, m - 1) if m >= 1 else (c - 1, MT - 1), 0)):
                            scalar.wait_ge(sT.h, nft)
                            if prev:
                                scalar.wait_ge(sV.h, prev)
                            scalar.activation(silu_b[:, :], ps_g[:, gb * 512:(gb + 1) * 512],
                                              AF.Silu, scale=cgu[:, m:m + 1]).then_inc(sS.h, 1)
                        p7_sink.append(s_f7)
                        sS.n += 1
                        n7s = sS.n

                        def v_f7(vector, c=c, m=m, gb=gb, n7s=n7s):
                            sS.wait(vector, n7s)
                            i = c * MT + m
                            if i >= 2:
                                sGU[gb].wait(vector, 16 * ((i - 2) // 2 + 1))
                            sV.wait_all(vector)
                            sV.inc(vector.scalar_tensor_tensor(gu_b[:, gb * 512:(gb + 1) * 512],
                                                               ps_u[:, gb * 512:(gb + 1) * 512],
                                                               cgu[:, MT + m:MT + m + 1], silu_b[:, :],
                                                               ALU.mult, ALU.mult))
                            f7v[(c, m)] = sV.n
                        vE(v_f7)

                        def g_f7(gpsimd, c=c, m=m, gb=gb):
                            sV.wait(gpsimd, f7v[(c, m)])
                            sGU[gb].inc(gpsimd.dma_start(gu_s[m * 128:(m + 1) * 128, c * 512:(c + 1) * 512],
                                                         gu_b[:, gb * 512:(gb + 1) * 512]), 16)
                        gE(g_f7)
                def s_p7(scalar):
                    for b_ in p7_sink:
                        b_(scalar)
                sE(s_p7)

                n_gu = (sGU[0].n, sGU[1].n)
                n_lt7 = (sLT[0].n, sLT[1].n)
                n_t7 = sT.n
                n_s7 = sS.n
                n_v7 = sV.n

            with contextlib.ExitStack() as esF2:
                sb8 = lambda n, s, dt: esF2.enter_context(nc.sbuf_tensor(n, s, dt))
                td_b = sb8("td_b", [128, IC * 1024], BF16)
                gur = sb8("gur", [128, 2 * INNER], F32)
                y8 = sb8("y8", [128, INNER], F32)
                a8 = sb8("a8", [128, INNER], BF16)
                guq = sb8("guq", [128, INNER], BF16)
                guqT = sb8("guqT", [128, INNER], BF16)
                f8s = sb8("f8s", [128, 8], F32)
                ost8 = sb8("ost8", [128, 12], F32)
                outb = sb8("outb", [128, D], F32)
                ps_tr8 = esF2.enter_context(nc.psum_tensor("ps_tr8", [128, 1024], BF16))
                ps_d = esF2.enter_context(nc.psum_tensor("ps_d", [128, 1024], F32))
                ps_r8 = esF2.enter_context(nc.psum_tensor("ps_r8", [128, 2048], F32))

                def ld_td(sync):
                    sSW[0].wait(sync, n_stash[0])
                    sSW[1].wait(sync, n_stash[1])
                    sLT[0].wait(sync, n_lt7[0])
                    sLT[1].wait(sync, n_lt7[1])
                    sT.wait(sync, n_t7)
                    sS.wait(sync, n_s7)
                    sV.wait(sync, n_v7)
                    sGU[0].wait(sync, n_gu[0])
                    sGU[1].wait(sync, n_gu[1])
                    for ic in range(IC):
                        sTD.inc(sync.dma_start(td_b[:, ic * 1024:(ic + 1) * 1024],
                                               td_s[0:1024, ic * 128:(ic + 1) * 128], transpose=True), 16)
                yE(ld_td)
                n_td = sTD.n

                f8, f8y, f8tr_ev, f8dn = {}, {}, {}, {}
                pool_end = {}
                p8_sink = []
                p8_tail = [None]
                for m in range(MT):
                    gb = m % 2

                    g8 = (m % 2) * INNER

                    def ld_gu(sync, m=m, g8=g8):
                        sGU[0].wait(sync, n_gu[0])
                        sGU[1].wait(sync, n_gu[1])
                        if m >= 2:
                            sV.wait(sync, f8.get(m - 2, 0))
                        sLT[m % 2].inc(sync.dma_start(gur[:, g8:g8 + INNER], gu_s[m * 128:(m + 1) * 128, :]), 16)
                    yE(ld_gu)
                    n_lg = sLT[m % 2].n

                    def v_f8a(vector, m=m, n_lg=n_lg, g8=g8):
                        sLT[m % 2].wait(vector, n_lg)
                        sS.wait_all(vector)
                        sV.wait_all(vector)
                        fs = (m % 2) * 4
                        sV.inc(vector.reduce_max(f8s[:, fs:fs + 1], gur[:, g8:g8 + INNER], axis=AX.X, apply_absolute_value=True))
                        sV.wait_all(vector)
                        sV.inc(vector.tensor_scalar_max(f8s[:, fs:fs + 1], f8s[:, fs:fs + 1], 1e-5))
                        sV.wait_all(vector)
                        sV.inc(vector.tensor_scalar_mul(f8s[:, fs + 1:fs + 2], f8s[:, fs:fs + 1], 1.0 / 127.0))
                        sV.wait_all(vector)
                        sV.inc(vector.reciprocal(f8s[:, fs + 2:fs + 3], f8s[:, fs + 1:fs + 2]))
                        sV.wait_all(vector)
                        sV.inc(vector.tensor_mul(f8s[:, fs + 3:fs + 4], f8s[:, fs + 1:fs + 2], mw2_b[:, 2:3]))
                        sV.wait_all(vector)
                        sV.inc(vector.tensor_scalar_mul(y8[:, :], gur[:, g8:g8 + INNER], f8s[:, fs + 2:fs + 3]))
                        sV.wait_all(vector)
                        f8y[m] = sV.n
                    vE(v_f8a)
                    HF = INNER // 2
                    items8 = [{"src": y8[:, hh * HF:(hh + 1) * HF],
                               "dst": guq[:, hh * HF:(hh + 1) * HF],
                               "f": gur[:, g8 + hh * HF:g8 + (hh + 1) * HF],
                               "a": a8[:, hh * HF:(hh + 1) * HF],
                               "t": ps_r8[:, :]} for hh in range(2)]
                    emit_round_ladder(blk, sV, sS, items8,
                                      lambda it: it["f"], lambda it: it["a"], lambda it: it["t"],
                                      s_sink=p8_sink)
                    pool_end[m] = sP.n

                    if p8_tail[0] is not None:
                        p8_tail[0]()
                        p8_tail[0] = None

                    def v_f8sync(vector, m=m):
                        sV.wait_all(vector)
                    vE(v_f8sync)
                    f8[m] = sV.n

                    for b in range(4):
                        def t_b(tensor, m=m, b=b):
                            sV.wait(tensor, f8[m])
                            sP.wait(tensor, pool_end[m])
                            prev = f8tr_ev.get((m, b - 1), 0) if b > 0 else f8tr_ev.get((m - 1, 3), 0)
                            sV.wait(tensor, prev)
                            for k in range(8):
                                ins = tensor.transpose(ps_tr8[:, k * 128:(k + 1) * 128],
                                                       guq[:, b * 1024 + k * 128: b * 1024 + (k + 1) * 128],
                                                       ident_b[:, :])
                            sT.inc(ins)
                        tE(t_b)
                        n8t = sT.n

                        def v_b(vector, m=m, b=b, n8t=n8t):
                            sT.wait(vector, n8t)
                            sV.wait_all(vector)
                            sV.inc(vector.tensor_copy(guqT[:, b * 1024:(b + 1) * 1024], ps_tr8[:, 0:1024]))
                            f8tr_ev[(m, b)] = sV.n
                        vE(v_b)

                    def t_down(tensor, m=m):
                        sV.wait(tensor, f8tr_ev[(m, 3)])
                        sTD.wait(tensor, n_td)
                        sV.wait(tensor, f8dn.get(m - 1, 0))
                        for nn in range(2):
                            for ic in range(IC):
                                ins = tensor.matmul(ps_d[:, nn * 512:(nn + 1) * 512],
                                                    guqT[:, ic * 128:(ic + 1) * 128],
                                                    td_b[:, ic * 1024 + nn * 512: ic * 1024 + nn * 512 + 512],
                                                    start=(ic == 0), stop=(ic == IC - 1))
                        sT.inc(ins)
                    tE(t_down)
                    n8d = sT.n

                    def emit_tail(m=m, n8d=n8d, fs=(m % 2) * 4):
                        def v_f8b(vector):
                            sT.wait(vector, n8d)
                            sV.wait_all(vector)
                            sV.inc(vector.tensor_scalar_mul(y8[:, 0:1024], ps_d[:, 0:1024], f8s[:, fs + 3:fs + 4]))
                            sV.wait_all(vector)
                            sV.inc(vector.bn_stats(ost8[:, 0:6], y8[:, 0:512]))
                            sV.inc(vector.bn_stats(ost8[:, 6:12], y8[:, 512:1024]))
                            sV.wait_all(vector)
                            sV.inc(vector.bn_aggr(f8s[:, fs:fs + 2], ost8[:, 0:12]))
                            sV.wait_all(vector)
                            sV.inc(vector.tensor_scalar_add(f8s[:, fs + 1:fs + 2], f8s[:, fs + 1:fs + 2], 1e-5))
                        vE(v_f8b)
                        n8e = sV.n

                        def s_f8(scalar, n8e=n8e):
                            scalar.wait_ge(sV.h, n8e)
                            scalar.activation(f8s[:, fs + 1:fs + 2], f8s[:, fs + 1:fs + 2],
                                              AF.Sqrt).then_inc(sS.h, 1)
                        p8_sink.append(s_f8)
                        sS.n += 1
                        n8f = sS.n

                        def v_f8c(vector, n8f=n8f):
                            sS.wait(vector, n8f)
                            sV.wait_all(vector)
                            sV.inc(vector.reciprocal(f8s[:, fs + 1:fs + 2], f8s[:, fs + 1:fs + 2]))
                            if m >= 1:
                                sOut[0].wait(vector, 16 * m)
                            sV.wait_all(vector)
                            sV.inc(vector.tensor_scalar(y8[:, 0:1024], y8[:, 0:1024], f8s[:, fs:fs + 1],
                                                        f8s[:, fs + 1:fs + 2], ALU.subtract, ALU.mult))
                            sV.wait_all(vector)
                            sV.inc(vector.tensor_add(outb[:, :], R[:, m * D:(m + 1) * D], y8[:, 0:1024]))
                            f8dn[m] = sV.n
                        vE(v_f8c)

                        def g_out(gpsimd):
                            sV.wait(gpsimd, f8dn[m])
                            sOut[0].inc(gpsimd.dma_start(out_d[m * 128:(m + 1) * 128, :], outb[:, :]), 16)
                        gE(g_out)
                    p8_tail[0] = emit_tail

                if p8_tail[0] is not None:
                    p8_tail[0]()
                    p8_tail[0] = None

                def s_p8(scalar):
                    for b_ in p8_sink:
                        b_(scalar)
                sE(s_p8)

                def g_fin(gpsimd):
                    sOut[0].wait_all(gpsimd)
                    sOut[1].wait_all(gpsimd)
                gE(g_fin)

    return nc


def kernel(L0_attn_x, L0_attn_Wq, L0_attn_Wk, L0_attn_Wv, L0_attn_Wo,
           L0_ffn_w_gate, L0_ffn_w_up, L0_ffn_w_down):
    x = np.ascontiguousarray(np.asarray(L0_attn_x, dtype=np.float32))
    B, S, D = x.shape
    T = S // 2
    wq = permute_wq(np.asarray(L0_attn_Wq, dtype=np.float32))
    wk = np.asarray(L0_attn_Wk, dtype=np.float32)
    wv = np.asarray(L0_attn_Wv, dtype=np.float32)
    wkv = np.ascontiguousarray(np.concatenate([wk, wv], axis=1))
    wo = np.ascontiguousarray(np.asarray(L0_attn_Wo, dtype=np.float32))
    wg = np.ascontiguousarray(np.asarray(L0_ffn_w_gate, dtype=np.float32))
    wu = np.ascontiguousarray(np.asarray(L0_ffn_w_up, dtype=np.float32))
    wd = np.ascontiguousarray(np.asarray(L0_ffn_w_down, dtype=np.float32))

    nc = build_program(T=T, S=S)
    in_maps = []
    for c in range(2 * B):
        b, hh = c // 2, c % 2
        own = np.ascontiguousarray(x[b, hh * T:(hh + 1) * T])
        oth = np.ascontiguousarray(x[b, (1 - hh) * T:(2 - hh) * T])
        ctab, stab = core_tables(hh * T, (1 - hh) * T, T, s_max=S)
        in_maps.append({"x_own": own, "x_oth": oth, "wq": wq, "wkv": wkv,
                        "wo": wo, "wg": wg, "wu": wu, "wd": wd,
                        "ctab": ctab, "stab": stab})
    res = run_bass_kernel_spmd(nc, in_maps, core_ids=list(range(len(in_maps))))
    out = np.empty((B, S, D), dtype=np.float32)
    for c in range(2 * B):
        b, hh = c // 2, c % 2
        out[b, hh * T:(hh + 1) * T] = res.results[c]["out"]
    return out
